# version 02
# baseline (speedup 1.0000x reference)
import math
from contextlib import ExitStack

import numpy as np
import ml_dtypes

import concourse.bass as bass
import concourse.mybir as mybir
from concourse.bass_utils import run_bass_kernel_spmd

F32 = mybir.dt.float32
BF16 = mybir.dt.bfloat16
U32 = mybir.dt.uint32
AF = mybir.ActivationFunctionType
ALU = mybir.AluOpType
AX = mybir.AxisListType
BF = ml_dtypes.bfloat16

B, T, D = 2, 8192, 1024
CH = 2048
HALO = 2176
RT = CH + HALO
NT = RT // 128
G = 3
NEG = -3.0e38
NEGH = -1.5e38
C_OFF = 4096.0
HEADS, DH, WIN = 8, 64, 512
INNER = 512
HID = 2730
QK_SCALE = 8.0

B1_T0 = 128
LA1_Q0 = 640
B2_T0 = 1024
B3_T0 = 1152
LA2_Q0 = 1664
B4_T0 = HALO
LA1_NW = (RT - LA1_Q0) // WIN
LA2_NW = (RT - LA2_Q0) // WIN


def variant_widths(ref_c0):
    ws = []
    for i in range(NT):
        row_end = ref_c0 - HALO + 128 * (i + 1)
        w = min(T, max(512, ((max(row_end, 1) + 511) // 512) * 512))
        ws.append(w)
    for c0 in (ref_c0 - CH, ref_c0):
        rel = (HALO - c0) // 128
        if 0 <= rel < NT:
            ws[rel] = T
    return ws


WIDTHS = [variant_widths(2 * CH), variant_widths(4 * CH)]
SCORE_SLAB_TILES = [set(), set()]
for _vi, _ref in ((0, 2 * CH), (1, 4 * CH)):
    for _c0 in (_ref - CH, _ref):
        _rel = (HALO - _c0) // 128
        if 0 <= _rel < NT:
            SCORE_SLAB_TILES[_vi].add(_rel)


def attn_groups(vi):
    ws = WIDTHS[vi]
    out = []
    for g0 in range(0, NT, G):
        tiles = list(range(g0, min(g0 + G, NT)))
        out.append((tiles, max(ws[t] for t in tiles), [ws[t] for t in tiles]))
    return out


def sim_slab_order(vi):
    order = []
    for tiles, wmax, ws in attn_groups(vi):
        for sc in range(wmax // 512):
            for j, t in enumerate(tiles):
                if ws[j] > sc * 512:
                    order.append((t, sc))
    return order


_ROT = None


def _rotary_tables():
    half = np.arange(0, DH, 2, dtype=np.float64)
    inv_freq = 1.0 / (10000.0 ** (half / DH))
    t = np.arange(T, dtype=np.float64)
    freqs = t[:, None] * inv_freq[None, :]
    base = (half + 0.4 * DH) / (1.4 * DH)
    power = (t - T // 2) / float(WIN // 2)
    scale = base[None, :] ** power[:, None]
    cos, sin = np.cos(freqs), np.sin(freqs)

    def mk(sc):
        Cfull = np.empty((64, T))
        Sfull = np.empty((64, T))
        for j in range(64):
            Cfull[j] = cos[:, j % 32] * sc[:, j % 32]
            if j < 32:
                Sfull[j] = sin[:, j] * sc[:, j]
            else:
                Sfull[j] = -sin[:, j - 32] * sc[:, j - 32]
        return (np.tile(Cfull, (2, 1)).astype(np.float32),
                np.tile(Sfull, (2, 1)).astype(np.float32))

    Cq, Sq = mk(scale)
    Ck, Sk = mk(1.0 / scale)
    return Cq, Sq, Ck, Sk


def _host_core_inputs(inputs, cid):
    global _ROT
    if _ROT is None:
        _ROT = _rotary_tables()
    Cq, Sq, Ck, Sk = _ROT
    b, v = cid // 4, cid % 4
    vi = 0 if v <= 1 else 1
    c0 = v * CH
    lo = c0 - HALO
    x = np.asarray(inputs["L0_attn_x"][b], dtype=np.float32)

    rows = np.zeros((RT, D), dtype=np.float32)
    src0 = max(0, lo)
    rows[src0 - lo:] = x[src0: c0 + CH]
    m = {
        "x_tok": np.ascontiguousarray(x),
        "x_rows_tok": rows,
        "x_rows_fm": np.ascontiguousarray(rows.T),
    }

    ws = WIDTHS[vi]
    parts = []
    col = np.arange(512, dtype=np.int64)
    for (t, sc) in sim_slab_order(vi):
        r = lo + 128 * t + np.arange(128, dtype=np.int64)
        cols = 512 * sc + col
        mask = np.where((cols[None, :] < r[:, None]) & (r[:, None] >= 0), 0.0, NEG)
        parts.append(mask.astype(BF))
    m["sim_slab"] = np.ascontiguousarray(np.concatenate(parts, axis=1))

    def rot_slice(tab, fill):
        out = np.full((128, RT - 128), fill, dtype=np.float32)
        a0 = c0 - CH
        s0 = max(0, a0)
        out[:, s0 - a0:] = tab[:, s0:c0 + CH]
        return out

    m["rot_cq"] = rot_slice(Cq, 1.0)
    m["rot_sq"] = rot_slice(Sq, 0.0)
    m["rot_ck"] = rot_slice(Ck, 1.0)
    m["rot_sk"] = rot_slice(Sk, 0.0)

    flags = np.zeros(96, dtype=np.float32)
    if v == 0:
        for s in range(LA1_NW):
            for kc in range(8):
                if lo + B1_T0 + WIN * s + 128 * kc < 0:
                    flags[s * 8 + kc] = NEGH
        for s in range(LA2_NW):
            for kc in range(8):
                if lo + B3_T0 + WIN * s + 128 * kc < 0:
                    flags[56 + s * 8 + kc] = NEGH
    m["la_flags"] = np.tile(flags[None, :], (128, 1)).astype(np.float32)
    m["shift_flag"] = np.full((128, 1), 0.0 if v == 0 else 1.0, dtype=np.float32)

    pp_ = np.arange(128)[:, None, None]
    kc_ = np.arange(8)[None, :, None]
    q_ = np.arange(512)[None, None, :]
    diff = q_ - 128 * kc_ + 512 - pp_
    m["la_mask"] = np.where((diff >= 0) & (diff <= 512), 0.0, NEGH).astype(BF)

    m["iden"] = np.eye(128).astype(BF)
    e8 = np.zeros((4, 128, 8), dtype=np.float32)
    e8t = np.zeros((4, 8, 128), dtype=np.float32)
    for c in range(4):
        for hh in range(2):
            h = 2 * c + hh
            e8[c, 64 * hh:64 * hh + 64, h] = 1.0
            e8t[c, h, 64 * hh:64 * hh + 64] = 1.0
    m["e8"] = np.ascontiguousarray(e8.transpose(1, 0, 2))
    m["e8t"] = np.ascontiguousarray(e8t.transpose(1, 0, 2))
    m["ones_col"] = np.ones((128, 1), dtype=BF)
    m["ones_row"] = np.ones((1, 128), dtype=np.float32)

    for k in ("L0_attn_W_q", "L0_attn_W_k", "L0_attn_gain", "L0_attn_bias",
              "L0_ffn_wqkv", "L0_ffn_q_scale", "L0_ffn_k_scale", "L0_ffn_wo",
              "L0_ffn_ln_g", "L0_ffn_ln_b", "L0_ffn_w1", "L0_ffn_b1",
              "L0_ffn_w2", "L0_ffn_b2"):
        a = np.asarray(inputs[k], dtype=np.float32)
        m[k] = np.ascontiguousarray(a.reshape(a.shape if a.ndim > 1 else (1, -1)))
    m["log_mix"] = np.asarray(inputs["L0_attn_log_mix"], np.float32).reshape(1, 1)
    m["log_scale"] = np.asarray(inputs["L0_attn_log_scale"], np.float32).reshape(1, 1)
    m["variant"] = np.asarray([[vi]], dtype=np.uint32)
    return m


class _AutoDrainEng:
    _SKIP = {"wait_ge", "drain", "dma_start", "dma_start_transpose", "reg_load",
             "alloc_register", "nop", "engine_nop", "wait_op", "sem_inc",
             "indirect_dma_start", "If_eq", "Else", "engine", "memzero"}

    def __init__(self, eng):
        object.__setattr__(self, "_e", eng)

    def __getattr__(self, n):
        a = getattr(self._e, n)
        if (not callable(a)) or n in self._SKIP or n[0].isupper():
            return a
        eng = self._e

        def w(*args, **kw):
            r = a(*args, **kw)
            eng.drain()
            return r
        return w


class NCWrap:
    def __init__(self, nc):
        object.__setattr__(self, "_nc", nc)
        object.__setattr__(self, "vector", _AutoDrainEng(nc.vector))
        object.__setattr__(self, "scalar", _AutoDrainEng(nc.scalar))
        object.__setattr__(self, "gpsimd", _AutoDrainEng(nc.gpsimd))

    def __getattr__(self, n):
        return getattr(self._nc, n)


class Sem:
    def __init__(self, stack, nc, name, issuer=None):
        self.h = stack.enter_context(nc.semaphore(name))
        self.n = 0
        self.issuer = issuer
        self.ckpt = 0

    def inc(self, instr, k=1):
        instr.then_inc(self.h, k)
        self.n += k
        return instr

    def dma(self, instr):
        return self.inc(instr, 16)

    def wait(self, eng, at=None):
        v = self.n if at is None else at
        if self.issuer is not None and at is None and self.ckpt != self.n:
            self.issuer.wait_ge(self.h, self.n)
            self.ckpt = self.n
        eng.wait_ge(self.h, v)


class Ctx:
    def __init__(self, nc, stack):
        self.nc = nc
        self.stack = stack
        self.d = {}
        self.b = {}
        issuers = {"qs": nc.sync, "qg": nc.gpsimd, "qa": nc.scalar}
        for name in ("qs", "qg", "qa", "p", "v", "sc", "pl"):
            setattr(self, name, Sem(stack, nc, "g_" + name, issuer=issuers.get(name)))
        self.sems = [self.qs, self.qg, self.qa, self.p, self.v, self.sc, self.pl]

    def sb(self, name, shape, dtype, stack=None, **kw):
        t = (stack or self.stack).enter_context(
            self.nc.sbuf_tensor("sb_" + name, shape, dtype, **kw))
        self.b[name] = t
        return t

    def barrier(self, engines=None):
        nc = self.nc
        engs = engines or (nc.sync, nc.tensor, nc.vector, nc.scalar, nc.gpsimd)
        for eng in engs:
            for s in self.sems:
                s.wait(eng)


def emit_ln_fm(ctx, s, x_chunks, out_chunks, tcols, ws,
               gain_cols=None, beta_cols=None):
    nc = ctx.nc
    ones = ctx.b["ones_col_f32"]
    onesr = ctx.b["ones_row"]
    sq = ws["sq"]
    stat_ps = ws["stat"]
    rep_ps = ws["rep"]
    stat_sb = ws["stat_sb"]
    rep_sb = ws["rep_sb"]
    tc = tcols
    s["v"].wait(nc.scalar)
    s["v"].wait(nc.tensor)
    for c in range(8):
        s["sc"].wait(nc.tensor)
        nc.tensor.matmul(stat_ps[0:1, :tc], ones[:, 0:1], x_chunks[c],
                         start=(c == 0), stop=(c == 7))
        s["p"].wait(nc.scalar)
        i = nc.scalar.activation(out=sq[:, :tc], in_=x_chunks[c], func=AF.Square)
        s["sc"].inc(i)
        s["sc"].wait(nc.tensor)
        i = nc.tensor.matmul(stat_ps[32:33, :tc], ones[:, 0:1], sq[:, :tc],
                             start=(c == 0), stop=(c == 7))
        s["p"].inc(i)
    s["p"].wait(nc.vector)
    mu = ws["mu_sb"][0:1, 0:tc]
    va = stat_sb[0:1, 0:tc]
    nc.vector.tensor_scalar(out=mu, in0=stat_ps[0:1, :tc], scalar1=1.0 / D,
                            scalar2=None, op0=ALU.mult)
    nc.vector.tensor_scalar(out=va, in0=stat_ps[32:33, :tc], scalar1=1.0 / D,
                            scalar2=None, op0=ALU.mult)
    nc.vector.tensor_tensor(rep_sb[0:1, :tc], mu, mu, ALU.mult)
    nc.vector.tensor_tensor(va, va, rep_sb[0:1, :tc], ALU.subtract)
    i = nc.vector.tensor_scalar(out=va, in0=va, scalar1=1e-5, scalar2=None, op0=ALU.add)
    s["v"].inc(i)
    s["v"].wait(nc.scalar)
    nc.scalar.activation(out=va, in_=va, func=AF.Ln)
    i = nc.scalar.activation(out=va, in_=va, func=AF.Exp, scale=-0.5)
    s["sc"].inc(i)
    s["sc"].wait(nc.vector)
    i = nc.vector.tensor_tensor(mu, mu, va, ALU.mult)
    s["v"].inc(i)
    s["v"].wait(nc.tensor)
    nc.tensor.matmul(rep_ps[:, 0:tc], onesr[0:1, :], va, start=True, stop=True)
    i = nc.tensor.matmul(rep_ps[:, tc:2 * tc], onesr[0:1, :], mu, start=True, stop=True)
    s["p"].inc(i)
    s["p"].wait(nc.scalar)
    nc.scalar.copy(out=rep_sb[:, 0:tc], in_=rep_ps[:, 0:tc])
    i = nc.scalar.copy(out=rep_sb[:, tc:2 * tc], in_=rep_ps[:, tc:2 * tc])
    s["sc"].inc(i)
    s["sc"].wait(nc.vector)
    for c in range(8):
        nc.vector.scalar_tensor_tensor(out=sq[:, :tc], in0=x_chunks[c],
                                       scalar=1.0, in1=rep_sb[:, 0:tc],
                                       op0=ALU.bypass, op1=ALU.mult)
        if gain_cols is None:
            i = nc.vector.tensor_tensor(out_chunks[c], sq[:, :tc],
                                        rep_sb[:, tc:2 * tc], ALU.subtract)
        else:
            nc.vector.tensor_tensor(sq[:, :tc], sq[:, :tc],
                                    rep_sb[:, tc:2 * tc], ALU.subtract)
            i = nc.vector.tensor_scalar(out=out_chunks[c], in0=sq[:, :tc],
                                        scalar1=gain_cols[c], scalar2=beta_cols[c],
                                        op0=ALU.mult, op1=ALU.add)
    s["v"].inc(i)


def emit_ln_tok_sweep(ctx, src_ap, dst_ap, nrows):
    nc = ctx.nc
    q = ctx.qs
    with (
        nc.sbuf_tensor("lt_x" + dst_ap.tensor.name, [128, 2, 1024], F32) as xt,
        nc.sbuf_tensor("lt_n" + dst_ap.tensor.name, [128, 2, 1024], BF16) as xnt,
        nc.sbuf_tensor("lt_s" + dst_ap.tensor.name, [128, 2, 4], F32) as stb,
    ):
        for i_t in range(nrows // 128):
            bi = i_t % 2
            if i_t >= 1:
                ctx.v.wait(nc.sync)
                ctx.sc.wait(nc.sync)
            q.dma(nc.sync.dma_start(out=xt[:, bi, :],
                                    in_=src_ap[128 * i_t:128 * (i_t + 1), :]))
            q.wait(nc.vector)
            q.wait(nc.scalar)
            ctx.qg.wait(nc.scalar)
            ctx.qg.wait(nc.vector)
            nc.vector.reduce_sum(stb[:, bi, 0:1], xt[:, bi, :], axis=AX.X)
            i = nc.vector.tensor_scalar(out=stb[:, bi, 0:1], in0=stb[:, bi, 0:1],
                                        scalar1=1.0 / D, scalar2=None, op0=ALU.mult)
            ctx.v.inc(i)
            i = nc.scalar.activation(out=xnt[:, bi, :], in_=xt[:, bi, :],
                                     func=AF.Square, accum_out=stb[:, bi, 1:2])
            ctx.sc.inc(i)
            ctx.v.wait(nc.scalar)
            ctx.sc.wait(nc.vector)
            nc.vector.tensor_scalar(out=stb[:, bi, 1:2], in0=stb[:, bi, 1:2],
                                    scalar1=1.0 / D, scalar2=None, op0=ALU.mult)
            nc.vector.tensor_tensor(stb[:, bi, 2:3], stb[:, bi, 0:1],
                                    stb[:, bi, 0:1], ALU.mult)
            nc.vector.tensor_tensor(stb[:, bi, 1:2], stb[:, bi, 1:2],
                                    stb[:, bi, 2:3], ALU.subtract)
            i = nc.vector.tensor_scalar(out=stb[:, bi, 1:2], in0=stb[:, bi, 1:2],
                                        scalar1=1e-5, scalar2=None, op0=ALU.add)
            ctx.v.inc(i)
            ctx.v.wait(nc.scalar)
            nc.scalar.activation(out=stb[:, bi, 1:2], in_=stb[:, bi, 1:2], func=AF.Ln)
            i = nc.scalar.activation(out=stb[:, bi, 1:2], in_=stb[:, bi, 1:2],
                                     func=AF.Exp, scale=-0.5)
            ctx.sc.inc(i)
            ctx.sc.wait(nc.vector)
            i = nc.vector.tensor_scalar(out=xnt[:, bi, :], in0=xt[:, bi, :],
                                        scalar1=stb[:, bi, 0:1], scalar2=stb[:, bi, 1:2],
                                        op0=ALU.subtract, op1=ALU.mult)
            ctx.v.inc(i)
            ctx.v.wait(nc.gpsimd)
            ctx.qg.dma(nc.gpsimd.dma_start(out=dst_ap[128 * i_t:128 * (i_t + 1), :],
                                           in_=xnt[:, bi, :]))
        ctx.qg.wait(nc.sync)
        ctx.qg.wait(nc.scalar)
        ctx.qg.wait(nc.vector)


def emit_prepass(ctx):
    nc = ctx.nc
    d = ctx.d

    iden = ctx.sb("iden_sb", [128, 128], BF16)
    iden32 = ctx.sb("iden32_sb", [128, 128], F32)
    ones_col = ctx.sb("ones_col", [128, 1], BF16)
    ones_col_f = ctx.sb("ones_col_f32", [128, 1], F32)
    ones_row = ctx.sb("ones_row", [1, 128], F32)
    la_mask = ctx.sb("la_mask", [128, 8, 512], BF16)
    e8 = ctx.sb("e8_sb", [128, 4, 8], F32)
    e8t = ctx.sb("e8t_sb", [8, 4, 128], F32)
    laflags = ctx.sb("laflags", [128, 96], F32)
    sflag = ctx.sb("sflag", [128, 1], F32)
    gain_rep = ctx.sb("gain_rep", [128, 1024], F32, stack=ctx.attn_es, side="right")
    bias_rep = ctx.sb("bias_rep", [128, 1024], F32, stack=ctx.attn_es, side="right")
    scal = ctx.sb("scal", [128, 4], F32)
    wqT = ctx.sb("wqT", [128, 8, 32], BF16, stack=ctx.attn_es, side="right")
    wkT = ctx.sb("wkT", [128, 8, 32], BF16, stack=ctx.attn_es, side="right")
    kT = ctx.sb("kT", [32, T], BF16, stack=ctx.attn_es, side="right")
    qT = ctx.sb("qT", [32, RT], BF16, stack=ctx.attn_es, side="right")

    q = ctx.qs
    q.dma(nc.sync.dma_start(out=iden[:], in_=d["iden"][:]))
    q.dma(nc.sync.dma_start(out=la_mask[:], in_=d["la_mask"][:]))
    q.dma(nc.sync.dma_start(out=e8[:], in_=d["e8"][:]))
    q.dma(nc.sync.dma_start(out=e8t[:], in_=d["e8t"][:]))
    q.dma(nc.sync.dma_start(out=laflags[:], in_=d["la_flags"][:]))
    q.dma(nc.sync.dma_start(out=sflag[:], in_=d["shift_flag"][:]))
    q.dma(nc.sync.dma_start(out=ones_col[:], in_=d["ones_col"][:]))
    q.dma(nc.sync.dma_start(out=ones_row[:], in_=d["ones_row"][:]))

    es = ExitStack()
    wq_sb = es.enter_context(nc.sbuf_tensor("pp_w", [32, 1024], F32))
    wk_sb = es.enter_context(nc.sbuf_tensor("pp_w2", [32, 1024], F32))
    row_sb = es.enter_context(nc.sbuf_tensor("pp_row", [1, 1024], F32))
    row2_sb = es.enter_context(nc.sbuf_tensor("pp_row2", [1, 1024], F32))
    sc1 = es.enter_context(nc.sbuf_tensor("pp_sc1", [1, 4], F32))
    pps = es.enter_context(nc.psum_tensor("pp_ps", [128, 512], F32))
    pps2 = es.enter_context(nc.psum_tensor("pp_ps2", [128, 512], F32))
    ppst = es.enter_context(nc.psum_tensor("pp_pst", [128, 128], F32))
    if True:
        q.dma(nc.sync.dma_start(out=wq_sb[:], in_=d["L0_attn_W_q"][:]))
        q.dma(nc.sync.dma_start(out=wk_sb[:], in_=d["L0_attn_W_k"][:]))
        q.dma(nc.sync.dma_start(out=row_sb[:], in_=d["L0_attn_gain"][:]))
        q.dma(nc.sync.dma_start(out=row2_sb[:], in_=d["L0_attn_bias"][:]))
        q.dma(nc.sync.dma_start(out=sc1[0:1, 0:1], in_=d["log_mix"][:]))
        q.dma(nc.sync.dma_start(out=sc1[0:1, 1:2], in_=d["log_scale"][:]))
        q.wait(nc.sync)
        q.wait(nc.vector)
        ctx.v.inc(nc.vector.tensor_copy(out=iden32[:], in_=iden[:]))
        ctx.v.inc(nc.vector.tensor_copy(out=ones_col_f[:], in_=ones_col[:]))
        q.wait(nc.tensor)
        ctx.v.wait(nc.tensor)
        for c in range(8):
            i = nc.tensor.transpose(ppst[0:128, 0:32], wq_sb[:, 128 * c:128 * (c + 1)], iden32[0:32, 0:32])
            ctx.p.inc(i)
            ctx.p.wait(nc.vector)
            ctx.v.inc(nc.vector.tensor_copy(out=wqT[:, c, :], in_=ppst[0:128, 0:32]))
            ctx.v.wait(nc.tensor)
            i = nc.tensor.transpose(ppst[0:128, 0:32], wk_sb[:, 128 * c:128 * (c + 1)], iden32[0:32, 0:32])
            ctx.p.inc(i)
            ctx.p.wait(nc.vector)
            ctx.v.inc(nc.vector.tensor_copy(out=wkT[:, c, :], in_=ppst[0:128, 0:32]))
            ctx.v.wait(nc.tensor)
        nc.tensor.matmul(pps[:, :], ones_row[0:1, :], row_sb[0:1, 0:512], start=True, stop=True)
        i = nc.tensor.matmul(pps2[:, :], ones_row[0:1, :], row_sb[0:1, 512:1024], start=True, stop=True)
        ctx.p.inc(i)
        ctx.p.wait(nc.scalar)
        nc.scalar.copy(out=gain_rep[:, 0:512], in_=pps[:, :])
        ctx.sc.inc(nc.scalar.copy(out=gain_rep[:, 512:1024], in_=pps2[:, :]))
        ctx.sc.wait(nc.tensor)
        nc.tensor.matmul(pps[:, :], ones_row[0:1, :], row2_sb[0:1, 0:512], start=True, stop=True)
        i = nc.tensor.matmul(pps2[:, :], ones_row[0:1, :], row2_sb[0:1, 512:1024], start=True, stop=True)
        ctx.p.inc(i)
        ctx.p.wait(nc.scalar)
        nc.scalar.copy(out=bias_rep[:, 0:512], in_=pps[:, :])
        ctx.sc.inc(nc.scalar.copy(out=bias_rep[:, 512:1024], in_=pps2[:, :]))
        q.wait(nc.scalar)
        ctx.sc.inc(nc.scalar.activation(out=sc1[0:1, 2:3], in_=sc1[0:1, 0:1], func=AF.Sigmoid))
        nc.scalar.activation(out=sc1[0:1, 3:4], in_=sc1[0:1, 1:2], func=AF.Abs)
        nc.scalar.activation(out=sc1[0:1, 3:4], in_=sc1[0:1, 3:4], func=AF.Exp, scale=-1.0)
        ctx.sc.inc(nc.scalar.activation(out=sc1[0:1, 3:4], in_=sc1[0:1, 3:4], func=AF.Ln, bias=1.0))
        ctx.sc.wait(nc.vector)
        nc.vector.tensor_relu(row2_sb[0:1, 0:1], sc1[0:1, 1:2])
        nc.vector.tensor_tensor(row2_sb[0:1, 0:1], row2_sb[0:1, 0:1], sc1[0:1, 3:4], ALU.add)
        nc.vector.tensor_scalar(out=row2_sb[0:1, 2:3], in0=row2_sb[0:1, 0:1],
                                scalar1=0.01, scalar2=None, op0=ALU.add)
        nc.vector.tensor_copy(out=row2_sb[0:1, 0:1], in_=sc1[0:1, 2:3])
        i = nc.vector.tensor_scalar(out=row2_sb[0:1, 1:2], in0=sc1[0:1, 2:3],
                                    scalar1=-1.0, scalar2=1.0, op0=ALU.mult, op1=ALU.add)
        ctx.v.inc(i)
        ctx.v.wait(nc.tensor)
        i = nc.tensor.matmul(pps[:, 0:3], ones_row[0:1, :], row2_sb[0:1, 0:3],
                             start=True, stop=True)
        ctx.p.inc(i)
        ctx.p.wait(nc.scalar)
        ctx.sc.inc(nc.scalar.copy(out=scal[:, 0:3], in_=pps[:, 0:3]))

        emit_ln_tok_sweep(ctx, d["x_tok"], d["xn_tok"], T)
        emit_ln_tok_sweep(ctx, d["x_rows_tok"], d["xnrows_tok"], RT)

        fm_es = ExitStack()
        fm = fm_es.enter_context(nc.sbuf_tensor("pp_fm", [128, 8, T], BF16, is_transpose_dest=True))
        if True:
            for c in range(8):
                ctx.qa.dma(nc.scalar.dma_start(
                    out=fm[:, c, :], in_=d["xn_tok"][:, 128 * c:128 * (c + 1)],
                    transpose=True))
            ctx.qa.wait(nc.sync)
            ctx.qa.wait(nc.tensor)
            for c in range(8):
                q.dma(nc.sync.dma_start(out=d["xn_fm"][128 * c:128 * (c + 1), :],
                                        in_=fm[:, c, :]))
            for sc_i in range(T // 512):
                for c in range(8):
                    i = nc.tensor.matmul(pps[0:32, :], wkT[:, c, :],
                                         fm[:, c, 512 * sc_i:512 * (sc_i + 1)],
                                         start=(c == 0), stop=(c == 7))
                ctx.p.inc(i)
                ctx.p.wait(nc.vector)
                ctx.v.inc(nc.vector.tensor_copy(out=kT[:, 512 * sc_i:512 * (sc_i + 1)],
                                                in_=pps[0:32, :]))
                ctx.v.wait(nc.tensor)
            q.wait(nc.sync)
        ctx.barrier()
        fm_es.close()

        sg = {"q": ctx.qs, "g": ctx.qg, "p": ctx.p, "v": ctx.v, "sc": ctx.sc}
        xr = es.enter_context(nc.sbuf_tensor("pp_xr", [128, 8, 512], F32))
        xn = es.enter_context(nc.sbuf_tensor("pp_xn", [128, 8, 512], BF16))
        sq = es.enter_context(nc.sbuf_tensor("pp_sq", [128, 512], F32))
        stat_sb = es.enter_context(nc.sbuf_tensor("pp_stsb", [33, 512], F32))
        mu_sb = es.enter_context(nc.sbuf_tensor("pp_musb", [1, 512], F32))
        rep_sb = es.enter_context(nc.sbuf_tensor("pp_repsb", [128, 1024], F32))
        stat_ps = es.enter_context(nc.psum_tensor("pp_stat", [33, 512], F32))
        rep_ps = es.enter_context(nc.psum_tensor("pp_rep", [128, 1024], F32))
        if True:
            ws = dict(sq=sq, stat=stat_ps, rep=rep_ps, stat_sb=stat_sb, rep_sb=rep_sb, mu_sb=mu_sb)
            for ci in range((RT + 511) // 512):
                t0 = 512 * ci
                tc = min(512, RT - t0)
                q.dma(nc.sync.dma_start(
                    out=xr[:, :, :tc],
                    in_=d["x_rows_fm"].rearrange("(c p) t -> p c t", p=128)[:, :, t0:t0 + tc]))
                q.wait(nc.vector)
                q.wait(nc.scalar)
                q.wait(nc.tensor)
                emit_ln_fm(ctx, sg, [xr[:, c, :tc] for c in range(8)],
                           [xn[:, c, :tc] for c in range(8)], tc, ws)
                ctx.v.wait(nc.gpsimd)
                ctx.qg.dma(nc.gpsimd.dma_start(
                    out=d["xnrows"].rearrange("(c p) t -> p c t", p=128)[:, :, t0:t0 + tc],
                    in_=xn[:, :, :tc]))
                ctx.v.wait(nc.tensor)
                for c in range(8):
                    i = nc.tensor.matmul(pps2[0:32, :tc], wqT[:, c, :], xn[:, c, :tc],
                                         start=(c == 0), stop=(c == 7))
                ctx.p.inc(i)
                ctx.p.wait(nc.vector)
                ctx.v.inc(nc.vector.tensor_copy(out=qT[:, t0:t0 + tc], in_=pps2[0:32, :tc]))
                ctx.v.wait(nc.tensor)
                ctx.qg.wait(nc.sync)
    ctx.barrier()
    es.close()


def emit_attn(ctx, vi, s):
    nc = ctx.nc
    d = ctx.d
    groups = attn_groups(vi)
    order = sim_slab_order(vi)
    slab_off = {}
    off = 0
    for key in order:
        slab_off[key] = off
        off += 512
    qT = ctx.b["qT"]
    kT = ctx.b["kT"]
    iden = ctx.b["iden_sb"]
    iden32 = ctx.b["iden32_sb"]
    scal = ctx.b["scal"]
    gain_rep = ctx.b["gain_rep"]
    bias_rep = ctx.b["bias_rep"]
    ISQ32 = 1.0 / math.sqrt(32.0)

    aes = ExitStack()
    sa = aes.enter_context(nc.sbuf_tensor("at_sa%d" % vi, [128, G, 16, 512], BF16))
    msb = aes.enter_context(nc.sbuf_tensor("at_ms%d" % vi, [128, 16, 512], F32))
    xf = aes.enter_context(nc.sbuf_tensor("at_xf%d" % vi, [128, 2, 8, 512], BF16))
    xt = aes.enter_context(nc.sbuf_tensor("at_xt%d" % vi, [128, 2, 4, 1024], BF16))
    sl = aes.enter_context(nc.sbuf_tensor("at_sl%d" % vi, [128, G, 512], BF16))
    rb = aes.enter_context(nc.sbuf_tensor("at_rb%d" % vi, [128, 8, G * 128], BF16))
    scb = aes.enter_context(nc.sbuf_tensor("at_sb%d" % vi, [128, 2, 512], F32))
    m8 = aes.enter_context(nc.sbuf_tensor("at_m8%d" % vi, [128, G, 128], BF16))
    t8 = aes.enter_context(nc.sbuf_tensor("at_t8%d" % vi, [128, 8], BF16))
    nr = aes.enter_context(nc.sbuf_tensor("at_nr%d" % vi, [128, G, 4], F32))
    aT = aes.enter_context(nc.sbuf_tensor("at_aT%d" % vi, [128, 4, 128], BF16))
    ep = aes.enter_context(nc.sbuf_tensor("at_ep%d" % vi, [128, 3, 1024], F32))
    xnt_t = aes.enter_context(nc.sbuf_tensor("at_xn%d" % vi, [128, 1024], BF16))
    xr_t = aes.enter_context(nc.sbuf_tensor("at_xr%d" % vi, [128, 1024], F32))
    x1T = aes.enter_context(nc.sbuf_tensor("at_xT%d" % vi, [128, 8, 128], F32))
    if True:
        sa_f = sa.ap().rearrange("p g a b -> p (g a b)")
        ms_f = msb.ap().rearrange("p a b -> p (a b)")
        for (tiles, wmax, tws) in groups:
            g0 = tiles[0]
            nsc = wmax // 512
            s["q"].dma(nc.sync.dma_start(
                out=rb[:, :, :len(tiles) * 128],
                in_=d["xnrows"].rearrange("(c p) t -> p c t", p=128)[:, :, 128 * g0:128 * (g0 + len(tiles))]))
            with (
                nc.psum_tensor("ps_sim0_%d_%d" % (vi, g0), [128, 512], F32) as ps0,
                nc.psum_tensor("ps_sim1_%d_%d" % (vi, g0), [128, 512], F32) as ps1,
                nc.psum_tensor("ps_sim2_%d_%d" % (vi, g0), [128, 512], F32) as ps2,
            ):
                simps = [ps0, ps1, ps2]
                for sc_i in range(nsc):
                    bi = sc_i % 2
                    if sc_i >= 2:
                        s["p"].wait(nc.sync)
                    s["q"].dma(nc.sync.dma_start(
                        out=xf[:, bi, :, :],
                        in_=d["xn_fm"].rearrange("(c p) t -> p c t", p=128)[:, :, 512 * sc_i:512 * (sc_i + 1)]))
                    for j, t in enumerate(tiles):
                        if tws[j] <= sc_i * 512:
                            continue
                        if sc_i >= 1:
                            s["v"].wait(nc.sync)
                        s["q"].dma(nc.sync.dma_start(
                            out=sl[:, j, :],
                            in_=d["sim_slab"][:, slab_off[(t, sc_i)]:slab_off[(t, sc_i)] + 512]))
                        s["q"].wait(nc.tensor)
                        s["v"].wait(nc.tensor)
                        for c in range(8):
                            i = nc.tensor.matmul(simps[j][:, :], rb[:, c, 128 * j:128 * (j + 1)],
                                                 xf[:, bi, c, :], start=(c == 0), stop=(c == 7))
                        s["p"].inc(i)
                        s["p"].wait(nc.vector)
                        s["q"].wait(nc.vector)
                        i = nc.vector.scalar_tensor_tensor(
                            out=sa_f[:, 8192 * j + 512 * sc_i:8192 * j + 512 * (sc_i + 1)],
                            in0=simps[j][:, :], scalar=1.0 / 32.0,
                            in1=sl[:, j, :], op0=ALU.mult, op1=ALU.add)
                        s["v"].inc(i)
                        i = nc.vector.max(out=m8[:, j, 8 * sc_i:8 * sc_i + 8],
                                          in_=sa_f[:, 8192 * j + 512 * sc_i:8192 * j + 512 * (sc_i + 1)])
                        s["v"].inc(i)
            with nc.psum_tensor("ps_scr_%d_%d" % (vi, g0), [128, 2, 512], F32) as scps:
                for j, t in enumerate(tiles):
                    ntc = tws[j] // 512
                    s["sc"].wait(nc.vector)
                    i = nc.vector.max(out=t8[:, :], in_=m8[:, j, 0:8 * ntc])
                    s["v"].inc(i)
                    for sc_i in range(ntc):
                        pbi = sc_i % 2
                        s["sc"].wait(nc.tensor)
                        i = nc.tensor.matmul(scps[:, pbi, :], qT[:, 128 * t:128 * (t + 1)],
                                             kT[:, 512 * sc_i:512 * (sc_i + 1)],
                                             start=True, stop=True)
                        s["p"].inc(i)
                        s["p"].wait(nc.scalar)
                        s["v"].wait(nc.scalar)
                        i = nc.scalar.activation(out=scb[:, pbi, :], in_=scps[:, pbi, :],
                                                 func=AF.Copy, bias=C_OFF, scale=ISQ32)
                        s["sc"].inc(i)
                        if t in SCORE_SLAB_TILES[vi]:
                            s["v"].wait(nc.sync)
                            s["q"].dma(nc.sync.dma_start(
                                out=sl[:, 0, :],
                                in_=d["sim_slab"][:, slab_off[(t, sc_i)]:slab_off[(t, sc_i)] + 512]))
                            s["q"].wait(nc.vector)
                            s["sc"].wait(nc.vector)
                            i = nc.vector.tensor_tensor(scb[:, pbi, :], scb[:, pbi, :],
                                                        sl[:, 0, :], ALU.add)
                            s["v"].inc(i)
                        s["sc"].wait(nc.vector)
                        i = nc.vector.scalar_tensor_tensor(
                            out=ms_f[:, 512 * sc_i:512 * (sc_i + 1)],
                            in0=sa_f[:, 8192 * j + 512 * sc_i:8192 * j + 512 * (sc_i + 1)],
                            scalar=t8[:, 7:8],
                            in1=scb[:, pbi, :], op0=ALU.is_ge, op1=ALU.mult)
                        s["v"].inc(i)
                    i = nc.vector.reduce_max(nr[:, j, 0:1], ms_f[:, 0:tws[j]],
                                             axis=AX.X, negate=True)
                    s["v"].inc(i)
                    s["v"].wait(nc.scalar)
                    i = nc.scalar.activation(
                        out=sa_f[:, 8192 * j:8192 * j + tws[j]],
                        in_=ms_f[:, 0:tws[j]],
                        func=AF.Exp, bias=nr[:, j, 0:1], accum_out=nr[:, j, 1:2])
                    s["sc"].inc(i)
                    s["sc"].wait(nc.vector)
                    i = nc.vector.reciprocal(out=nr[:, j, 2:3], in_=nr[:, j, 1:2])
                    s["v"].inc(i)
            pmes = ExitStack()
            pm0 = pmes.enter_context(nc.psum_tensor("ps_m0_%d_%d" % (vi, g0), [128, 2, 512], F32))
            pm1 = pmes.enter_context(nc.psum_tensor("ps_m1_%d_%d" % (vi, g0), [128, 2, 512], F32))
            pm2 = pmes.enter_context(nc.psum_tensor("ps_m2_%d_%d" % (vi, g0), [128, 2, 512], F32))
            tpes = ExitStack()
            tpps = tpes.enter_context(nc.psum_tensor("ps_tp_%d_%d" % (vi, g0), [128, 512], BF16))
            if True:
                msgps = [pm0, pm1, pm2]
                for sc_i in range(nsc):
                    bi = sc_i % 2
                    if sc_i >= 2:
                        s["p"].wait(nc.gpsimd)
                    s["g"].dma(nc.gpsimd.dma_start(
                        out=xt[:, bi, :, :],
                        in_=d["xn_tok"].rearrange("(a p) f -> p a f", p=128)[:, 4 * sc_i:4 * (sc_i + 1), :]))
                    for j, t in enumerate(tiles):
                        if tws[j] <= sc_i * 512:
                            continue
                        s["g"].wait(nc.tensor)
                        for q4 in range(4):
                            i = nc.tensor.transpose(
                                tpps[:, 128 * q4:128 * (q4 + 1)],
                                sa_f[:, 8192 * j + 512 * sc_i + 128 * q4:8192 * j + 512 * sc_i + 128 * (q4 + 1)],
                                iden[:])
                        s["p"].inc(i)
                        s["p"].wait(nc.vector)
                        i = nc.vector.tensor_copy(
                            out=aT.ap().rearrange("p a b -> p (a b)")[:, :],
                            in_=tpps[:, :])
                        s["v"].inc(i)
                        s["v"].wait(nc.tensor)
                        first = (sc_i == 0)
                        last = (sc_i == tws[j] // 512 - 1)
                        for q4 in range(4):
                            nc.tensor.matmul(msgps[j][:, 0, :], aT[:, q4, :],
                                             xt[:, bi, q4, 0:512],
                                             start=(first and q4 == 0), stop=(last and q4 == 3))
                            i = nc.tensor.matmul(msgps[j][:, 1, :], aT[:, q4, :],
                                                 xt[:, bi, q4, 512:1024],
                                                 start=(first and q4 == 0), stop=(last and q4 == 3))
                        s["p"].inc(i)
                tpes.close()
                with nc.psum_tensor("ps_x1_%d_%d" % (vi, g0), [128, 2, 512], F32) as xtp:
                    for j, t in enumerate(tiles):
                        s["q"].wait(nc.sync, at=None)
                        s["q"].dma(nc.sync.dma_start(
                            out=xnt_t[:, :], in_=d["xnrows_tok"][128 * t:128 * (t + 1), :]))
                        s["q"].dma(nc.sync.dma_start(
                            out=xr_t[:, :], in_=d["x_rows_tok"][128 * t:128 * (t + 1), :]))
                        s["q"].wait(nc.vector)
                        s["p"].wait(nc.vector)
                        msgv = msgps[j].ap().rearrange("p a b -> p (a b)")
                        nc.vector.tensor_scalar(out=ep[:, 0, :], in0=msgv[:, :],
                                                scalar1=nr[:, j, 2:3], scalar2=scal[:, 1:2],
                                                op0=ALU.mult, op1=ALU.mult)
                        nc.vector.scalar_tensor_tensor(out=ep[:, 1, :], in0=xnt_t[:, :],
                                                       scalar=scal[:, 0:1], in1=ep[:, 0, :],
                                                       op0=ALU.mult, op1=ALU.add)
                        nc.vector.tensor_tensor(ep[:, 1, :], ep[:, 1, :], gain_rep[:, :], ALU.mult)
                        i = nc.vector.tensor_tensor(ep[:, 1, :], ep[:, 1, :], bias_rep[:, :], ALU.add)
                        s["v"].inc(i)
                        s["v"].wait(nc.scalar)
                        i = nc.scalar.activation(out=ep[:, 2, :], in_=ep[:, 1, :], func=AF.Gelu)
                        s["sc"].inc(i)
                        s["sc"].wait(nc.vector)
                        nc.vector.tensor_scalar(out=ep[:, 2, :], in0=ep[:, 2, :],
                                                scalar1=scal[:, 2:3], scalar2=None, op0=ALU.mult)
                        i = nc.vector.tensor_tensor(ep[:, 0, :], ep[:, 2, :], xr_t[:, :], ALU.add)
                        s["v"].inc(i)
                        s["v"].wait(nc.tensor)
                        s["q"].wait(nc.tensor)
                        for c in range(8):
                            i = nc.tensor.transpose(
                                xtp[:, c // 4, 128 * (c % 4):128 * (c % 4) + 128],
                                ep[:, 0, 128 * c:128 * (c + 1)], iden32[:])
                            if c in (3, 7):
                                s["p"].inc(i)
                        s["p"].wait(nc.vector)
                        x1T_f = x1T.ap().rearrange("p a b -> p (a b)")
                        nc.vector.tensor_copy(out=x1T_f[:, 0:512], in_=xtp[:, 0, :])
                        i = nc.vector.tensor_copy(out=x1T_f[:, 512:1024], in_=xtp[:, 1, :])
                        s["v"].inc(i)
                        s["v"].wait(nc.sync)
                        s["q"].dma(nc.sync.dma_start(
                            out=d["x1_fm"].rearrange("(c p) t -> p c t", p=128)[:, :, 128 * t:128 * (t + 1)],
                            in_=x1T[:, :, :]))
            pmes.close()
            for eng in (nc.sync, nc.tensor, nc.vector, nc.scalar, nc.gpsimd):
                for ss in s.values():
                    ss.wait(eng)
    aes.close()


def emit_la(ctx, layer, t0, q0, nw, flag_base, in_stage, ln_first, z0_stage,
            resid_stage, out_stage):
    nc = ctx.nc
    d = ctx.d
    s = {"q": ctx.qs, "g": ctx.qg, "p": ctx.p, "v": ctx.v, "sc": ctx.sc}
    iden = ctx.b["iden_sb"]
    ones_col = ctx.b["ones_col"]
    ones_row = ctx.b["ones_row"]
    la_mask = ctx.b["la_mask"]
    e8 = ctx.b["e8_sb"]
    e8t = ctx.b["e8t_sb"]
    laflags = ctx.b["laflags"]
    sflag = ctx.b["sflag"]
    nch = (RT - t0) // 512
    NK = RT - t0

    les = ExitStack()
    wqkv = les.enter_context(nc.sbuf_tensor("la_wqkv_%d" % layer, [128, 8, 1536], BF16))
    wo = les.enter_context(nc.sbuf_tensor("la_wo_%d" % layer, [128, 4, 1024], BF16))
    qs_col = les.enter_context(nc.sbuf_tensor("la_qs_%d" % layer, [128, 1], F32))
    ks_col = les.enter_context(nc.sbuf_tensor("la_ks_%d" % layer, [128, 1], F32))
    kh = les.enter_context(nc.sbuf_tensor("la_kh_%d" % layer, [128, 4, NK], BF16))
    qh = les.enter_context(nc.sbuf_tensor("la_qh_%d" % layer, [128, 4, NK - 512], BF16))
    vt = les.enter_context(nc.sbuf_tensor("la_vt_%d" % layer, [128, NK // 128, 512], BF16))
    if True:
        ctx.qg.dma(nc.gpsimd.dma_start(
            out=wqkv[:],
            in_=d["L0_ffn_wqkv"][layer].rearrange("(c p) f -> p c f", p=128)))
        ctx.qg.dma(nc.gpsimd.dma_start(
            out=wo[:],
            in_=d["L0_ffn_wo"][layer].rearrange("(c p) f -> p c f", p=128)))
        s["q"].dma(nc.sync.dma_start(out=qs_col[:], in_=d["qscale_col"][layer]))
        s["q"].dma(nc.sync.dma_start(out=ks_col[:], in_=d["kscale_col"][layer]))
        pa_es = ExitStack()
        zt = pa_es.enter_context(nc.sbuf_tensor("laA_z_%d" % layer, [128, 8, 513], F32))
        qraw = pa_es.enter_context(nc.sbuf_tensor("laA_qr_%d" % layer, [128, 4, 512], F32))
        kraw = pa_es.enter_context(nc.sbuf_tensor("laA_kr_%d" % layer, [128, 4, 512], F32))
        ub = pa_es.enter_context(nc.sbuf_tensor("laA_u_%d" % layer, [128, 512], F32))
        ab = pa_es.enter_context(nc.sbuf_tensor("laA_a_%d" % layer, [128, 512], F32))
        bb = pa_es.enter_context(nc.sbuf_tensor("laA_b_%d" % layer, [128, 512], F32))
        sq = pa_es.enter_context(nc.sbuf_tensor("laA_sq_%d" % layer, [128, 512], F32))
        stat_sb = pa_es.enter_context(nc.sbuf_tensor("laA_st_%d" % layer, [33, 512], F32))
        rep_sb = pa_es.enter_context(nc.sbuf_tensor("laA_rp_%d" % layer, [128, 1024], F32))
        r8 = pa_es.enter_context(nc.sbuf_tensor("laA_r8_%d" % layer, [8, 512], F32))
        rrep = pa_es.enter_context(nc.sbuf_tensor("laA_rr_%d" % layer, [128, 512], F32))
        rot = pa_es.enter_context(nc.sbuf_tensor("laA_rot_%d" % layer, [128, 4, 512], F32))
        ps0 = pa_es.enter_context(nc.psum_tensor("laA_ps_%d" % layer, [128, 512], F32))
        ps1 = pa_es.enter_context(nc.psum_tensor("laA_ps2_%d" % layer, [128, 512], F32))
        ps8 = pa_es.enter_context(nc.psum_tensor("laA_st8_%d" % layer, [8, 512], F32))
        stat_ps = pa_es.enter_context(nc.psum_tensor("laA_stat_%d" % layer, [33, 512], F32))
        rep_ps = pa_es.enter_context(nc.psum_tensor("laA_rep_%d" % layer, [128, 1024], F32))
        mu_sb = pa_es.enter_context(nc.sbuf_tensor("laA_mu_%d" % layer, [1, 512], F32))
        if True:
            lnws = dict(sq=sq, stat=stat_ps, rep=rep_ps, stat_sb=stat_sb, rep_sb=rep_sb, mu_sb=mu_sb)
            ctx.qg.wait(nc.tensor)
            ctx.qg.wait(nc.sync)
            s["q"].wait(nc.tensor)
            s["q"].wait(nc.vector)
            for ci in range(nch):
                start = t0 + 512 * ci
                s["v"].wait(nc.sync)
                s["p"].wait(nc.sync)
                s["sc"].wait(nc.sync)
                ctx.qg.wait(nc.sync)
                s["q"].dma(nc.sync.dma_start(
                    out=zt[:, :, :],
                    in_=in_stage.rearrange("(c p) t -> p c t", p=128)[:, :, start - 1:start + 512]))
                s["q"].wait(nc.vector)
                s["q"].wait(nc.scalar)
                s["q"].wait(nc.tensor)
                if ln_first:
                    emit_ln_fm(ctx, s, [zt[:, c, 0:1] for c in range(8)],
                               [zt[:, c, 0:1] for c in range(8)], 1, lnws)
                    emit_ln_fm(ctx, s, [zt[:, c, 1:513] for c in range(8)],
                               [zt[:, c, 1:513] for c in range(8)], 512, lnws)
                    s["v"].wait(nc.gpsimd)
                    ctx.qg.dma(nc.gpsimd.dma_start(
                        out=z0_stage.rearrange("(c p) t -> p c t", p=128)[:, :, start:start + 512],
                        in_=zt[:, :, 1:513]))
                if start - 1 <= HALO - 1 < start + 512:
                    loc = HALO - start
                    ctx.qg.wait(nc.vector)
                    i = nc.vector.tensor_scalar(out=zt[:, 4:8, loc:loc + 1],
                                                in0=zt[:, 4:8, loc:loc + 1],
                                                scalar1=sflag[:, 0:1], scalar2=None,
                                                op0=ALU.mult)
                    s["v"].inc(i)
                hviews = [zt[:, c, 1:513] for c in range(4)] + [zt[:, c, 0:512] for c in range(4, 8)]
                s["v"].wait(nc.vector)
                if ci == 0:
                    hbf_ctx = nc.sbuf_tensor("laA_hbf_%d" % layer, [128, 8, 512], BF16)
                    hbf = hbf_ctx.__enter__()
                for c in range(8):
                    i = nc.vector.tensor_copy(out=hbf[:, c, :], in_=hviews[c])
                s["v"].inc(i)
                s["v"].wait(nc.tensor)
                for fc in range(8):
                    dst = qraw if fc < 4 else kraw
                    ps = ps0 if fc % 2 == 0 else ps1
                    for c in range(8):
                        i = nc.tensor.matmul(ps[:, :], wqkv[:, c, 128 * fc:128 * (fc + 1)],
                                             hbf[:, c, :], start=(c == 0), stop=(c == 7))
                    s["p"].inc(i)
                    s["p"].wait(nc.scalar)
                    i = nc.scalar.copy(out=dst[:, fc % 4, :], in_=ps[:, :])
                    s["sc"].inc(i)
                    s["sc"].wait(nc.tensor)
                for t4 in range(4):
                    ps = ps0 if t4 % 2 == 0 else ps1
                    for c in range(8):
                        i = nc.tensor.matmul(ps[:, :], hbf[:, c, 128 * t4:128 * (t4 + 1)],
                                             wqkv[:, c, 1024:1536], start=(c == 0), stop=(c == 7))
                    s["p"].inc(i)
                    s["p"].wait(nc.vector)
                    i = nc.vector.tensor_copy(out=vt[:, 4 * ci + t4, :], in_=ps[:, :])
                    s["v"].inc(i)
                    s["v"].wait(nc.tensor)
                for (rslot, rname) in ((0, "rot_cq"), (1, "rot_sq"), (2, "rot_ck"), (3, "rot_sk")):
                    s["q"].dma(nc.sync.dma_start(
                        out=rot[:, rslot, :],
                        in_=d[rname][:, start - B1_T0:start - B1_T0 + 512]))
                if DEBUG_LA and layer == 0 and ci == 4:
                    s["sc"].wait(nc.gpsimd)
                    ctx.qg.dma(nc.gpsimd.dma_start(out=d["dbg_kraw"][:], in_=kraw[:, :, :]))
                    ctx.qg.dma(nc.gpsimd.dma_start(out=d["dbg_hbf"][:], in_=hbf[:, :, :]))
                    ctx.qg.wait(nc.vector)
                    ctx.qg.wait(nc.scalar)
                for (raw, outfm, sccol, cslot, sslot) in (
                        (qraw, qh, qs_col, 0, 1), (kraw, kh, ks_col, 2, 3)):
                    if outfm is qh and ci == 0:
                        continue
                    s["v"].wait(nc.scalar)
                    s["p"].wait(nc.scalar)
                    for c in range(4):
                        i = nc.scalar.activation(out=sq[:, :], in_=raw[:, c, :], func=AF.Square)
                        s["sc"].inc(i)
                        s["sc"].wait(nc.tensor)
                        i = nc.tensor.matmul(ps8[:, :], e8[:, c, :], sq[:, :],
                                             start=(c == 0), stop=(c == 3))
                        s["p"].inc(i)
                        s["p"].wait(nc.scalar)
                    s["v"].wait(nc.scalar)
                    i = nc.scalar.copy(out=r8[:, :], in_=ps8[:, :])
                    s["sc"].inc(i)
                    s["sc"].wait(nc.vector)
                    i = nc.vector.tensor_scalar_max(r8[:, :], r8[:, :], 1e-24)
                    s["v"].inc(i)
                    s["v"].wait(nc.scalar)
                    nc.scalar.activation(out=r8[:, :], in_=r8[:, :], func=AF.Ln)
                    i = nc.scalar.activation(out=r8[:, :], in_=r8[:, :], func=AF.Exp, scale=-0.5)
                    s["sc"].inc(i)
                    s["sc"].wait(nc.vector)
                    i = nc.vector.tensor_copy(out=r8[:, :], in_=r8[:, :])
                    s["v"].inc(i)
                    if DEBUG_LA and layer == 0 and ci == 4 and raw is kraw:
                        s["v"].wait(nc.gpsimd)
                        ctx.qg.dma(nc.gpsimd.dma_start(out=d["dbg_r8"][:], in_=r8[:, :]))
                        ctx.qg.wait(nc.tensor)
                        ctx.qg.wait(nc.vector)
                    s["v"].wait(nc.tensor)
                    for c in range(4):
                        i = nc.tensor.matmul(rep_ps[:, 0:512], e8t[:, c, :], r8[:, :],
                                             start=True, stop=True)
                        s["p"].inc(i)
                        s["p"].wait(nc.scalar)
                        i = nc.scalar.copy(out=rrep[:, :], in_=rep_ps[:, 0:512])
                        s["sc"].inc(i)
                        s["sc"].wait(nc.vector)
                        s["q"].wait(nc.vector)
                        nc.vector.scalar_tensor_tensor(out=ub[:, :], in0=raw[:, c, :],
                                                       scalar=sccol[:, 0:1], in1=rrep[:, :],
                                                       op0=ALU.mult, op1=ALU.mult)
                        nc.vector.tensor_tensor(ab[:, :], ub[:, :], rot[:, cslot, :], ALU.mult)
                        for po in (0, 32, 64, 96):
                            src = po + 32 if po % 64 == 0 else po - 32
                            nc.vector.tensor_tensor(bb[po:po + 32, :], ub[src:src + 32, :],
                                                    rot[src:src + 32, sslot, :], ALU.mult)
                        oc = 512 * (ci - 1) if outfm is qh else 512 * ci
                        i = nc.vector.tensor_tensor(outfm[:, c, oc:oc + 512],
                                                    ab[:, :], bb[:, :], ALU.add)
                        s["v"].inc(i)
                        s["v"].wait(nc.tensor)
            if nch > 0:
                hbf_ctx.__exit__(None, None, None)
            if DEBUG_LA and layer == 0:
                ctx.qg.dma(nc.gpsimd.dma_start(out=d["dbg_qh"][:], in_=qh[:, :, 0:RT - CH - 512]))
                ctx.qg.dma(nc.gpsimd.dma_start(out=d["dbg_kh"][:], in_=kh[:, :, 0:RT - CH]))
                ctx.qg.dma(nc.gpsimd.dma_start(out=d["dbg_vt"][:], in_=vt[:, 0:(RT - CH) // 128, :]))
                ctx.qg.wait(nc.sync)
        ctx.barrier()
        pa_es.close()
        pb_es = ExitStack()
        sT = pb_es.enter_context(nc.sbuf_tensor("laB_sT_%d" % layer, [128, 512], F32))
        eT = pb_es.enter_context(nc.sbuf_tensor("laB_eT_%d" % layer, [128, 8, 512], BF16))
        ofm = pb_es.enter_context(nc.sbuf_tensor("laB_of_%d" % layer, [128, 4, 512], F32))
        obf = pb_es.enter_context(nc.sbuf_tensor("laB_ob_%d" % layer, [128, 4, 512], BF16))
        mrep = pb_es.enter_context(nc.sbuf_tensor("laB_mr_%d" % layer, [128, 512], F32))
        srep = pb_es.enter_context(nc.sbuf_tensor("laB_sr_%d" % layer, [128, 512], F32))
        rq = pb_es.enter_context(nc.sbuf_tensor("laB_rq_%d" % layer, [128, 2, 512], F32))
        kst = pb_es.enter_context(nc.sbuf_tensor("laB_k2_%d" % layer, [128, 4], F32))
        sums = pb_es.enter_context(nc.sbuf_tensor("laB_sm_%d" % layer, [1, 4096], F32))
        brow = pb_es.enter_context(nc.sbuf_tensor("laB_br_%d" % layer, [1, 512], F32))
        resid = pb_es.enter_context(nc.sbuf_tensor("laB_zi_%d" % layer, [128, 8, 512], F32))
        sps0 = pb_es.enter_context(nc.psum_tensor("laB_s1_%d" % layer, [128, 512], F32))
        sps1 = pb_es.enter_context(nc.psum_tensor("laB_s2_%d" % layer, [128, 512], F32))
        oTps = pb_es.enter_context(nc.psum_tensor("laB_oT_%d" % layer, [64, 512], F32))
        swps = pb_es.enter_context(nc.psum_tensor("laB_sw_%d" % layer, [1, 512], F32))
        bdps = pb_es.enter_context(nc.psum_tensor("laB_bd_%d" % layer, [1, 512], F32))
        mrps = pb_es.enter_context(nc.psum_tensor("laB_mrp_%d" % layer, [128, 512], F32))
        if True:
            for w in range(nw):
                q0w = q0 + WIN * w
                qcol = q0w - t0
                kb = q0w - WIN
                kcol = kb - t0
                s["v"].wait(nc.sync)
                ctx.qg.wait(nc.sync)
                s["q"].dma(nc.sync.dma_start(
                    out=resid[:, :, :],
                    in_=resid_stage.rearrange("(c p) t -> p c t", p=128)[:, :, q0w:q0w + 512]))
                for h in range(HEADS):
                    c = h // 2
                    po = 64 * (h % 2)
                    ksl = kh[po:po + 64, c, kcol:kcol + 1024]
                    qsl = qh[po:po + 64, c, qcol - 512:qcol]
                    s["v"].wait(nc.vector)
                    s["p"].wait(nc.vector)
                    nc.vector.reduce_max(kst[po:po + 64, 0:1], ksl, axis=AX.X)
                    i = nc.vector.tensor_reduce(kst[po:po + 64, 1:2], ksl, axis=AX.X,
                                                op=ALU.min, negate=True)
                    s["v"].inc(i)
                    s["v"].wait(nc.scalar)
                    nc.scalar.activation(out=rq[po:po + 64, 0, :], in_=qsl, func=AF.Relu)
                    i = nc.scalar.activation(out=rq[po:po + 64, 1, :], in_=qsl,
                                             func=AF.Relu, scale=-1.0)
                    s["sc"].inc(i)
                    s["sc"].wait(nc.tensor)
                    s["v"].wait(nc.tensor)
                    nc.tensor.matmul(bdps[0:1, :], kst[po:po + 64, 0:1], rq[po:po + 64, 0, :],
                                     start=True, stop=False)
                    i = nc.tensor.matmul(bdps[0:1, :], kst[po:po + 64, 1:2], rq[po:po + 64, 1, :],
                                         start=False, stop=True)
                    s["p"].inc(i)
                    s["p"].wait(nc.scalar)
                    i = nc.scalar.activation(out=brow[0:1, :], in_=bdps[0:1, :], func=AF.Copy, scale=QK_SCALE)
                    s["sc"].inc(i)
                    s["sc"].wait(nc.tensor)
                    i = nc.tensor.matmul(mrps[:, :], ones_row[0:1, :], brow[0:1, :],
                                         start=True, stop=True)
                    s["p"].inc(i)
                    s["p"].wait(nc.scalar)
                    i = nc.scalar.activation(out=mrep[:, :], in_=mrps[:, :], func=AF.Copy, bias=-45.0)
                    s["sc"].inc(i)
                    for kc in range(8):
                        ps = sps0 if kc % 2 == 0 else sps1
                        s["sc"].wait(nc.tensor)
                        s["v"].wait(nc.tensor)
                        i = nc.tensor.matmul(ps[:, :], kh[po:po + 64, c, kcol + 128 * kc:kcol + 128 * (kc + 1)],
                                             qh[po:po + 64, c, qcol - 512:qcol],
                                             start=True, stop=True)
                        s["p"].inc(i)
                        s["p"].wait(nc.vector)
                        s["sc"].wait(nc.vector)
                        nc.vector.scalar_tensor_tensor(out=sT[:, :], in0=ps[:, :],
                                                       scalar=QK_SCALE, in1=mrep[:, :],
                                                       op0=ALU.mult, op1=ALU.subtract)
                        nc.vector.tensor_tensor(sT[:, :], sT[:, :], la_mask[:, kc, :], ALU.add)
                        fi = flag_base + 8 * w + kc
                        i = nc.vector.tensor_scalar(out=sT[:, :], in0=sT[:, :],
                                                    scalar1=laflags[:, fi:fi + 1], scalar2=None,
                                                    op0=ALU.add)
                        s["v"].inc(i)
                        s["v"].wait(nc.scalar)
                        i = nc.scalar.activation(out=eT[:, kc, :], in_=sT[:, :], func=AF.Exp)
                        s["sc"].inc(i)
                    s["sc"].wait(nc.tensor)
                    for kc in range(8):
                        vidx = (kb + 128 * kc - t0) // 128
                        nc.tensor.matmul(oTps[0:64, :],
                                         vt[:, vidx, 64 * h:64 * h + 64],
                                         eT[:, kc, :], start=(kc == 0), stop=(kc == 7))
                    for kc in range(8):
                        i = nc.tensor.matmul(swps[0:1, :], ones_col[:, 0:1],
                                             eT[:, kc, :], start=(kc == 0), stop=(kc == 7))
                    s["p"].inc(i)
                    s["p"].wait(nc.scalar)
                    s["p"].wait(nc.vector)
                    i = nc.scalar.copy(out=ofm[po:po + 64, c, :], in_=oTps[0:64, :])
                    s["sc"].inc(i)
                    i = nc.vector.tensor_copy(out=sums[0:1, 512 * h:512 * (h + 1)], in_=swps[0:1, :])
                    s["v"].inc(i)
                nc.vector.tensor_scalar(out=sums[0:1, :], in0=sums[0:1, :],
                                        scalar1=1e-30, scalar2=None, op0=ALU.add)
                i = nc.vector.reciprocal(out=sums[0:1, :], in_=sums[0:1, :])
                s["v"].inc(i)
                s["v"].wait(nc.tensor)
                s["sc"].wait(nc.tensor)
                for c in range(4):
                    nc.tensor.matmul(mrps[0:64, :], ones_row[0:1, 0:64],
                                     sums[0:1, 512 * 2 * c:512 * (2 * c + 1)],
                                     start=True, stop=True, skip_group_check=True)
                    i = nc.tensor.matmul(mrps[64:128, :], ones_row[0:1, 0:64],
                                         sums[0:1, 512 * (2 * c + 1):512 * (2 * c + 2)],
                                         start=True, stop=True, skip_group_check=True)
                    s["p"].inc(i)
                    s["p"].wait(nc.scalar)
                    i = nc.scalar.copy(out=srep[:, :], in_=mrps[:, :])
                    s["sc"].inc(i)
                    s["sc"].wait(nc.vector)
                    i = nc.vector.tensor_tensor(obf[:, c, :], ofm[:, c, :], srep[:, :], ALU.mult)
                    s["v"].inc(i)
                    s["v"].wait(nc.tensor)
                s["q"].wait(nc.vector)
                for dc in range(8):
                    ps = sps0 if dc % 2 == 0 else sps1
                    s["v"].wait(nc.tensor)
                    for ic in range(4):
                        i = nc.tensor.matmul(ps[:, :], wo[:, ic, 128 * dc:128 * (dc + 1)],
                                             obf[:, ic, :], start=(ic == 0), stop=(ic == 3))
                    s["p"].inc(i)
                    s["p"].wait(nc.vector)
                    i = nc.vector.scalar_tensor_tensor(out=resid[:, dc, :], in0=ps[:, :],
                                                       scalar=1.0, in1=resid[:, dc, :],
                                                       op0=ALU.bypass, op1=ALU.add)
                    s["v"].inc(i)
                s["v"].wait(nc.gpsimd)
                ctx.qg.dma(nc.gpsimd.dma_start(
                    out=out_stage.rearrange("(c p) t -> p c t", p=128)[:, :, q0w:q0w + 512],
                    in_=resid[:, :, :]))
                ctx.qg.wait(nc.vector)
        ctx.barrier()
        pb_es.close()
    les.close()


def emit_mlp(ctx, layer, t0, in_stage, out_stage, final):
    nc = ctx.nc
    d = ctx.d
    s = {"q": ctx.qs, "g": ctx.qg, "p": ctx.p, "v": ctx.v, "sc": ctx.sc}
    sflag = ctx.b["sflag"]
    nch = (RT - t0 + 511) // 512
    mes = ExitStack()
    w1 = mes.enter_context(nc.sbuf_tensor("ml_w1_%d" % layer, [128, 8, 5632], BF16))
    w2s = mes.enter_context(nc.sbuf_tensor("ml_w2_%d" % layer, [128, 2, 1024], BF16))
    zin = mes.enter_context(nc.sbuf_tensor("ml_zi_%d" % layer, [128, 8, 513], F32))
    lnh = mes.enter_context(nc.sbuf_tensor("ml_ln_%d" % layer, [128, 8, 512], BF16))
    hh = mes.enter_context(nc.sbuf_tensor("ml_hh_%d" % layer, [128, 22, 512], BF16))
    gl = mes.enter_context(nc.sbuf_tensor("ml_gl_%d" % layer, [128, 512], BF16))
    x1t = mes.enter_context(nc.sbuf_tensor("ml_x1_%d" % layer, [128, 8, 512], F32))
    sq = mes.enter_context(nc.sbuf_tensor("ml_sq_%d" % layer, [128, 512], F32))
    stat_sb = mes.enter_context(nc.sbuf_tensor("ml_st_%d" % layer, [33, 512], F32))
    rep_sb = mes.enter_context(nc.sbuf_tensor("ml_rp_%d" % layer, [128, 1024], F32))
    lng = mes.enter_context(nc.sbuf_tensor("ml_g_%d" % layer, [128, 8], F32))
    lnb = mes.enter_context(nc.sbuf_tensor("ml_b_%d" % layer, [128, 8], F32))
    b1c = mes.enter_context(nc.sbuf_tensor("ml_b1_%d" % layer, [128, 44], F32))
    b2c = mes.enter_context(nc.sbuf_tensor("ml_b2_%d" % layer, [128, 8], F32))
    psA = mes.enter_context(nc.psum_tensor("ml_pa_%d" % layer, [128, 512], F32))
    psB = mes.enter_context(nc.psum_tensor("ml_pb_%d" % layer, [128, 512], F32))
    psZ = mes.enter_context(nc.psum_tensor("ml_pz_%d" % layer, [128, 512], F32))
    psZ2 = mes.enter_context(nc.psum_tensor("ml_pz2_%d" % layer, [128, 512], F32))
    stat_ps = mes.enter_context(nc.psum_tensor("ml_stat_%d" % layer, [33, 512], F32))
    rep_ps = mes.enter_context(nc.psum_tensor("ml_rep_%d" % layer, [128, 1024], F32))
    if True:
        ctx.qg.dma(nc.gpsimd.dma_start(
            out=w1[:], in_=d["w1p"][layer].rearrange("(c p) f -> p c f", p=128)))
        s["q"].dma(nc.sync.dma_start(out=lng[:], in_=d["lng_col"][layer]))
        s["q"].dma(nc.sync.dma_start(out=lnb[:], in_=d["lnb_col"][layer]))
        s["q"].dma(nc.sync.dma_start(out=b1c[:], in_=d["b1p_col"][layer]))
        s["q"].dma(nc.sync.dma_start(out=b2c[:], in_=d["b2_col"][layer]))
        mu_sb = mes.enter_context(nc.sbuf_tensor("ml_mu_%d" % layer, [1, 512], F32))
        lnws = dict(sq=sq, stat=stat_ps, rep=rep_ps, stat_sb=stat_sb, rep_sb=rep_sb, mu_sb=mu_sb)
        for ci in range(nch):
            start = t0 + 512 * ci
            tc = min(512, RT - start)
            s["v"].wait(nc.sync)
            s["q"].dma(nc.sync.dma_start(
                out=zin[:, :, :tc + 1],
                in_=in_stage.rearrange("(c p) t -> p c t", p=128)[:, :, start - 1:start + tc]))
            s["q"].wait(nc.vector)
            s["q"].wait(nc.scalar)
            s["q"].wait(nc.tensor)
            if start - 1 <= HALO - 1 < start + tc:
                loc = HALO - start
                i = nc.vector.tensor_scalar(out=zin[:, 4:8, loc:loc + 1],
                                            in0=zin[:, 4:8, loc:loc + 1],
                                            scalar1=sflag[:, 0:1], scalar2=None,
                                            op0=ALU.mult)
                s["v"].inc(i)
                s["v"].wait(nc.tensor)
                s["v"].wait(nc.scalar)
            hviews = [zin[:, c, 1:1 + tc] for c in range(4)] + [zin[:, c, 0:tc] for c in range(4, 8)]
            emit_ln_fm(ctx, s, hviews, [lnh[:, c, :tc] for c in range(8)], tc, lnws,
                       gain_cols=[lng[:, c:c + 1] for c in range(8)],
                       beta_cols=[lnb[:, c:c + 1] for c in range(8)])
            s["v"].wait(nc.tensor)
            if ci == 0:
                ctx.qg.wait(nc.tensor)
            for ih in range(22):
                for c in range(8):
                    nc.tensor.matmul(psA[:, :tc], w1[:, c, 128 * ih:128 * (ih + 1)],
                                     lnh[:, c, :tc], start=(c == 0), stop=(c == 7))
                for c in range(8):
                    i = nc.tensor.matmul(psB[:, :tc], w1[:, c, 2816 + 128 * ih:2816 + 128 * (ih + 1)],
                                         lnh[:, c, :tc], start=(c == 0), stop=(c == 7))
                s["p"].inc(i)
                s["p"].wait(nc.scalar)
                i = nc.scalar.activation(out=gl[:, :tc], in_=psB[:, :tc], func=AF.Gelu,
                                         bias=b1c[:, 22 + ih:23 + ih])
                s["sc"].inc(i)
                s["sc"].wait(nc.vector)
                i = nc.vector.scalar_tensor_tensor(out=hh[:, ih, :tc], in0=psA[:, :tc],
                                                   scalar=b1c[:, ih:ih + 1], in1=gl[:, :tc],
                                                   op0=ALU.add, op1=ALU.mult)
                s["v"].inc(i)
                s["v"].wait(nc.tensor)
            if final:
                s["q"].dma(nc.sync.dma_start(
                    out=x1t[:, :, :tc],
                    in_=d["x1_fm"].rearrange("(c p) t -> p c t", p=128)[:, :, start:start + tc]))
                s["q"].wait(nc.vector)
            for dc in range(8):
                psz = psZ if dc % 2 == 0 else psZ2
                for ic in range(22):
                    wbi = ic % 2
                    if ic >= 2 or (dc > 0 and ic < 2):
                        s["p"].wait(nc.gpsimd)
                    ctx.qg.dma(nc.gpsimd.dma_start(
                        out=w2s[:, wbi, :],
                        in_=d["w2p"][layer, 128 * ic:128 * (ic + 1), :]))
                    ctx.qg.wait(nc.tensor)
                    i = nc.tensor.matmul(psz[:, :tc], w2s[:, wbi, 128 * dc:128 * (dc + 1)],
                                         hh[:, ic, :tc], start=(ic == 0), stop=(ic == 21))
                    s["p"].inc(i)
                s["p"].wait(nc.vector)
                i = nc.vector.scalar_tensor_tensor(out=zin[:, dc, 1:1 + tc], in0=psz[:, :tc],
                                                   scalar=b2c[:, dc:dc + 1],
                                                   in1=zin[:, dc, 1:1 + tc],
                                                   op0=ALU.add, op1=ALU.add)
                if final:
                    i = nc.vector.tensor_tensor(zin[:, dc, 1:1 + tc], zin[:, dc, 1:1 + tc],
                                                x1t[:, dc, :tc], ALU.add)
                s["v"].inc(i)
                s["v"].wait(nc.tensor)
            s["v"].wait(nc.gpsimd)
            if final:
                ctx.qg.dma(nc.gpsimd.dma_start(
                    out=d["out_fm"].rearrange("(c p) t -> p c t", p=128)[:, :, start - HALO:start - HALO + tc],
                    in_=zin[:, :, 1:1 + tc]))
            else:
                ctx.qg.dma(nc.gpsimd.dma_start(
                    out=out_stage.rearrange("(c p) t -> p c t", p=128)[:, :, start:start + tc],
                    in_=zin[:, :, 1:1 + tc]))
            ctx.qg.wait(nc.vector)
            ctx.qg.wait(nc.sync)
    ctx.barrier()
    mes.close()


_IN_SPECS = {
    "x_tok": ([T, D], F32), "x_rows_tok": ([RT, D], F32), "x_rows_fm": ([D, RT], F32),
    "sim_slab": ([128, None], BF16),
    "rot_cq": ([128, RT - 128], F32), "rot_sq": ([128, RT - 128], F32),
    "rot_ck": ([128, RT - 128], F32), "rot_sk": ([128, RT - 128], F32),
    "la_flags": ([128, 96], F32), "shift_flag": ([128, 1], F32),
    "la_mask": ([128, 8, 512], BF16),
    "iden": ([128, 128], BF16), "e8": ([128, 4, 8], F32), "e8t": ([8, 4, 128], F32),
    "ones_col": ([128, 1], BF16), "ones_row": ([1, 128], F32),
    "L0_attn_W_q": ([32, D], F32), "L0_attn_W_k": ([32, D], F32),
    "L0_attn_gain": ([1, D], F32), "L0_attn_bias": ([1, D], F32),
    "log_mix": ([1, 1], F32), "log_scale": ([1, 1], F32),
    "L0_ffn_wqkv": ([2, D, 1536], F32), "L0_ffn_wo": ([2, 512, D], F32),
    "qscale_col": ([2, 128, 1], F32), "kscale_col": ([2, 128, 1], F32),
    "lng_col": ([2, 128, 8], F32), "lnb_col": ([2, 128, 8], F32),
    "w1p": ([2, D, 5632], F32), "b1p_col": ([2, 128, 44], F32),
    "w2p": ([2, 2816, D], F32), "b2_col": ([2, 128, 8], F32),
    "variant": ([1, 1], U32),
}
_SLAB_W = max(len(sim_slab_order(0)), len(sim_slab_order(1))) * 512


def build_program():
    nc0 = bass.Bass("TRN2", target_bir_lowering=False)
    nc = NCWrap(nc0)
    stack = ExitStack()
    ctx = Ctx(nc, stack)
    d = ctx.d
    for name, (shape, dt) in _IN_SPECS.items():
        shp = [s if s is not None else _SLAB_W for s in shape]
        d[name] = nc.dram_tensor(name, shp, dt, kind="ExternalInput").ap()
    for name, shape, dt in (
            ("xn_tok", [T, D], BF16), ("xn_fm", [D, T], BF16),
            ("xnrows", [D, RT], BF16), ("xnrows_tok", [RT, D], BF16),
            ("x1_fm", [D, RT], F32), ("z0_fm", [D, RT], F32),
            ("z1_fm", [D, RT], F32), ("z2_fm", [D, RT], F32),
            ("z3_fm", [D, RT], F32)):
        d[name] = nc.dram_tensor(name, shape, dt, kind="Internal").ap()
    d["out_fm"] = nc.dram_tensor("out_fm", [D, CH], F32, kind="ExternalOutput").ap()
    if DEBUG_LA:
        d["dbg_kraw"] = nc.dram_tensor("dbg_kraw", [128, 4, 512], F32, kind="Internal").ap()
        d["dbg_hbf"] = nc.dram_tensor("dbg_hbf", [128, 8, 512], BF16, kind="Internal").ap()
        d["dbg_r8"] = nc.dram_tensor("dbg_r8", [8, 512], F32, kind="Internal").ap()
        d["dbg_qh"] = nc.dram_tensor("dbg_qh", [128, 4, RT - CH - 512], BF16, kind="Internal").ap()
        d["dbg_kh"] = nc.dram_tensor("dbg_kh", [128, 4, RT - CH], BF16, kind="Internal").ap()
        d["dbg_vt"] = nc.dram_tensor("dbg_vt", [128, (RT - CH) // 128, 512], BF16, kind="Internal").ap()

    with stack:
        ctx.attn_es = ExitStack()
        vf = ctx.sb("vflag_sb", [1, 1], U32)
        ctx.qs.dma(nc.sync.dma_start(out=vf[:], in_=d["variant"][:]))

        emit_prepass(ctx)
        if PHASES < 1.5:
            return nc0

        engines = [nc.sync, nc.tensor, nc.vector, nc.scalar, nc.gpsimd]
        regs = nc.alloc_registers("vreg")
        for eng in engines:
            ctx.qs.wait(eng)
            eng.reg_load(regs[eng.engine], vf[0:1, 0:1])
        iss = {"q": nc.sync, "g": nc.gpsimd}
        sA = {k: Sem(stack, nc, "a0_" + k, issuer=iss.get(k)) for k in ("q", "g", "p", "v", "sc")}
        sB = {k: Sem(stack, nc, "a1_" + k, issuer=iss.get(k)) for k in ("q", "g", "p", "v", "sc")}
        with nc.If_eq(regs, 0):
            emit_attn(ctx, 0, sA)
        with nc.Else():
            emit_attn(ctx, 1, sB)
        ctx.attn_es.close()
        if PHASES < 2.5:
            return nc0

        emit_la(ctx, 0, B1_T0, LA1_Q0, LA1_NW, 0, d["x1_fm"], True,
                d["z0_fm"], d["z0_fm"], d["z1_fm"])
        if PHASES < 3.5:
            return nc0
        emit_mlp(ctx, 0, B2_T0, d["z1_fm"], d["z2_fm"], final=False)
        if PHASES < 4.5:
            return nc0
        emit_la(ctx, 1, B3_T0, LA2_Q0, LA2_NW, 56, d["z2_fm"], False,
                None, d["z2_fm"], d["z3_fm"])
        if PHASES < 5.5:
            return nc0
        emit_mlp(ctx, 1, B4_T0, d["z3_fm"], None, final=True)
    return nc0


_NC_CACHE = None
DEBUG_LA = False
PHASES = 6


def make_in_maps(inputs, cids=range(8)):
    in_maps = []
    for cid in cids:
        m = _host_core_inputs(inputs, cid)
        sl = m["sim_slab"]
        if sl.shape[1] < _SLAB_W:
            m["sim_slab"] = np.concatenate(
                [sl, np.zeros((128, _SLAB_W - sl.shape[1]), dtype=BF)], axis=1)
        w1 = m.pop("L0_ffn_w1")
        w1p = np.zeros((2, D, 5632), dtype=np.float32)
        w1p[:, :, 0:HID] = w1[:, :, 0:HID]
        w1p[:, :, 2816:2816 + HID] = w1[:, :, HID:2 * HID]
        m["w1p"] = w1p
        w2 = m.pop("L0_ffn_w2")
        w2p = np.zeros((2, 2816, D), dtype=np.float32)
        w2p[:, 0:HID, :] = w2
        m["w2p"] = w2p
        b1 = m.pop("L0_ffn_b1")
        b1p = np.zeros((2, 5632), dtype=np.float32)
        b1p[:, 0:HID] = b1[:, 0:HID]
        b1p[:, 2816:2816 + HID] = b1[:, HID:2 * HID]
        m["b1p_col"] = np.ascontiguousarray(
            b1p.reshape(2, 44, 128).transpose(0, 2, 1))
        m["b2_col"] = np.ascontiguousarray(
            m.pop("L0_ffn_b2").reshape(2, 8, 128).transpose(0, 2, 1))
        m["lng_col"] = np.ascontiguousarray(
            m.pop("L0_ffn_ln_g").reshape(2, 8, 128).transpose(0, 2, 1))
        m["lnb_col"] = np.ascontiguousarray(
            m.pop("L0_ffn_ln_b").reshape(2, 8, 128).transpose(0, 2, 1))
        m["qscale_col"] = np.ascontiguousarray(
            np.tile(m.pop("L0_ffn_q_scale"), (1, 2))[:, :, None])
        m["kscale_col"] = np.ascontiguousarray(
            np.tile(m.pop("L0_ffn_k_scale"), (1, 2))[:, :, None])
        in_maps.append(m)
    return in_maps


def kernel(**inputs):
    global _NC_CACHE
    in_maps = make_in_maps(inputs)
    if _NC_CACHE is None:
        _NC_CACHE = build_program()
    res = run_bass_kernel_spmd(_NC_CACHE, in_maps, core_ids=list(range(8)))
    out = np.empty((B, T, D), dtype=np.float32)
    for cid in range(8):
        b, v = cid // 4, cid % 4
        out[b, v * CH:(v + 1) * CH, :] = np.asarray(res.results[cid]["out_fm"]).T
    return out


# version 03
# speedup vs baseline: 1.5187x; 1.5187x over previous
import math
from contextlib import ExitStack

import numpy as np
import ml_dtypes

import concourse.bass as bass
import concourse.mybir as mybir
from concourse.bass_utils import run_bass_kernel_spmd

F32 = mybir.dt.float32
BF16 = mybir.dt.bfloat16
U32 = mybir.dt.uint32
AF = mybir.ActivationFunctionType
ALU = mybir.AluOpType
AX = mybir.AxisListType
BF = ml_dtypes.bfloat16

B, T, D = 2, 8192, 1024
CH = 2048
HALO = 2176
RT = CH + HALO
NT = RT // 128
G = 3
NEG = -3.0e38
NEGH = -1.5e38
C_OFF = 4096.0
HEADS, DH, WIN = 8, 64, 512
INNER = 512
HID = 2730
QK_SCALE = 8.0

B1_T0 = 128
LA1_Q0 = 640
B2_T0 = 1024
B3_T0 = 1152
LA2_Q0 = 1664
B4_T0 = HALO
LA1_NW = (RT - LA1_Q0) // WIN
LA2_NW = (RT - LA2_Q0) // WIN


def variant_widths(ref_c0):
    ws = []
    for i in range(NT):
        row_end = ref_c0 - HALO + 128 * (i + 1)
        w = min(T, max(512, ((max(row_end, 1) + 511) // 512) * 512))
        ws.append(w)
    for c0 in (ref_c0 - CH, ref_c0):
        rel = (HALO - c0) // 128
        if 0 <= rel < NT:
            ws[rel] = T
    return ws


WIDTHS = [variant_widths(2 * CH), variant_widths(4 * CH)]
SCORE_SLAB_TILES = [set(), set()]
for _vi, _ref in ((0, 2 * CH), (1, 4 * CH)):
    for _c0 in (_ref - CH, _ref):
        _rel = (HALO - _c0) // 128
        if 0 <= _rel < NT:
            SCORE_SLAB_TILES[_vi].add(_rel)


def attn_groups(vi):
    ws = WIDTHS[vi]
    out = []
    for g0 in range(0, NT, G):
        tiles = list(range(g0, min(g0 + G, NT)))
        out.append((tiles, max(ws[t] for t in tiles), [ws[t] for t in tiles]))
    return out


def sim_slab_order(vi):
    order = []
    for tiles, wmax, ws in attn_groups(vi):
        for sc in range(wmax // 512):
            for j, t in enumerate(tiles):
                if ws[j] > sc * 512:
                    order.append((t, sc))
    return order


_ROT = None


def _rotary_tables():
    half = np.arange(0, DH, 2, dtype=np.float64)
    inv_freq = 1.0 / (10000.0 ** (half / DH))
    t = np.arange(T, dtype=np.float64)
    freqs = t[:, None] * inv_freq[None, :]
    base = (half + 0.4 * DH) / (1.4 * DH)
    power = (t - T // 2) / float(WIN // 2)
    scale = base[None, :] ** power[:, None]
    cos, sin = np.cos(freqs), np.sin(freqs)

    def mk(sc):
        Cfull = np.empty((64, T))
        Sfull = np.empty((64, T))
        for j in range(64):
            Cfull[j] = cos[:, j % 32] * sc[:, j % 32]
            if j < 32:
                Sfull[j] = sin[:, j] * sc[:, j]
            else:
                Sfull[j] = -sin[:, j - 32] * sc[:, j - 32]
        return (np.tile(Cfull, (2, 1)).astype(np.float32),
                np.tile(Sfull, (2, 1)).astype(np.float32))

    Cq, Sq = mk(scale)
    Ck, Sk = mk(1.0 / scale)
    return Cq, Sq, Ck, Sk


def _host_core_inputs(inputs, cid):
    global _ROT
    if _ROT is None:
        _ROT = _rotary_tables()
    Cq, Sq, Ck, Sk = _ROT
    b, v = cid // 4, cid % 4
    vi = 0 if v <= 1 else 1
    c0 = v * CH
    lo = c0 - HALO
    x = np.asarray(inputs["L0_attn_x"][b], dtype=np.float32)

    rows = np.zeros((RT, D), dtype=np.float32)
    src0 = max(0, lo)
    rows[src0 - lo:] = x[src0: c0 + CH]
    m = {
        "x_tok": np.ascontiguousarray(x),
        "x_rows_tok": rows,
        "x_rows_fm": np.ascontiguousarray(rows.T),
    }

    ws = WIDTHS[vi]
    parts = []
    col = np.arange(512, dtype=np.int64)
    for (t, sc) in sim_slab_order(vi):
        r = lo + 128 * t + np.arange(128, dtype=np.int64)
        cols = 512 * sc + col
        mask = np.where((cols[None, :] < r[:, None]) & (r[:, None] >= 0), 0.0, NEG)
        parts.append(mask.astype(BF))
    m["sim_slab"] = np.ascontiguousarray(np.concatenate(parts, axis=1))

    def rot_slice(tab, fill):
        out = np.full((128, RT - 128), fill, dtype=np.float32)
        a0 = c0 - CH
        s0 = max(0, a0)
        out[:, s0 - a0:] = tab[:, s0:c0 + CH]
        return out

    m["rot_cq"] = rot_slice(Cq, 1.0)
    m["rot_sq"] = rot_slice(Sq, 0.0)
    m["rot_ck"] = rot_slice(Ck, 1.0)
    m["rot_sk"] = rot_slice(Sk, 0.0)

    flags = np.zeros(96, dtype=np.float32)
    if v == 0:
        for s in range(LA1_NW):
            for kc in range(8):
                if lo + B1_T0 + WIN * s + 128 * kc < 0:
                    flags[s * 8 + kc] = NEGH
        for s in range(LA2_NW):
            for kc in range(8):
                if lo + B3_T0 + WIN * s + 128 * kc < 0:
                    flags[56 + s * 8 + kc] = NEGH
    m["la_flags"] = np.tile(flags[None, :], (128, 1)).astype(np.float32)
    m["shift_flag"] = np.full((128, 1), 0.0 if v == 0 else 1.0, dtype=np.float32)

    pp_ = np.arange(128)[:, None, None]
    kc_ = np.arange(8)[None, :, None]
    q_ = np.arange(512)[None, None, :]
    diff = q_ - 128 * kc_ + 512 - pp_
    m["la_mask"] = np.where((diff >= 0) & (diff <= 512), 0.0, NEGH).astype(BF)

    m["iden"] = np.eye(128).astype(BF)
    e8 = np.zeros((4, 128, 8), dtype=np.float32)
    e8t = np.zeros((4, 8, 128), dtype=np.float32)
    for c in range(4):
        for hh in range(2):
            h = 2 * c + hh
            e8[c, 64 * hh:64 * hh + 64, h] = 1.0
            e8t[c, h, 64 * hh:64 * hh + 64] = 1.0
    m["e8"] = np.ascontiguousarray(e8.transpose(1, 0, 2))
    m["e8t"] = np.ascontiguousarray(e8t.transpose(1, 0, 2))
    m["ones_col"] = np.ones((128, 1), dtype=BF)
    m["ones_row"] = np.ones((1, 128), dtype=np.float32)

    for k in ("L0_attn_W_q", "L0_attn_W_k", "L0_attn_gain", "L0_attn_bias",
              "L0_ffn_wqkv", "L0_ffn_q_scale", "L0_ffn_k_scale", "L0_ffn_wo",
              "L0_ffn_ln_g", "L0_ffn_ln_b", "L0_ffn_w1", "L0_ffn_b1",
              "L0_ffn_w2", "L0_ffn_b2"):
        a = np.asarray(inputs[k], dtype=np.float32)
        m[k] = np.ascontiguousarray(a.reshape(a.shape if a.ndim > 1 else (1, -1)))
    m["log_mix"] = np.asarray(inputs["L0_attn_log_mix"], np.float32).reshape(1, 1)
    m["log_scale"] = np.asarray(inputs["L0_attn_log_scale"], np.float32).reshape(1, 1)
    m["variant"] = np.asarray([[vi]], dtype=np.uint32)
    return m


class _AutoDrainEng:
    _SKIP = {"wait_ge", "drain", "dma_start", "dma_start_transpose", "reg_load",
             "alloc_register", "nop", "engine_nop", "wait_op", "sem_inc",
             "indirect_dma_start", "If_eq", "Else", "engine", "memzero"}

    def __init__(self, eng):
        object.__setattr__(self, "_e", eng)

    def __getattr__(self, n):
        a = getattr(self._e, n)
        if (not callable(a)) or n in self._SKIP or n[0].isupper():
            return a
        eng = self._e

        def w(*args, **kw):
            r = a(*args, **kw)
            eng.drain()
            return r
        return w


class NCWrap:
    def __init__(self, nc):
        object.__setattr__(self, "_nc", nc)
        object.__setattr__(self, "vector", _AutoDrainEng(nc.vector))
        object.__setattr__(self, "scalar", _AutoDrainEng(nc.scalar))
        object.__setattr__(self, "gpsimd", _AutoDrainEng(nc.gpsimd))

    def __getattr__(self, n):
        return getattr(self._nc, n)


class Sem:
    def __init__(self, stack, nc, name, issuer=None):
        self.h = stack.enter_context(nc.semaphore(name))
        self.n = 0
        self.issuer = issuer
        self.ckpt = 0

    def inc(self, instr, k=1):
        instr.then_inc(self.h, k)
        self.n += k
        return instr

    def dma(self, instr):
        return self.inc(instr, 16)

    def wait(self, eng, at=None):
        v = self.n if at is None else at
        if self.issuer is not None and at is None and self.ckpt != self.n:
            self.issuer.wait_ge(self.h, self.n)
            self.ckpt = self.n
        eng.wait_ge(self.h, v)


class Ctx:
    def __init__(self, nc, stack):
        self.nc = nc
        self.stack = stack
        self.d = {}
        self.b = {}
        issuers = {"qs": nc.sync, "qg": nc.gpsimd, "qa": nc.scalar}
        for name in ("qs", "qg", "qa", "p", "v", "sc", "pl"):
            setattr(self, name, Sem(stack, nc, "g_" + name, issuer=issuers.get(name)))
        self.sems = [self.qs, self.qg, self.qa, self.p, self.v, self.sc, self.pl]

    def sb(self, name, shape, dtype, stack=None, **kw):
        t = (stack or self.stack).enter_context(
            self.nc.sbuf_tensor("sb_" + name, shape, dtype, **kw))
        self.b[name] = t
        return t

    def barrier(self, engines=None):
        nc = self.nc
        engs = engines or (nc.sync, nc.tensor, nc.vector, nc.scalar, nc.gpsimd)
        for eng in engs:
            for s in self.sems:
                s.wait(eng)


def emit_ln_fm(ctx, s, x_chunks, out_chunks, tcols, ws,
               gain_cols=None, beta_cols=None):
    nc = ctx.nc
    ones = ctx.b["ones_col_f32"]
    onesr = ctx.b["ones_row"]
    sq = ws["sq"]
    stat_ps = ws["stat"]
    rep_ps = ws["rep"]
    stat_sb = ws["stat_sb"]
    rep_sb = ws["rep_sb"]
    tc = tcols
    s["v"].wait(nc.scalar)
    s["v"].wait(nc.tensor)
    for c in range(8):
        s["sc"].wait(nc.tensor)
        nc.tensor.matmul(stat_ps[0:1, :tc], ones[:, 0:1], x_chunks[c],
                         start=(c == 0), stop=(c == 7))
        s["p"].wait(nc.scalar)
        i = nc.scalar.activation(out=sq[:, :tc], in_=x_chunks[c], func=AF.Square)
        s["sc"].inc(i)
        s["sc"].wait(nc.tensor)
        i = nc.tensor.matmul(stat_ps[32:33, :tc], ones[:, 0:1], sq[:, :tc],
                             start=(c == 0), stop=(c == 7))
        s["p"].inc(i)
    s["p"].wait(nc.vector)
    mu = ws["mu_sb"][0:1, 0:tc]
    va = stat_sb[0:1, 0:tc]
    nc.vector.tensor_scalar(out=mu, in0=stat_ps[0:1, :tc], scalar1=1.0 / D,
                            scalar2=None, op0=ALU.mult)
    nc.vector.tensor_scalar(out=va, in0=stat_ps[32:33, :tc], scalar1=1.0 / D,
                            scalar2=None, op0=ALU.mult)
    nc.vector.tensor_tensor(rep_sb[0:1, :tc], mu, mu, ALU.mult)
    nc.vector.tensor_tensor(va, va, rep_sb[0:1, :tc], ALU.subtract)
    i = nc.vector.tensor_scalar(out=va, in0=va, scalar1=1e-5, scalar2=None, op0=ALU.add)
    s["v"].inc(i)
    s["v"].wait(nc.scalar)
    nc.scalar.activation(out=va, in_=va, func=AF.Ln)
    i = nc.scalar.activation(out=va, in_=va, func=AF.Exp, scale=-0.5)
    s["sc"].inc(i)
    s["sc"].wait(nc.vector)
    i = nc.vector.tensor_tensor(mu, mu, va, ALU.mult)
    s["v"].inc(i)
    s["v"].wait(nc.tensor)
    nc.tensor.matmul(rep_ps[:, 0:tc], onesr[0:1, :], va, start=True, stop=True)
    i = nc.tensor.matmul(rep_ps[:, tc:2 * tc], onesr[0:1, :], mu, start=True, stop=True)
    s["p"].inc(i)
    s["p"].wait(nc.scalar)
    nc.scalar.copy(out=rep_sb[:, 0:tc], in_=rep_ps[:, 0:tc])
    i = nc.scalar.copy(out=rep_sb[:, tc:2 * tc], in_=rep_ps[:, tc:2 * tc])
    s["sc"].inc(i)
    s["sc"].wait(nc.vector)
    for c in range(8):
        nc.vector.scalar_tensor_tensor(out=sq[:, :tc], in0=x_chunks[c],
                                       scalar=1.0, in1=rep_sb[:, 0:tc],
                                       op0=ALU.bypass, op1=ALU.mult)
        if gain_cols is None:
            i = nc.vector.tensor_tensor(out_chunks[c], sq[:, :tc],
                                        rep_sb[:, tc:2 * tc], ALU.subtract)
        else:
            nc.vector.tensor_tensor(sq[:, :tc], sq[:, :tc],
                                    rep_sb[:, tc:2 * tc], ALU.subtract)
            i = nc.vector.tensor_scalar(out=out_chunks[c], in0=sq[:, :tc],
                                        scalar1=gain_cols[c], scalar2=beta_cols[c],
                                        op0=ALU.mult, op1=ALU.add)
    s["v"].inc(i)


def emit_ln_tok_sweep(ctx, src_ap, dst_ap, nrows):
    nc = ctx.nc
    q = ctx.qs
    with (
        nc.sbuf_tensor("lt_x" + dst_ap.tensor.name, [128, 2, 1024], F32) as xt,
        nc.sbuf_tensor("lt_n" + dst_ap.tensor.name, [128, 2, 1024], BF16) as xnt,
        nc.sbuf_tensor("lt_s" + dst_ap.tensor.name, [128, 2, 4], F32) as stb,
    ):
        for i_t in range(nrows // 128):
            bi = i_t % 2
            if i_t >= 1:
                ctx.v.wait(nc.sync)
                ctx.sc.wait(nc.sync)
            q.dma(nc.sync.dma_start(out=xt[:, bi, :],
                                    in_=src_ap[128 * i_t:128 * (i_t + 1), :]))
            q.wait(nc.vector)
            q.wait(nc.scalar)
            ctx.qg.wait(nc.scalar)
            ctx.qg.wait(nc.vector)
            nc.vector.reduce_sum(stb[:, bi, 0:1], xt[:, bi, :], axis=AX.X)
            i = nc.vector.tensor_scalar(out=stb[:, bi, 0:1], in0=stb[:, bi, 0:1],
                                        scalar1=1.0 / D, scalar2=None, op0=ALU.mult)
            ctx.v.inc(i)
            i = nc.scalar.activation(out=xnt[:, bi, :], in_=xt[:, bi, :],
                                     func=AF.Square, accum_out=stb[:, bi, 1:2])
            ctx.sc.inc(i)
            ctx.v.wait(nc.scalar)
            ctx.sc.wait(nc.vector)
            nc.vector.tensor_scalar(out=stb[:, bi, 1:2], in0=stb[:, bi, 1:2],
                                    scalar1=1.0 / D, scalar2=None, op0=ALU.mult)
            nc.vector.tensor_tensor(stb[:, bi, 2:3], stb[:, bi, 0:1],
                                    stb[:, bi, 0:1], ALU.mult)
            nc.vector.tensor_tensor(stb[:, bi, 1:2], stb[:, bi, 1:2],
                                    stb[:, bi, 2:3], ALU.subtract)
            i = nc.vector.tensor_scalar(out=stb[:, bi, 1:2], in0=stb[:, bi, 1:2],
                                        scalar1=1e-5, scalar2=None, op0=ALU.add)
            ctx.v.inc(i)
            ctx.v.wait(nc.scalar)
            nc.scalar.activation(out=stb[:, bi, 1:2], in_=stb[:, bi, 1:2], func=AF.Ln)
            i = nc.scalar.activation(out=stb[:, bi, 1:2], in_=stb[:, bi, 1:2],
                                     func=AF.Exp, scale=-0.5)
            ctx.sc.inc(i)
            ctx.sc.wait(nc.vector)
            i = nc.vector.tensor_scalar(out=xnt[:, bi, :], in0=xt[:, bi, :],
                                        scalar1=stb[:, bi, 0:1], scalar2=stb[:, bi, 1:2],
                                        op0=ALU.subtract, op1=ALU.mult)
            ctx.v.inc(i)
            ctx.v.wait(nc.gpsimd)
            ctx.qg.dma(nc.gpsimd.dma_start(out=dst_ap[128 * i_t:128 * (i_t + 1), :],
                                           in_=xnt[:, bi, :]))
        ctx.qg.wait(nc.sync)
        ctx.qg.wait(nc.scalar)
        ctx.qg.wait(nc.vector)


def emit_prepass(ctx):
    nc = ctx.nc
    d = ctx.d

    iden = ctx.sb("iden_sb", [128, 128], BF16)
    iden32 = ctx.sb("iden32_sb", [128, 128], F32)
    ones_col = ctx.sb("ones_col", [128, 1], BF16)
    ones_col_f = ctx.sb("ones_col_f32", [128, 1], F32)
    ones_row = ctx.sb("ones_row", [1, 128], F32)
    la_mask = ctx.sb("la_mask", [128, 8, 512], BF16)
    e8 = ctx.sb("e8_sb", [128, 4, 8], F32)
    e8t = ctx.sb("e8t_sb", [8, 4, 128], F32)
    laflags = ctx.sb("laflags", [128, 96], F32)
    sflag = ctx.sb("sflag", [128, 1], F32)
    gain_rep = ctx.sb("gain_rep", [128, 1024], F32, stack=ctx.attn_es, side="right")
    bias_rep = ctx.sb("bias_rep", [128, 1024], F32, stack=ctx.attn_es, side="right")
    scal = ctx.sb("scal", [128, 4], F32)
    wqT = ctx.sb("wqT", [128, 8, 32], BF16, stack=ctx.attn_es, side="right")
    wkT = ctx.sb("wkT", [128, 8, 32], BF16, stack=ctx.attn_es, side="right")
    kT = ctx.sb("kT", [32, T], BF16, stack=ctx.attn_es, side="right")
    qT = ctx.sb("qT", [32, RT], BF16, stack=ctx.attn_es, side="right")

    q = ctx.qs
    q.dma(nc.sync.dma_start(out=iden[:], in_=d["iden"][:]))
    q.dma(nc.sync.dma_start(out=la_mask[:], in_=d["la_mask"][:]))
    q.dma(nc.sync.dma_start(out=e8[:], in_=d["e8"][:]))
    q.dma(nc.sync.dma_start(out=e8t[:], in_=d["e8t"][:]))
    q.dma(nc.sync.dma_start(out=laflags[:], in_=d["la_flags"][:]))
    q.dma(nc.sync.dma_start(out=sflag[:], in_=d["shift_flag"][:]))
    q.dma(nc.sync.dma_start(out=ones_col[:], in_=d["ones_col"][:]))
    q.dma(nc.sync.dma_start(out=ones_row[:], in_=d["ones_row"][:]))

    es = ExitStack()
    wq_sb = es.enter_context(nc.sbuf_tensor("pp_w", [32, 1024], F32))
    wk_sb = es.enter_context(nc.sbuf_tensor("pp_w2", [32, 1024], F32))
    row_sb = es.enter_context(nc.sbuf_tensor("pp_row", [1, 1024], F32))
    row2_sb = es.enter_context(nc.sbuf_tensor("pp_row2", [1, 1024], F32))
    sc1 = es.enter_context(nc.sbuf_tensor("pp_sc1", [1, 4], F32))
    pps = es.enter_context(nc.psum_tensor("pp_ps", [128, 512], F32))
    pps2 = es.enter_context(nc.psum_tensor("pp_ps2", [128, 512], F32))
    ppst = es.enter_context(nc.psum_tensor("pp_pst", [128, 128], F32))
    if True:
        q.dma(nc.sync.dma_start(out=wq_sb[:], in_=d["L0_attn_W_q"][:]))
        q.dma(nc.sync.dma_start(out=wk_sb[:], in_=d["L0_attn_W_k"][:]))
        q.dma(nc.sync.dma_start(out=row_sb[:], in_=d["L0_attn_gain"][:]))
        q.dma(nc.sync.dma_start(out=row2_sb[:], in_=d["L0_attn_bias"][:]))
        q.dma(nc.sync.dma_start(out=sc1[0:1, 0:1], in_=d["log_mix"][:]))
        q.dma(nc.sync.dma_start(out=sc1[0:1, 1:2], in_=d["log_scale"][:]))
        q.wait(nc.sync)
        q.wait(nc.vector)
        ctx.v.inc(nc.vector.tensor_copy(out=iden32[:], in_=iden[:]))
        ctx.v.inc(nc.vector.tensor_copy(out=ones_col_f[:], in_=ones_col[:]))
        q.wait(nc.tensor)
        ctx.v.wait(nc.tensor)
        for c in range(8):
            i = nc.tensor.transpose(ppst[0:128, 0:32], wq_sb[:, 128 * c:128 * (c + 1)], iden32[0:32, 0:32])
            ctx.p.inc(i)
            ctx.p.wait(nc.vector)
            ctx.v.inc(nc.vector.tensor_copy(out=wqT[:, c, :], in_=ppst[0:128, 0:32]))
            ctx.v.wait(nc.tensor)
            i = nc.tensor.transpose(ppst[0:128, 0:32], wk_sb[:, 128 * c:128 * (c + 1)], iden32[0:32, 0:32])
            ctx.p.inc(i)
            ctx.p.wait(nc.vector)
            ctx.v.inc(nc.vector.tensor_copy(out=wkT[:, c, :], in_=ppst[0:128, 0:32]))
            ctx.v.wait(nc.tensor)
        nc.tensor.matmul(pps[:, :], ones_row[0:1, :], row_sb[0:1, 0:512], start=True, stop=True)
        i = nc.tensor.matmul(pps2[:, :], ones_row[0:1, :], row_sb[0:1, 512:1024], start=True, stop=True)
        ctx.p.inc(i)
        ctx.p.wait(nc.scalar)
        nc.scalar.copy(out=gain_rep[:, 0:512], in_=pps[:, :])
        ctx.sc.inc(nc.scalar.copy(out=gain_rep[:, 512:1024], in_=pps2[:, :]))
        ctx.sc.wait(nc.tensor)
        nc.tensor.matmul(pps[:, :], ones_row[0:1, :], row2_sb[0:1, 0:512], start=True, stop=True)
        i = nc.tensor.matmul(pps2[:, :], ones_row[0:1, :], row2_sb[0:1, 512:1024], start=True, stop=True)
        ctx.p.inc(i)
        ctx.p.wait(nc.scalar)
        nc.scalar.copy(out=bias_rep[:, 0:512], in_=pps[:, :])
        ctx.sc.inc(nc.scalar.copy(out=bias_rep[:, 512:1024], in_=pps2[:, :]))
        q.wait(nc.scalar)
        ctx.sc.inc(nc.scalar.activation(out=sc1[0:1, 2:3], in_=sc1[0:1, 0:1], func=AF.Sigmoid))
        nc.scalar.activation(out=sc1[0:1, 3:4], in_=sc1[0:1, 1:2], func=AF.Abs)
        nc.scalar.activation(out=sc1[0:1, 3:4], in_=sc1[0:1, 3:4], func=AF.Exp, scale=-1.0)
        ctx.sc.inc(nc.scalar.activation(out=sc1[0:1, 3:4], in_=sc1[0:1, 3:4], func=AF.Ln, bias=1.0))
        ctx.sc.wait(nc.vector)
        nc.vector.tensor_relu(row2_sb[0:1, 0:1], sc1[0:1, 1:2])
        nc.vector.tensor_tensor(row2_sb[0:1, 0:1], row2_sb[0:1, 0:1], sc1[0:1, 3:4], ALU.add)
        nc.vector.tensor_scalar(out=row2_sb[0:1, 2:3], in0=row2_sb[0:1, 0:1],
                                scalar1=0.01, scalar2=None, op0=ALU.add)
        nc.vector.tensor_copy(out=row2_sb[0:1, 0:1], in_=sc1[0:1, 2:3])
        i = nc.vector.tensor_scalar(out=row2_sb[0:1, 1:2], in0=sc1[0:1, 2:3],
                                    scalar1=-1.0, scalar2=1.0, op0=ALU.mult, op1=ALU.add)
        ctx.v.inc(i)
        ctx.v.wait(nc.tensor)
        i = nc.tensor.matmul(pps[:, 0:3], ones_row[0:1, :], row2_sb[0:1, 0:3],
                             start=True, stop=True)
        ctx.p.inc(i)
        ctx.p.wait(nc.scalar)
        ctx.sc.inc(nc.scalar.copy(out=scal[:, 0:3], in_=pps[:, 0:3]))

        emit_ln_tok_sweep(ctx, d["x_tok"], d["xn_tok"], T)
        emit_ln_tok_sweep(ctx, d["x_rows_tok"], d["xnrows_tok"], RT)

        fm_es = ExitStack()
        fm = fm_es.enter_context(nc.sbuf_tensor("pp_fm", [128, 8, T], BF16, is_transpose_dest=True))
        if True:
            for c in range(8):
                ctx.qa.dma(nc.scalar.dma_start(
                    out=fm[:, c, :], in_=d["xn_tok"][:, 128 * c:128 * (c + 1)],
                    transpose=True))
            ctx.qa.wait(nc.sync)
            ctx.qa.wait(nc.tensor)
            for c in range(8):
                q.dma(nc.sync.dma_start(out=d["xn_fm"][128 * c:128 * (c + 1), :],
                                        in_=fm[:, c, :]))
            for sc_i in range(T // 512):
                for c in range(8):
                    i = nc.tensor.matmul(pps[0:32, :], wkT[:, c, :],
                                         fm[:, c, 512 * sc_i:512 * (sc_i + 1)],
                                         start=(c == 0), stop=(c == 7))
                ctx.p.inc(i)
                ctx.p.wait(nc.vector)
                ctx.v.inc(nc.vector.tensor_copy(out=kT[:, 512 * sc_i:512 * (sc_i + 1)],
                                                in_=pps[0:32, :]))
                ctx.v.wait(nc.tensor)
            q.wait(nc.sync)
        ctx.barrier()
        fm_es.close()

        stg_es = ExitStack()
        stg = stg_es.enter_context(nc.sbuf_tensor("pp_w2stg", [128, 2, 11, 512], BF16))
        idx = 0
        for l2 in range(2):
            for g2 in range(2):
                for h2 in range(2):
                    sl2 = idx % 2
                    if idx >= 2:
                        ctx.qs.wait(nc.gpsimd)
                    ctx.qg.dma(nc.gpsimd.dma_start(
                        out=stg[:, sl2],
                        in_=d["w2h"][l2, g2, 11 * h2:11 * h2 + 11].rearrange("i p f -> p i f")))
                    ctx.qg.wait(nc.sync)
                    ctx.qs.dma(nc.sync.dma_start(
                        out=d["w2b"][l2, g2, 11 * h2:11 * h2 + 11].rearrange("i p f -> p i f"),
                        in_=stg[:, sl2]))
                    idx += 1
        ctx.qs.wait(nc.sync)
        stg_es.close()

        sg = {"q": ctx.qs, "g": ctx.qg, "p": ctx.p, "v": ctx.v, "sc": ctx.sc}
        xr = es.enter_context(nc.sbuf_tensor("pp_xr", [128, 8, 512], F32))
        xn = es.enter_context(nc.sbuf_tensor("pp_xn", [128, 8, 512], BF16))
        sq = es.enter_context(nc.sbuf_tensor("pp_sq", [128, 512], F32))
        stat_sb = es.enter_context(nc.sbuf_tensor("pp_stsb", [33, 512], F32))
        mu_sb = es.enter_context(nc.sbuf_tensor("pp_musb", [1, 512], F32))
        rep_sb = es.enter_context(nc.sbuf_tensor("pp_repsb", [128, 1024], F32))
        stat_ps = es.enter_context(nc.psum_tensor("pp_stat", [33, 512], F32))
        rep_ps = es.enter_context(nc.psum_tensor("pp_rep", [128, 1024], F32))
        if True:
            ws = dict(sq=sq, stat=stat_ps, rep=rep_ps, stat_sb=stat_sb, rep_sb=rep_sb, mu_sb=mu_sb)
            for ci in range((RT + 511) // 512):
                t0 = 512 * ci
                tc = min(512, RT - t0)
                q.dma(nc.sync.dma_start(
                    out=xr[:, :, :tc],
                    in_=d["x_rows_fm"].rearrange("(c p) t -> p c t", p=128)[:, :, t0:t0 + tc]))
                q.wait(nc.vector)
                q.wait(nc.scalar)
                q.wait(nc.tensor)
                emit_ln_fm(ctx, sg, [xr[:, c, :tc] for c in range(8)],
                           [xn[:, c, :tc] for c in range(8)], tc, ws)
                ctx.v.wait(nc.gpsimd)
                ctx.qg.dma(nc.gpsimd.dma_start(
                    out=d["xnrows"].rearrange("(c p) t -> p c t", p=128)[:, :, t0:t0 + tc],
                    in_=xn[:, :, :tc]))
                ctx.v.wait(nc.tensor)
                for c in range(8):
                    i = nc.tensor.matmul(pps2[0:32, :tc], wqT[:, c, :], xn[:, c, :tc],
                                         start=(c == 0), stop=(c == 7))
                ctx.p.inc(i)
                ctx.p.wait(nc.vector)
                ctx.v.inc(nc.vector.tensor_copy(out=qT[:, t0:t0 + tc], in_=pps2[0:32, :tc]))
                ctx.v.wait(nc.tensor)
                ctx.qg.wait(nc.sync)
    ctx.barrier()
    es.close()


def emit_attn(ctx, vi, s):
    nc = ctx.nc
    d = ctx.d
    groups = attn_groups(vi)
    order = sim_slab_order(vi)
    slab_off = {}
    off = 0
    for key in order:
        slab_off[key] = off
        off += 512
    qT = ctx.b["qT"]
    kT = ctx.b["kT"]
    iden = ctx.b["iden_sb"]
    iden32 = ctx.b["iden32_sb"]
    scal = ctx.b["scal"]
    gain_rep = ctx.b["gain_rep"]
    bias_rep = ctx.b["bias_rep"]
    ISQ32 = 1.0 / math.sqrt(32.0)

    aes = ExitStack()
    sa = aes.enter_context(nc.sbuf_tensor("at_sa%d" % vi, [128, G, 16, 512], BF16))
    msb = aes.enter_context(nc.sbuf_tensor("at_ms%d" % vi, [128, 16, 512], F32))
    xf = aes.enter_context(nc.sbuf_tensor("at_xf%d" % vi, [128, 2, 8, 512], BF16))
    xt = aes.enter_context(nc.sbuf_tensor("at_xt%d" % vi, [128, 2, 4, 1024], BF16))
    sl = aes.enter_context(nc.sbuf_tensor("at_sl%d" % vi, [128, G, 512], BF16))
    rb = aes.enter_context(nc.sbuf_tensor("at_rb%d" % vi, [128, 8, G * 128], BF16))
    scb = aes.enter_context(nc.sbuf_tensor("at_sb%d" % vi, [128, 2, 512], F32))
    m8 = aes.enter_context(nc.sbuf_tensor("at_m8%d" % vi, [128, G, 128], BF16))
    t8 = aes.enter_context(nc.sbuf_tensor("at_t8%d" % vi, [128, 8], BF16))
    nr = aes.enter_context(nc.sbuf_tensor("at_nr%d" % vi, [128, G, 4], F32))
    aT = aes.enter_context(nc.sbuf_tensor("at_aT%d" % vi, [128, 4, 128], BF16))
    ep = aes.enter_context(nc.sbuf_tensor("at_ep%d" % vi, [128, 3, 1024], F32))
    xnt_t = aes.enter_context(nc.sbuf_tensor("at_xn%d" % vi, [128, 1024], BF16))
    xr_t = aes.enter_context(nc.sbuf_tensor("at_xr%d" % vi, [128, 1024], F32))
    x1T = aes.enter_context(nc.sbuf_tensor("at_xT%d" % vi, [128, 8, 128], F32))
    if True:
        sa_f = sa.ap().rearrange("p g a b -> p (g a b)")
        ms_f = msb.ap().rearrange("p a b -> p (a b)")
        for (tiles, wmax, tws) in groups:
            g0 = tiles[0]
            nsc = wmax // 512
            s["q"].dma(nc.sync.dma_start(
                out=rb[:, :, :len(tiles) * 128],
                in_=d["xnrows"].rearrange("(c p) t -> p c t", p=128)[:, :, 128 * g0:128 * (g0 + len(tiles))]))
            with (
                nc.psum_tensor("ps_sim0_%d_%d" % (vi, g0), [128, 512], F32) as ps0,
                nc.psum_tensor("ps_sim1_%d_%d" % (vi, g0), [128, 512], F32) as ps1,
                nc.psum_tensor("ps_sim2_%d_%d" % (vi, g0), [128, 512], F32) as ps2,
            ):
                simps = [ps0, ps1, ps2]
                for sc_i in range(nsc):
                    bi = sc_i % 2
                    if sc_i >= 2:
                        s["p"].wait(nc.sync)
                    s["q"].dma(nc.sync.dma_start(
                        out=xf[:, bi, :, :],
                        in_=d["xn_fm"].rearrange("(c p) t -> p c t", p=128)[:, :, 512 * sc_i:512 * (sc_i + 1)]))
                    for j, t in enumerate(tiles):
                        if tws[j] <= sc_i * 512:
                            continue
                        if sc_i >= 1:
                            s["v"].wait(nc.sync)
                        s["q"].dma(nc.sync.dma_start(
                            out=sl[:, j, :],
                            in_=d["sim_slab"][:, slab_off[(t, sc_i)]:slab_off[(t, sc_i)] + 512]))
                        s["q"].wait(nc.tensor)
                        s["v"].wait(nc.tensor)
                        for c in range(8):
                            i = nc.tensor.matmul(simps[j][:, :], rb[:, c, 128 * j:128 * (j + 1)],
                                                 xf[:, bi, c, :], start=(c == 0), stop=(c == 7))
                        s["p"].inc(i)
                        s["p"].wait(nc.vector)
                        s["q"].wait(nc.vector)
                        i = nc.vector.scalar_tensor_tensor(
                            out=sa_f[:, 8192 * j + 512 * sc_i:8192 * j + 512 * (sc_i + 1)],
                            in0=simps[j][:, :], scalar=1.0 / 32.0,
                            in1=sl[:, j, :], op0=ALU.mult, op1=ALU.add)
                        s["v"].inc(i)
                        i = nc.vector.max(out=m8[:, j, 8 * sc_i:8 * sc_i + 8],
                                          in_=sa_f[:, 8192 * j + 512 * sc_i:8192 * j + 512 * (sc_i + 1)])
                        s["v"].inc(i)
            with nc.psum_tensor("ps_scr_%d_%d" % (vi, g0), [128, 2, 512], F32) as scps:
                for j, t in enumerate(tiles):
                    ntc = tws[j] // 512
                    s["sc"].wait(nc.vector)
                    i = nc.vector.max(out=t8[:, :], in_=m8[:, j, 0:8 * ntc])
                    s["v"].inc(i)
                    for sc_i in range(ntc):
                        pbi = sc_i % 2
                        s["sc"].wait(nc.tensor)
                        i = nc.tensor.matmul(scps[:, pbi, :], qT[:, 128 * t:128 * (t + 1)],
                                             kT[:, 512 * sc_i:512 * (sc_i + 1)],
                                             start=True, stop=True)
                        s["p"].inc(i)
                        s["p"].wait(nc.scalar)
                        s["v"].wait(nc.scalar)
                        i = nc.scalar.activation(out=scb[:, pbi, :], in_=scps[:, pbi, :],
                                                 func=AF.Copy, bias=C_OFF, scale=ISQ32)
                        s["sc"].inc(i)
                        if t in SCORE_SLAB_TILES[vi]:
                            s["v"].wait(nc.sync)
                            s["q"].dma(nc.sync.dma_start(
                                out=sl[:, 0, :],
                                in_=d["sim_slab"][:, slab_off[(t, sc_i)]:slab_off[(t, sc_i)] + 512]))
                            s["q"].wait(nc.vector)
                            s["sc"].wait(nc.vector)
                            i = nc.vector.tensor_tensor(scb[:, pbi, :], scb[:, pbi, :],
                                                        sl[:, 0, :], ALU.add)
                            s["v"].inc(i)
                        s["sc"].wait(nc.vector)
                        i = nc.vector.scalar_tensor_tensor(
                            out=ms_f[:, 512 * sc_i:512 * (sc_i + 1)],
                            in0=sa_f[:, 8192 * j + 512 * sc_i:8192 * j + 512 * (sc_i + 1)],
                            scalar=t8[:, 7:8],
                            in1=scb[:, pbi, :], op0=ALU.is_ge, op1=ALU.mult)
                        s["v"].inc(i)
                    i = nc.vector.reduce_max(nr[:, j, 0:1], ms_f[:, 0:tws[j]],
                                             axis=AX.X, negate=True)
                    s["v"].inc(i)
                    s["v"].wait(nc.scalar)
                    i = nc.scalar.activation(
                        out=sa_f[:, 8192 * j:8192 * j + tws[j]],
                        in_=ms_f[:, 0:tws[j]],
                        func=AF.Exp, bias=nr[:, j, 0:1], accum_out=nr[:, j, 1:2])
                    s["sc"].inc(i)
                    s["sc"].wait(nc.vector)
                    i = nc.vector.reciprocal(out=nr[:, j, 2:3], in_=nr[:, j, 1:2])
                    s["v"].inc(i)
            pmes = ExitStack()
            pm0 = pmes.enter_context(nc.psum_tensor("ps_m0_%d_%d" % (vi, g0), [128, 2, 512], F32))
            pm1 = pmes.enter_context(nc.psum_tensor("ps_m1_%d_%d" % (vi, g0), [128, 2, 512], F32))
            pm2 = pmes.enter_context(nc.psum_tensor("ps_m2_%d_%d" % (vi, g0), [128, 2, 512], F32))
            tpes = ExitStack()
            tpps = tpes.enter_context(nc.psum_tensor("ps_tp_%d_%d" % (vi, g0), [128, 512], BF16))
            if True:
                msgps = [pm0, pm1, pm2]
                for sc_i in range(nsc):
                    bi = sc_i % 2
                    if sc_i >= 2:
                        s["p"].wait(nc.gpsimd)
                    s["g"].dma(nc.gpsimd.dma_start(
                        out=xt[:, bi, :, :],
                        in_=d["xn_tok"].rearrange("(a p) f -> p a f", p=128)[:, 4 * sc_i:4 * (sc_i + 1), :]))
                    for j, t in enumerate(tiles):
                        if tws[j] <= sc_i * 512:
                            continue
                        s["g"].wait(nc.tensor)
                        for q4 in range(4):
                            i = nc.tensor.transpose(
                                tpps[:, 128 * q4:128 * (q4 + 1)],
                                sa_f[:, 8192 * j + 512 * sc_i + 128 * q4:8192 * j + 512 * sc_i + 128 * (q4 + 1)],
                                iden[:])
                        s["p"].inc(i)
                        s["p"].wait(nc.vector)
                        i = nc.vector.tensor_copy(
                            out=aT.ap().rearrange("p a b -> p (a b)")[:, :],
                            in_=tpps[:, :])
                        s["v"].inc(i)
                        s["v"].wait(nc.tensor)
                        first = (sc_i == 0)
                        last = (sc_i == tws[j] // 512 - 1)
                        for q4 in range(4):
                            nc.tensor.matmul(msgps[j][:, 0, :], aT[:, q4, :],
                                             xt[:, bi, q4, 0:512],
                                             start=(first and q4 == 0), stop=(last and q4 == 3))
                            i = nc.tensor.matmul(msgps[j][:, 1, :], aT[:, q4, :],
                                                 xt[:, bi, q4, 512:1024],
                                                 start=(first and q4 == 0), stop=(last and q4 == 3))
                        s["p"].inc(i)
                tpes.close()
                with nc.psum_tensor("ps_x1_%d_%d" % (vi, g0), [128, 2, 512], F32) as xtp:
                    for j, t in enumerate(tiles):
                        s["q"].wait(nc.sync, at=None)
                        s["q"].dma(nc.sync.dma_start(
                            out=xnt_t[:, :], in_=d["xnrows_tok"][128 * t:128 * (t + 1), :]))
                        s["q"].dma(nc.sync.dma_start(
                            out=xr_t[:, :], in_=d["x_rows_tok"][128 * t:128 * (t + 1), :]))
                        s["q"].wait(nc.vector)
                        s["p"].wait(nc.vector)
                        msgv = msgps[j].ap().rearrange("p a b -> p (a b)")
                        nc.vector.tensor_scalar(out=ep[:, 0, :], in0=msgv[:, :],
                                                scalar1=nr[:, j, 2:3], scalar2=scal[:, 1:2],
                                                op0=ALU.mult, op1=ALU.mult)
                        nc.vector.scalar_tensor_tensor(out=ep[:, 1, :], in0=xnt_t[:, :],
                                                       scalar=scal[:, 0:1], in1=ep[:, 0, :],
                                                       op0=ALU.mult, op1=ALU.add)
                        nc.vector.tensor_tensor(ep[:, 1, :], ep[:, 1, :], gain_rep[:, :], ALU.mult)
                        i = nc.vector.tensor_tensor(ep[:, 1, :], ep[:, 1, :], bias_rep[:, :], ALU.add)
                        s["v"].inc(i)
                        s["v"].wait(nc.scalar)
                        i = nc.scalar.activation(out=ep[:, 2, :], in_=ep[:, 1, :], func=AF.Gelu)
                        s["sc"].inc(i)
                        s["sc"].wait(nc.vector)
                        nc.vector.tensor_scalar(out=ep[:, 2, :], in0=ep[:, 2, :],
                                                scalar1=scal[:, 2:3], scalar2=None, op0=ALU.mult)
                        i = nc.vector.tensor_tensor(ep[:, 0, :], ep[:, 2, :], xr_t[:, :], ALU.add)
                        s["v"].inc(i)
                        s["v"].wait(nc.tensor)
                        s["q"].wait(nc.tensor)
                        for c in range(8):
                            i = nc.tensor.transpose(
                                xtp[:, c // 4, 128 * (c % 4):128 * (c % 4) + 128],
                                ep[:, 0, 128 * c:128 * (c + 1)], iden32[:])
                            if c in (3, 7):
                                s["p"].inc(i)
                        s["p"].wait(nc.vector)
                        x1T_f = x1T.ap().rearrange("p a b -> p (a b)")
                        nc.vector.tensor_copy(out=x1T_f[:, 0:512], in_=xtp[:, 0, :])
                        i = nc.vector.tensor_copy(out=x1T_f[:, 512:1024], in_=xtp[:, 1, :])
                        s["v"].inc(i)
                        s["v"].wait(nc.sync)
                        s["q"].dma(nc.sync.dma_start(
                            out=d["x1_fm"].rearrange("(c p) t -> p c t", p=128)[:, :, 128 * t:128 * (t + 1)],
                            in_=x1T[:, :, :]))
            pmes.close()
            for eng in (nc.sync, nc.tensor, nc.vector, nc.scalar, nc.gpsimd):
                for ss in s.values():
                    ss.wait(eng)
    aes.close()


def emit_la(ctx, layer, t0, q0, nw, flag_base, in_stage, ln_first, z0_stage,
            resid_stage, out_stage):
    nc = ctx.nc
    d = ctx.d
    s = {"q": ctx.qs, "g": ctx.qg, "p": ctx.p, "v": ctx.v, "sc": ctx.sc}
    iden = ctx.b["iden_sb"]
    ones_col = ctx.b["ones_col"]
    ones_row = ctx.b["ones_row"]
    la_mask = ctx.b["la_mask"]
    e8 = ctx.b["e8_sb"]
    e8t = ctx.b["e8t_sb"]
    laflags = ctx.b["laflags"]
    sflag = ctx.b["sflag"]
    nch = (RT - t0) // 512
    NK = RT - t0

    les = ExitStack()
    wqkv = les.enter_context(nc.sbuf_tensor("la_wqkv_%d" % layer, [128, 8, 1536], BF16))
    wo = les.enter_context(nc.sbuf_tensor("la_wo_%d" % layer, [128, 4, 1024], BF16))
    qs_col = les.enter_context(nc.sbuf_tensor("la_qs_%d" % layer, [128, 1], F32))
    ks_col = les.enter_context(nc.sbuf_tensor("la_ks_%d" % layer, [128, 1], F32))
    kh = les.enter_context(nc.sbuf_tensor("la_kh_%d" % layer, [128, 4, NK], BF16))
    qh = les.enter_context(nc.sbuf_tensor("la_qh_%d" % layer, [128, 4, NK - 512], BF16))
    vt = les.enter_context(nc.sbuf_tensor("la_vt_%d" % layer, [128, NK // 128, 512], BF16))
    if True:
        ctx.qg.dma(nc.gpsimd.dma_start(
            out=wqkv[:],
            in_=d["L0_ffn_wqkv"][layer].rearrange("(c p) f -> p c f", p=128)))
        ctx.qg.dma(nc.gpsimd.dma_start(
            out=wo[:],
            in_=d["L0_ffn_wo"][layer].rearrange("(c p) f -> p c f", p=128)))
        s["q"].dma(nc.sync.dma_start(out=qs_col[:], in_=d["qscale_col"][layer]))
        s["q"].dma(nc.sync.dma_start(out=ks_col[:], in_=d["kscale_col"][layer]))
        pa_es = ExitStack()
        zt = pa_es.enter_context(nc.sbuf_tensor("laA_z_%d" % layer, [128, 8, 513], F32))
        qraw = pa_es.enter_context(nc.sbuf_tensor("laA_qr_%d" % layer, [128, 4, 512], F32))
        kraw = pa_es.enter_context(nc.sbuf_tensor("laA_kr_%d" % layer, [128, 4, 512], F32))
        ub = pa_es.enter_context(nc.sbuf_tensor("laA_u_%d" % layer, [128, 512], F32))
        ab = pa_es.enter_context(nc.sbuf_tensor("laA_a_%d" % layer, [128, 512], F32))
        bb = pa_es.enter_context(nc.sbuf_tensor("laA_b_%d" % layer, [128, 512], F32))
        sq = pa_es.enter_context(nc.sbuf_tensor("laA_sq_%d" % layer, [128, 512], F32))
        stat_sb = pa_es.enter_context(nc.sbuf_tensor("laA_st_%d" % layer, [33, 512], F32))
        rep_sb = pa_es.enter_context(nc.sbuf_tensor("laA_rp_%d" % layer, [128, 1024], F32))
        r8 = pa_es.enter_context(nc.sbuf_tensor("laA_r8_%d" % layer, [8, 512], F32))
        rrep = pa_es.enter_context(nc.sbuf_tensor("laA_rr_%d" % layer, [128, 512], F32))
        rot = pa_es.enter_context(nc.sbuf_tensor("laA_rot_%d" % layer, [128, 4, 512], F32))
        ps0 = pa_es.enter_context(nc.psum_tensor("laA_ps_%d" % layer, [128, 512], F32))
        ps1 = pa_es.enter_context(nc.psum_tensor("laA_ps2_%d" % layer, [128, 512], F32))
        ps8 = pa_es.enter_context(nc.psum_tensor("laA_st8_%d" % layer, [8, 512], F32))
        stat_ps = pa_es.enter_context(nc.psum_tensor("laA_stat_%d" % layer, [33, 512], F32))
        rep_ps = pa_es.enter_context(nc.psum_tensor("laA_rep_%d" % layer, [128, 1024], F32))
        mu_sb = pa_es.enter_context(nc.sbuf_tensor("laA_mu_%d" % layer, [1, 512], F32))
        if True:
            lnws = dict(sq=sq, stat=stat_ps, rep=rep_ps, stat_sb=stat_sb, rep_sb=rep_sb, mu_sb=mu_sb)
            ctx.qg.wait(nc.tensor)
            ctx.qg.wait(nc.sync)
            s["q"].wait(nc.tensor)
            s["q"].wait(nc.vector)
            for ci in range(nch):
                start = t0 + 512 * ci
                s["v"].wait(nc.sync)
                s["p"].wait(nc.sync)
                s["sc"].wait(nc.sync)
                ctx.qg.wait(nc.sync)
                s["q"].dma(nc.sync.dma_start(
                    out=zt[:, :, :],
                    in_=in_stage.rearrange("(c p) t -> p c t", p=128)[:, :, start - 1:start + 512]))
                s["q"].wait(nc.vector)
                s["q"].wait(nc.scalar)
                s["q"].wait(nc.tensor)
                if ln_first:
                    emit_ln_fm(ctx, s, [zt[:, c, 0:1] for c in range(8)],
                               [zt[:, c, 0:1] for c in range(8)], 1, lnws)
                    emit_ln_fm(ctx, s, [zt[:, c, 1:513] for c in range(8)],
                               [zt[:, c, 1:513] for c in range(8)], 512, lnws)
                    s["v"].wait(nc.gpsimd)
                    ctx.qg.dma(nc.gpsimd.dma_start(
                        out=z0_stage.rearrange("(c p) t -> p c t", p=128)[:, :, start:start + 512],
                        in_=zt[:, :, 1:513]))
                if start - 1 <= HALO - 1 < start + 512:
                    loc = HALO - start
                    ctx.qg.wait(nc.vector)
                    i = nc.vector.tensor_scalar(out=zt[:, 4:8, loc:loc + 1],
                                                in0=zt[:, 4:8, loc:loc + 1],
                                                scalar1=sflag[:, 0:1], scalar2=None,
                                                op0=ALU.mult)
                    s["v"].inc(i)
                hviews = [zt[:, c, 1:513] for c in range(4)] + [zt[:, c, 0:512] for c in range(4, 8)]
                s["v"].wait(nc.vector)
                if ci == 0:
                    hbf_ctx = nc.sbuf_tensor("laA_hbf_%d" % layer, [128, 8, 512], BF16)
                    hbf = hbf_ctx.__enter__()
                for c in range(8):
                    i = nc.vector.tensor_copy(out=hbf[:, c, :], in_=hviews[c])
                s["v"].inc(i)
                s["v"].wait(nc.tensor)
                for fc in range(8):
                    dst = qraw if fc < 4 else kraw
                    ps = ps0 if fc % 2 == 0 else ps1
                    for c in range(8):
                        i = nc.tensor.matmul(ps[:, :], wqkv[:, c, 128 * fc:128 * (fc + 1)],
                                             hbf[:, c, :], start=(c == 0), stop=(c == 7))
                    s["p"].inc(i)
                    s["p"].wait(nc.scalar)
                    i = nc.scalar.copy(out=dst[:, fc % 4, :], in_=ps[:, :])
                    s["sc"].inc(i)
                    s["sc"].wait(nc.tensor)
                for t4 in range(4):
                    ps = ps0 if t4 % 2 == 0 else ps1
                    for c in range(8):
                        i = nc.tensor.matmul(ps[:, :], hbf[:, c, 128 * t4:128 * (t4 + 1)],
                                             wqkv[:, c, 1024:1536], start=(c == 0), stop=(c == 7))
                    s["p"].inc(i)
                    s["p"].wait(nc.vector)
                    i = nc.vector.tensor_copy(out=vt[:, 4 * ci + t4, :], in_=ps[:, :])
                    s["v"].inc(i)
                    s["v"].wait(nc.tensor)
                for (rslot, rname) in ((0, "rot_cq"), (1, "rot_sq"), (2, "rot_ck"), (3, "rot_sk")):
                    s["q"].dma(nc.sync.dma_start(
                        out=rot[:, rslot, :],
                        in_=d[rname][:, start - B1_T0:start - B1_T0 + 512]))
                if DEBUG_LA and layer == 0 and ci == 4:
                    s["sc"].wait(nc.gpsimd)
                    ctx.qg.dma(nc.gpsimd.dma_start(out=d["dbg_kraw"][:], in_=kraw[:, :, :]))
                    ctx.qg.dma(nc.gpsimd.dma_start(out=d["dbg_hbf"][:], in_=hbf[:, :, :]))
                    ctx.qg.wait(nc.vector)
                    ctx.qg.wait(nc.scalar)
                for (raw, outfm, sccol, cslot, sslot) in (
                        (qraw, qh, qs_col, 0, 1), (kraw, kh, ks_col, 2, 3)):
                    if outfm is qh and ci == 0:
                        continue
                    s["v"].wait(nc.scalar)
                    s["p"].wait(nc.scalar)
                    for c in range(4):
                        i = nc.scalar.activation(out=sq[:, :], in_=raw[:, c, :], func=AF.Square)
                        s["sc"].inc(i)
                        s["sc"].wait(nc.tensor)
                        i = nc.tensor.matmul(ps8[:, :], e8[:, c, :], sq[:, :],
                                             start=(c == 0), stop=(c == 3))
                        s["p"].inc(i)
                        s["p"].wait(nc.scalar)
                    s["v"].wait(nc.scalar)
                    i = nc.scalar.copy(out=r8[:, :], in_=ps8[:, :])
                    s["sc"].inc(i)
                    s["sc"].wait(nc.vector)
                    i = nc.vector.tensor_scalar_max(r8[:, :], r8[:, :], 1e-24)
                    s["v"].inc(i)
                    s["v"].wait(nc.scalar)
                    nc.scalar.activation(out=r8[:, :], in_=r8[:, :], func=AF.Ln)
                    i = nc.scalar.activation(out=r8[:, :], in_=r8[:, :], func=AF.Exp, scale=-0.5)
                    s["sc"].inc(i)
                    s["sc"].wait(nc.vector)
                    i = nc.vector.tensor_copy(out=r8[:, :], in_=r8[:, :])
                    s["v"].inc(i)
                    if DEBUG_LA and layer == 0 and ci == 4 and raw is kraw:
                        s["v"].wait(nc.gpsimd)
                        ctx.qg.dma(nc.gpsimd.dma_start(out=d["dbg_r8"][:], in_=r8[:, :]))
                        ctx.qg.wait(nc.tensor)
                        ctx.qg.wait(nc.vector)
                    s["v"].wait(nc.tensor)
                    for c in range(4):
                        i = nc.tensor.matmul(rep_ps[:, 0:512], e8t[:, c, :], r8[:, :],
                                             start=True, stop=True)
                        s["p"].inc(i)
                        s["p"].wait(nc.scalar)
                        i = nc.scalar.copy(out=rrep[:, :], in_=rep_ps[:, 0:512])
                        s["sc"].inc(i)
                        s["sc"].wait(nc.vector)
                        s["q"].wait(nc.vector)
                        nc.vector.scalar_tensor_tensor(out=ub[:, :], in0=raw[:, c, :],
                                                       scalar=sccol[:, 0:1], in1=rrep[:, :],
                                                       op0=ALU.mult, op1=ALU.mult)
                        nc.vector.tensor_tensor(ab[:, :], ub[:, :], rot[:, cslot, :], ALU.mult)
                        for po in (0, 32, 64, 96):
                            src = po + 32 if po % 64 == 0 else po - 32
                            nc.vector.tensor_tensor(bb[po:po + 32, :], ub[src:src + 32, :],
                                                    rot[src:src + 32, sslot, :], ALU.mult)
                        oc = 512 * (ci - 1) if outfm is qh else 512 * ci
                        i = nc.vector.tensor_tensor(outfm[:, c, oc:oc + 512],
                                                    ab[:, :], bb[:, :], ALU.add)
                        s["v"].inc(i)
                        s["v"].wait(nc.tensor)
            if nch > 0:
                hbf_ctx.__exit__(None, None, None)
            if DEBUG_LA and layer == 0:
                ctx.qg.dma(nc.gpsimd.dma_start(out=d["dbg_qh"][:], in_=qh[:, :, 0:RT - CH - 512]))
                ctx.qg.dma(nc.gpsimd.dma_start(out=d["dbg_kh"][:], in_=kh[:, :, 0:RT - CH]))
                ctx.qg.dma(nc.gpsimd.dma_start(out=d["dbg_vt"][:], in_=vt[:, 0:(RT - CH) // 128, :]))
                ctx.qg.wait(nc.sync)
        ctx.barrier()
        pa_es.close()
        pb_es = ExitStack()
        sT = pb_es.enter_context(nc.sbuf_tensor("laB_sT_%d" % layer, [128, 512], F32))
        eT = pb_es.enter_context(nc.sbuf_tensor("laB_eT_%d" % layer, [128, 8, 512], BF16))
        ofm = pb_es.enter_context(nc.sbuf_tensor("laB_of_%d" % layer, [128, 4, 512], F32))
        obf = pb_es.enter_context(nc.sbuf_tensor("laB_ob_%d" % layer, [128, 4, 512], BF16))
        mrep = pb_es.enter_context(nc.sbuf_tensor("laB_mr_%d" % layer, [128, 512], F32))
        srep = pb_es.enter_context(nc.sbuf_tensor("laB_sr_%d" % layer, [128, 512], F32))
        rq = pb_es.enter_context(nc.sbuf_tensor("laB_rq_%d" % layer, [128, 2, 512], F32))
        kst = pb_es.enter_context(nc.sbuf_tensor("laB_k2_%d" % layer, [128, 4], F32))
        sums = pb_es.enter_context(nc.sbuf_tensor("laB_sm_%d" % layer, [1, 4096], F32))
        brow = pb_es.enter_context(nc.sbuf_tensor("laB_br_%d" % layer, [1, 512], F32))
        resid = pb_es.enter_context(nc.sbuf_tensor("laB_zi_%d" % layer, [128, 8, 512], F32))
        sps0 = pb_es.enter_context(nc.psum_tensor("laB_s1_%d" % layer, [128, 512], F32))
        sps1 = pb_es.enter_context(nc.psum_tensor("laB_s2_%d" % layer, [128, 512], F32))
        oTps = pb_es.enter_context(nc.psum_tensor("laB_oT_%d" % layer, [64, 512], F32))
        swps = pb_es.enter_context(nc.psum_tensor("laB_sw_%d" % layer, [1, 512], F32))
        bdps = pb_es.enter_context(nc.psum_tensor("laB_bd_%d" % layer, [1, 512], F32))
        mrps = pb_es.enter_context(nc.psum_tensor("laB_mrp_%d" % layer, [128, 512], F32))
        if True:
            for w in range(nw):
                q0w = q0 + WIN * w
                qcol = q0w - t0
                kb = q0w - WIN
                kcol = kb - t0
                s["v"].wait(nc.sync)
                ctx.qg.wait(nc.sync)
                s["q"].dma(nc.sync.dma_start(
                    out=resid[:, :, :],
                    in_=resid_stage.rearrange("(c p) t -> p c t", p=128)[:, :, q0w:q0w + 512]))
                for h in range(HEADS):
                    c = h // 2
                    po = 64 * (h % 2)
                    ksl = kh[po:po + 64, c, kcol:kcol + 1024]
                    qsl = qh[po:po + 64, c, qcol - 512:qcol]
                    s["v"].wait(nc.vector)
                    s["p"].wait(nc.vector)
                    nc.vector.reduce_max(kst[po:po + 64, 0:1], ksl, axis=AX.X)
                    i = nc.vector.tensor_reduce(kst[po:po + 64, 1:2], ksl, axis=AX.X,
                                                op=ALU.min, negate=True)
                    s["v"].inc(i)
                    s["v"].wait(nc.scalar)
                    nc.scalar.activation(out=rq[po:po + 64, 0, :], in_=qsl, func=AF.Relu)
                    i = nc.scalar.activation(out=rq[po:po + 64, 1, :], in_=qsl,
                                             func=AF.Relu, scale=-1.0)
                    s["sc"].inc(i)
                    s["sc"].wait(nc.tensor)
                    s["v"].wait(nc.tensor)
                    nc.tensor.matmul(bdps[0:1, :], kst[po:po + 64, 0:1], rq[po:po + 64, 0, :],
                                     start=True, stop=False)
                    i = nc.tensor.matmul(bdps[0:1, :], kst[po:po + 64, 1:2], rq[po:po + 64, 1, :],
                                         start=False, stop=True)
                    s["p"].inc(i)
                    s["p"].wait(nc.scalar)
                    i = nc.scalar.activation(out=brow[0:1, :], in_=bdps[0:1, :], func=AF.Copy, scale=QK_SCALE)
                    s["sc"].inc(i)
                    s["sc"].wait(nc.tensor)
                    i = nc.tensor.matmul(mrps[:, :], ones_row[0:1, :], brow[0:1, :],
                                         start=True, stop=True)
                    s["p"].inc(i)
                    s["p"].wait(nc.scalar)
                    i = nc.scalar.activation(out=mrep[:, :], in_=mrps[:, :], func=AF.Copy, bias=-45.0)
                    s["sc"].inc(i)
                    for kc in range(8):
                        ps = sps0 if kc % 2 == 0 else sps1
                        s["sc"].wait(nc.tensor)
                        s["v"].wait(nc.tensor)
                        i = nc.tensor.matmul(ps[:, :], kh[po:po + 64, c, kcol + 128 * kc:kcol + 128 * (kc + 1)],
                                             qh[po:po + 64, c, qcol - 512:qcol],
                                             start=True, stop=True)
                        s["p"].inc(i)
                        s["p"].wait(nc.vector)
                        s["sc"].wait(nc.vector)
                        nc.vector.scalar_tensor_tensor(out=sT[:, :], in0=ps[:, :],
                                                       scalar=QK_SCALE, in1=mrep[:, :],
                                                       op0=ALU.mult, op1=ALU.subtract)
                        nc.vector.tensor_tensor(sT[:, :], sT[:, :], la_mask[:, kc, :], ALU.add)
                        fi = flag_base + 8 * w + kc
                        i = nc.vector.tensor_scalar(out=sT[:, :], in0=sT[:, :],
                                                    scalar1=laflags[:, fi:fi + 1], scalar2=None,
                                                    op0=ALU.add)
                        s["v"].inc(i)
                        s["v"].wait(nc.scalar)
                        i = nc.scalar.activation(out=eT[:, kc, :], in_=sT[:, :], func=AF.Exp)
                        s["sc"].inc(i)
                    s["sc"].wait(nc.tensor)
                    for kc in range(8):
                        vidx = (kb + 128 * kc - t0) // 128
                        nc.tensor.matmul(oTps[0:64, :],
                                         vt[:, vidx, 64 * h:64 * h + 64],
                                         eT[:, kc, :], start=(kc == 0), stop=(kc == 7))
                    for kc in range(8):
                        i = nc.tensor.matmul(swps[0:1, :], ones_col[:, 0:1],
                                             eT[:, kc, :], start=(kc == 0), stop=(kc == 7))
                    s["p"].inc(i)
                    s["p"].wait(nc.scalar)
                    s["p"].wait(nc.vector)
                    i = nc.scalar.copy(out=ofm[po:po + 64, c, :], in_=oTps[0:64, :])
                    s["sc"].inc(i)
                    i = nc.vector.tensor_copy(out=sums[0:1, 512 * h:512 * (h + 1)], in_=swps[0:1, :])
                    s["v"].inc(i)
                nc.vector.tensor_scalar(out=sums[0:1, :], in0=sums[0:1, :],
                                        scalar1=1e-30, scalar2=None, op0=ALU.add)
                i = nc.vector.reciprocal(out=sums[0:1, :], in_=sums[0:1, :])
                s["v"].inc(i)
                s["v"].wait(nc.tensor)
                s["sc"].wait(nc.tensor)
                for c in range(4):
                    nc.tensor.matmul(mrps[0:64, :], ones_row[0:1, 0:64],
                                     sums[0:1, 512 * 2 * c:512 * (2 * c + 1)],
                                     start=True, stop=True, skip_group_check=True)
                    i = nc.tensor.matmul(mrps[64:128, :], ones_row[0:1, 0:64],
                                         sums[0:1, 512 * (2 * c + 1):512 * (2 * c + 2)],
                                         start=True, stop=True, skip_group_check=True)
                    s["p"].inc(i)
                    s["p"].wait(nc.scalar)
                    i = nc.scalar.copy(out=srep[:, :], in_=mrps[:, :])
                    s["sc"].inc(i)
                    s["sc"].wait(nc.vector)
                    i = nc.vector.tensor_tensor(obf[:, c, :], ofm[:, c, :], srep[:, :], ALU.mult)
                    s["v"].inc(i)
                    s["v"].wait(nc.tensor)
                s["q"].wait(nc.vector)
                for dc in range(8):
                    ps = sps0 if dc % 2 == 0 else sps1
                    s["v"].wait(nc.tensor)
                    for ic in range(4):
                        i = nc.tensor.matmul(ps[:, :], wo[:, ic, 128 * dc:128 * (dc + 1)],
                                             obf[:, ic, :], start=(ic == 0), stop=(ic == 3))
                    s["p"].inc(i)
                    s["p"].wait(nc.vector)
                    i = nc.vector.scalar_tensor_tensor(out=resid[:, dc, :], in0=ps[:, :],
                                                       scalar=1.0, in1=resid[:, dc, :],
                                                       op0=ALU.bypass, op1=ALU.add)
                    s["v"].inc(i)
                s["v"].wait(nc.gpsimd)
                ctx.qg.dma(nc.gpsimd.dma_start(
                    out=out_stage.rearrange("(c p) t -> p c t", p=128)[:, :, q0w:q0w + 512],
                    in_=resid[:, :, :]))
                ctx.qg.wait(nc.vector)
        ctx.barrier()
        pb_es.close()
    les.close()


def emit_mlp(ctx, layer, t0, in_stage, out_stage, final):
    nc = ctx.nc
    d = ctx.d
    s = {"q": ctx.qs, "g": ctx.qg, "p": ctx.p, "v": ctx.v, "sc": ctx.sc}
    sflag = ctx.b["sflag"]
    nch = (RT - t0 + 511) // 512
    mes = ExitStack()
    w1 = mes.enter_context(nc.sbuf_tensor("ml_w1_%d" % layer, [128, 8, 5632], BF16))
    w2r = mes.enter_context(nc.sbuf_tensor("ml_w2r_%d" % layer, [128, 2, 11, 512], BF16))
    zin = mes.enter_context(nc.sbuf_tensor("ml_zi_%d" % layer, [128, 8, 513], F32))
    lnh = mes.enter_context(nc.sbuf_tensor("ml_ln_%d" % layer, [128, 8, 512], BF16))
    hh = mes.enter_context(nc.sbuf_tensor("ml_hh_%d" % layer, [128, 22, 512], BF16))
    gl = mes.enter_context(nc.sbuf_tensor("ml_gl_%d" % layer, [128, 2, 512], BF16))
    x1t = mes.enter_context(nc.sbuf_tensor("ml_x1_%d" % layer, [128, 8, 512], F32))
    sq = mes.enter_context(nc.sbuf_tensor("ml_sq_%d" % layer, [128, 512], F32))
    stat_sb = mes.enter_context(nc.sbuf_tensor("ml_st_%d" % layer, [33, 512], F32))
    rep_sb = mes.enter_context(nc.sbuf_tensor("ml_rp_%d" % layer, [128, 1024], F32))
    lng = mes.enter_context(nc.sbuf_tensor("ml_g_%d" % layer, [128, 8], F32))
    lnb = mes.enter_context(nc.sbuf_tensor("ml_b_%d" % layer, [128, 8], F32))
    b1c = mes.enter_context(nc.sbuf_tensor("ml_b1_%d" % layer, [128, 44], F32))
    b2c = mes.enter_context(nc.sbuf_tensor("ml_b2_%d" % layer, [128, 8], F32))
    stat_ps = mes.enter_context(nc.psum_tensor("ml_stat_%d" % layer, [33, 512], F32))
    rep_ps = mes.enter_context(nc.psum_tensor("ml_rep_%d" % layer, [128, 1024], F32))
    queues = [ctx.qs, ctx.qg]
    engs = {id(ctx.qs): nc.sync, id(ctx.qg): nc.gpsimd}
    mm_done = {}

    def issue_group(kg):
        g, half = (kg % 4) // 2, kg % 2
        q = queues[kg % 2]
        eng = engs[id(q)]
        if kg - 2 in mm_done:
            s["p"].wait(eng, at=mm_done[kg - 2])
        q.dma(eng.dma_start(
            out=w2r[:, kg % 2, :, :],
            in_=d["w2b"][layer, g, 11 * half:11 * half + 11].rearrange("i p f -> p i f")))

    if True:
        ctx.qg.dma(nc.gpsimd.dma_start(
            out=w1[:], in_=d["w1p"][layer].rearrange("(c p) f -> p c f", p=128)))
        s["q"].dma(nc.sync.dma_start(out=lng[:], in_=d["lng_col"][layer]))
        s["q"].dma(nc.sync.dma_start(out=lnb[:], in_=d["lnb_col"][layer]))
        s["q"].dma(nc.sync.dma_start(out=b1c[:], in_=d["b1p_col"][layer]))
        s["q"].dma(nc.sync.dma_start(out=b2c[:], in_=d["b2_col"][layer]))
        mu_sb = mes.enter_context(nc.sbuf_tensor("ml_mu_%d" % layer, [1, 512], F32))
        lnws = dict(sq=sq, stat=stat_ps, rep=rep_ps, stat_sb=stat_sb, rep_sb=rep_sb, mu_sb=mu_sb)
        for ci in range(nch):
            start = t0 + 512 * ci
            tc = min(512, RT - start)
            K = 4 * ci
            s["v"].wait(nc.sync)
            s["q"].dma(nc.sync.dma_start(
                out=zin[:, :, :tc + 1],
                in_=in_stage.rearrange("(c p) t -> p c t", p=128)[:, :, start - 1:start + tc]))
            if ci == 0:
                ctx.qg.wait(nc.tensor)
            issue_group(K)
            issue_group(K + 1)
            s["q"].wait(nc.vector)
            s["q"].wait(nc.scalar)
            s["q"].wait(nc.tensor)
            if start - 1 <= HALO - 1 < start + tc:
                loc = HALO - start
                i = nc.vector.tensor_scalar(out=zin[:, 4:8, loc:loc + 1],
                                            in0=zin[:, 4:8, loc:loc + 1],
                                            scalar1=sflag[:, 0:1], scalar2=None,
                                            op0=ALU.mult)
                s["v"].inc(i)
                s["v"].wait(nc.tensor)
                s["v"].wait(nc.scalar)
            hviews = [zin[:, c, 1:1 + tc] for c in range(4)] + [zin[:, c, 0:tc] for c in range(4, 8)]
            emit_ln_fm(ctx, s, hviews, [lnh[:, c, :tc] for c in range(8)], tc, lnws,
                       gain_cols=[lng[:, c:c + 1] for c in range(8)],
                       beta_cols=[lnb[:, c:c + 1] for c in range(8)])
            s["v"].wait(nc.tensor)
            vcnt = {}
            with (
                nc.psum_tensor("ml_pa_%d_%d" % (layer, ci), [128, 2, 512], F32) as psA,
                nc.psum_tensor("ml_pb_%d_%d" % (layer, ci), [128, 2, 512], F32) as psB,
            ):
                for ih in range(22):
                    sl = ih % 2
                    if ih >= 2:
                        s["v"].wait(nc.tensor, at=vcnt[ih - 2])
                    for c in range(8):
                        nc.tensor.matmul(psA[:, sl, :tc], w1[:, c, 128 * ih:128 * (ih + 1)],
                                         lnh[:, c, :tc], start=(c == 0), stop=(c == 7))
                    for c in range(8):
                        i = nc.tensor.matmul(psB[:, sl, :tc], w1[:, c, 2816 + 128 * ih:2816 + 128 * (ih + 1)],
                                             lnh[:, c, :tc], start=(c == 0), stop=(c == 7))
                    s["p"].inc(i)
                    pc = s["p"].n
                    s["p"].wait(nc.scalar, at=pc)
                    if ih >= 2:
                        s["v"].wait(nc.scalar, at=vcnt[ih - 2])
                    i = nc.scalar.activation(out=gl[:, sl, :tc], in_=psB[:, sl, :tc], func=AF.Gelu,
                                             bias=b1c[:, 22 + ih:23 + ih])
                    s["sc"].inc(i)
                    s["sc"].wait(nc.vector, at=s["sc"].n)
                    s["p"].wait(nc.vector, at=pc)
                    i = nc.vector.scalar_tensor_tensor(out=hh[:, ih, :tc], in0=psA[:, sl, :tc],
                                                       scalar=b1c[:, ih:ih + 1], in1=gl[:, sl, :tc],
                                                       op0=ALU.add, op1=ALU.mult)
                    s["v"].inc(i)
                    vcnt[ih] = s["v"].n
                s["v"].wait(nc.tensor)
            if final:
                s["q"].dma(nc.sync.dma_start(
                    out=x1t[:, :, :tc],
                    in_=d["x1_fm"].rearrange("(c p) t -> p c t", p=128)[:, :, start:start + tc]))
                s["q"].wait(nc.vector)
            with nc.psum_tensor("ml_pz_%d_%d" % (layer, ci), [128, 4, 512], F32) as psZ4:
                for k4 in range(4):
                    kg = K + k4
                    g, half = k4 // 2, k4 % 2
                    if k4 == 2:
                        s["v"].wait(nc.tensor)
                    queues[kg % 2].wait(nc.tensor)
                    for i_g in range(11):
                        ic = 11 * half + i_g
                        for dc4 in range(4):
                            i = nc.tensor.matmul(
                                psZ4[:, dc4, :tc], w2r[:, kg % 2, i_g, 128 * dc4:128 * (dc4 + 1)],
                                hh[:, ic, :tc], start=(ic == 0), stop=(ic == 21))
                    s["p"].inc(i)
                    mm_done[kg] = s["p"].n
                    if k4 < 2:
                        issue_group(kg + 2)
                    if half == 1:
                        s["p"].wait(nc.vector)
                        for dc4 in range(4):
                            dc = 4 * g + dc4
                            i = nc.vector.scalar_tensor_tensor(
                                out=zin[:, dc, 1:1 + tc], in0=psZ4[:, dc4, :tc],
                                scalar=b2c[:, dc:dc + 1], in1=zin[:, dc, 1:1 + tc],
                                op0=ALU.add, op1=ALU.add)
                            if final:
                                i = nc.vector.tensor_tensor(zin[:, dc, 1:1 + tc],
                                                            zin[:, dc, 1:1 + tc],
                                                            x1t[:, dc, :tc], ALU.add)
                        s["v"].inc(i)
            s["v"].wait(nc.gpsimd)
            if final:
                ctx.qg.dma(nc.gpsimd.dma_start(
                    out=d["out_fm"].rearrange("(c p) t -> p c t", p=128)[:, :, start - HALO:start - HALO + tc],
                    in_=zin[:, :, 1:1 + tc]))
            else:
                ctx.qg.dma(nc.gpsimd.dma_start(
                    out=out_stage.rearrange("(c p) t -> p c t", p=128)[:, :, start:start + tc],
                    in_=zin[:, :, 1:1 + tc]))
            ctx.qg.wait(nc.vector)
            ctx.qg.wait(nc.sync)
    ctx.barrier()
    mes.close()


_IN_SPECS = {
    "x_tok": ([T, D], F32), "x_rows_tok": ([RT, D], F32), "x_rows_fm": ([D, RT], F32),
    "sim_slab": ([128, None], BF16),
    "rot_cq": ([128, RT - 128], F32), "rot_sq": ([128, RT - 128], F32),
    "rot_ck": ([128, RT - 128], F32), "rot_sk": ([128, RT - 128], F32),
    "la_flags": ([128, 96], F32), "shift_flag": ([128, 1], F32),
    "la_mask": ([128, 8, 512], BF16),
    "iden": ([128, 128], BF16), "e8": ([128, 4, 8], F32), "e8t": ([8, 4, 128], F32),
    "ones_col": ([128, 1], BF16), "ones_row": ([1, 128], F32),
    "L0_attn_W_q": ([32, D], F32), "L0_attn_W_k": ([32, D], F32),
    "L0_attn_gain": ([1, D], F32), "L0_attn_bias": ([1, D], F32),
    "log_mix": ([1, 1], F32), "log_scale": ([1, 1], F32),
    "L0_ffn_wqkv": ([2, D, 1536], F32), "L0_ffn_wo": ([2, 512, D], F32),
    "qscale_col": ([2, 128, 1], F32), "kscale_col": ([2, 128, 1], F32),
    "lng_col": ([2, 128, 8], F32), "lnb_col": ([2, 128, 8], F32),
    "w1p": ([2, D, 5632], F32), "b1p_col": ([2, 128, 44], F32),
    "w2h": ([2, 2, 22, 128, 512], F32), "b2_col": ([2, 128, 8], F32),
    "variant": ([1, 1], U32),
}
_SLAB_W = max(len(sim_slab_order(0)), len(sim_slab_order(1))) * 512


def build_program():
    nc0 = bass.Bass("TRN2", target_bir_lowering=False)
    nc = NCWrap(nc0)
    stack = ExitStack()
    ctx = Ctx(nc, stack)
    d = ctx.d
    for name, (shape, dt) in _IN_SPECS.items():
        shp = [s if s is not None else _SLAB_W for s in shape]
        d[name] = nc.dram_tensor(name, shp, dt, kind="ExternalInput").ap()
    for name, shape, dt in (
            ("xn_tok", [T, D], BF16), ("xn_fm", [D, T], BF16),
            ("xnrows", [D, RT], BF16), ("xnrows_tok", [RT, D], BF16),
            ("x1_fm", [D, RT], F32), ("z0_fm", [D, RT], F32),
            ("z1_fm", [D, RT], F32), ("z2_fm", [D, RT], F32),
            ("z3_fm", [D, RT], F32), ("w2b", [2, 2, 22, 128, 512], BF16)):
        d[name] = nc.dram_tensor(name, shape, dt, kind="Internal").ap()
    d["out_fm"] = nc.dram_tensor("out_fm", [D, CH], F32, kind="ExternalOutput").ap()
    if DEBUG_LA:
        d["dbg_kraw"] = nc.dram_tensor("dbg_kraw", [128, 4, 512], F32, kind="Internal").ap()
        d["dbg_hbf"] = nc.dram_tensor("dbg_hbf", [128, 8, 512], BF16, kind="Internal").ap()
        d["dbg_r8"] = nc.dram_tensor("dbg_r8", [8, 512], F32, kind="Internal").ap()
        d["dbg_qh"] = nc.dram_tensor("dbg_qh", [128, 4, RT - CH - 512], BF16, kind="Internal").ap()
        d["dbg_kh"] = nc.dram_tensor("dbg_kh", [128, 4, RT - CH], BF16, kind="Internal").ap()
        d["dbg_vt"] = nc.dram_tensor("dbg_vt", [128, (RT - CH) // 128, 512], BF16, kind="Internal").ap()

    with stack:
        ctx.attn_es = ExitStack()
        vf = ctx.sb("vflag_sb", [1, 1], U32)
        ctx.qs.dma(nc.sync.dma_start(out=vf[:], in_=d["variant"][:]))

        emit_prepass(ctx)
        if PHASES < 1.5:
            return nc0

        engines = [nc.sync, nc.tensor, nc.vector, nc.scalar, nc.gpsimd]
        regs = nc.alloc_registers("vreg")
        for eng in engines:
            ctx.qs.wait(eng)
            eng.reg_load(regs[eng.engine], vf[0:1, 0:1])
        iss = {"q": nc.sync, "g": nc.gpsimd}
        sA = {k: Sem(stack, nc, "a0_" + k, issuer=iss.get(k)) for k in ("q", "g", "p", "v", "sc")}
        sB = {k: Sem(stack, nc, "a1_" + k, issuer=iss.get(k)) for k in ("q", "g", "p", "v", "sc")}
        with nc.If_eq(regs, 0):
            emit_attn(ctx, 0, sA)
        with nc.Else():
            emit_attn(ctx, 1, sB)
        ctx.attn_es.close()
        if PHASES < 2.5:
            return nc0

        emit_la(ctx, 0, B1_T0, LA1_Q0, LA1_NW, 0, d["x1_fm"], True,
                d["z0_fm"], d["z0_fm"], d["z1_fm"])
        if PHASES < 3.5:
            return nc0
        emit_mlp(ctx, 0, B2_T0, d["z1_fm"], d["z2_fm"], final=False)
        if PHASES < 4.5:
            return nc0
        emit_la(ctx, 1, B3_T0, LA2_Q0, LA2_NW, 56, d["z2_fm"], False,
                None, d["z2_fm"], d["z3_fm"])
        if PHASES < 5.5:
            return nc0
        emit_mlp(ctx, 1, B4_T0, d["z3_fm"], None, final=True)
    return nc0


_NC_CACHE = None
DEBUG_LA = False
PHASES = 6


def make_in_maps(inputs, cids=range(8)):
    in_maps = []
    for cid in cids:
        m = _host_core_inputs(inputs, cid)
        sl = m["sim_slab"]
        if sl.shape[1] < _SLAB_W:
            m["sim_slab"] = np.concatenate(
                [sl, np.zeros((128, _SLAB_W - sl.shape[1]), dtype=BF)], axis=1)
        w1 = m.pop("L0_ffn_w1")
        w1p = np.zeros((2, D, 5632), dtype=np.float32)
        w1p[:, :, 0:HID] = w1[:, :, 0:HID]
        w1p[:, :, 2816:2816 + HID] = w1[:, :, HID:2 * HID]
        m["w1p"] = w1p
        w2 = m.pop("L0_ffn_w2")
        w2p = np.zeros((2, 2816, D), dtype=np.float32)
        w2p[:, 0:HID, :] = w2
        m["w2h"] = np.ascontiguousarray(
            w2p.reshape(2, 22, 128, 2, 512).transpose(0, 3, 1, 2, 4))
        b1 = m.pop("L0_ffn_b1")
        b1p = np.zeros((2, 5632), dtype=np.float32)
        b1p[:, 0:HID] = b1[:, 0:HID]
        b1p[:, 2816:2816 + HID] = b1[:, HID:2 * HID]
        m["b1p_col"] = np.ascontiguousarray(
            b1p.reshape(2, 44, 128).transpose(0, 2, 1))
        m["b2_col"] = np.ascontiguousarray(
            m.pop("L0_ffn_b2").reshape(2, 8, 128).transpose(0, 2, 1))
        m["lng_col"] = np.ascontiguousarray(
            m.pop("L0_ffn_ln_g").reshape(2, 8, 128).transpose(0, 2, 1))
        m["lnb_col"] = np.ascontiguousarray(
            m.pop("L0_ffn_ln_b").reshape(2, 8, 128).transpose(0, 2, 1))
        m["qscale_col"] = np.ascontiguousarray(
            np.tile(m.pop("L0_ffn_q_scale"), (1, 2))[:, :, None])
        m["kscale_col"] = np.ascontiguousarray(
            np.tile(m.pop("L0_ffn_k_scale"), (1, 2))[:, :, None])
        in_maps.append(m)
    return in_maps


def kernel(**inputs):
    global _NC_CACHE
    in_maps = make_in_maps(inputs)
    if _NC_CACHE is None:
        _NC_CACHE = build_program()
    res = run_bass_kernel_spmd(_NC_CACHE, in_maps, core_ids=list(range(8)))
    out = np.empty((B, T, D), dtype=np.float32)
    for cid in range(8):
        b, v = cid // 4, cid % 4
        out[b, v * CH:(v + 1) * CH, :] = np.asarray(res.results[cid]["out_fm"]).T
    return out


# version 07
# speedup vs baseline: 2.2436x; 1.4773x over previous
import math
from contextlib import ExitStack

import numpy as np
import ml_dtypes

import concourse.bass as bass
import concourse.mybir as mybir
from concourse.bass_utils import run_bass_kernel_spmd

F32 = mybir.dt.float32
BF16 = mybir.dt.bfloat16
U32 = mybir.dt.uint32
AF = mybir.ActivationFunctionType
ALU = mybir.AluOpType
AX = mybir.AxisListType
BF = ml_dtypes.bfloat16

B, T, D = 2, 8192, 1024
CH = 2048
HALO = 2176
RT = CH + HALO
NT = RT // 128
G = 3
NEG = -3.0e38
NEGH = -1.5e38
C_OFF = 4096.0
HEADS, DH, WIN = 8, 64, 512
INNER = 512
HID = 2730
QK_SCALE = 8.0

B1_T0 = 128
LA1_Q0 = 640
B2_T0 = 1024
B3_T0 = 1152
LA2_Q0 = 1664
B4_T0 = HALO
LA1_NW = (RT - LA1_Q0) // WIN
LA2_NW = (RT - LA2_Q0) // WIN


def variant_widths(c0):
    lo = c0 - HALO
    ws = []
    for i in range(NT):
        row_end = lo + 128 * (i + 1)
        w = min(T, max(512, ((max(row_end, 1) + 511) // 512) * 512))
        ws.append(w)
    rel = (HALO - c0) // 128
    if 0 <= rel < NT:
        ws[rel] = T
    return ws


WIDTHS = [variant_widths(v * CH) for v in range(4)]
SCORE_SLAB_TILES = []
for _v in range(4):
    _rel = (HALO - _v * CH) // 128
    SCORE_SLAB_TILES.append({_rel} if 0 <= _rel < NT else set())


def attn_groups(vi):
    ws = WIDTHS[vi]
    out = []
    for g0 in range(0, NT, G):
        tiles = list(range(g0, min(g0 + G, NT)))
        out.append((tiles, max(ws[t] for t in tiles), [ws[t] for t in tiles]))
    return out


def sim_slab_order(vi):
    order = []
    for tiles, wmax, ws in attn_groups(vi):
        for sc in range(wmax // 512):
            for j, t in enumerate(tiles):
                if ws[j] > sc * 512:
                    order.append((t, sc))
    return order


_ROT = None


def _rotary_tables():
    half = np.arange(0, DH, 2, dtype=np.float64)
    inv_freq = 1.0 / (10000.0 ** (half / DH))
    t = np.arange(T, dtype=np.float64)
    freqs = t[:, None] * inv_freq[None, :]
    base = (half + 0.4 * DH) / (1.4 * DH)
    power = (t - T // 2) / float(WIN // 2)
    scale = base[None, :] ** power[:, None]
    cos, sin = np.cos(freqs), np.sin(freqs)

    def mk(sc):
        Cfull = np.empty((64, T))
        Sfull = np.empty((64, T))
        for j in range(64):
            Cfull[j] = cos[:, j % 32] * sc[:, j % 32]
            if j < 32:
                Sfull[j] = sin[:, j] * sc[:, j]
            else:
                Sfull[j] = -sin[:, j - 32] * sc[:, j - 32]
        return (np.tile(Cfull, (2, 1)).astype(np.float32),
                np.tile(Sfull, (2, 1)).astype(np.float32))

    Cq, Sq = mk(scale)
    Ck, Sk = mk(1.0 / scale)
    return Cq, Sq, Ck, Sk


def _host_core_inputs(inputs, cid):
    global _ROT
    if _ROT is None:
        _ROT = _rotary_tables()
    Cq, Sq, Ck, Sk = _ROT
    b, v = cid // 4, cid % 4
    vi = v
    c0 = v * CH
    lo = c0 - HALO
    x = np.asarray(inputs["L0_attn_x"][b], dtype=np.float32)

    rows = np.zeros((RT, D), dtype=np.float32)
    src0 = max(0, lo)
    rows[src0 - lo:] = x[src0: c0 + CH]
    m = {
        "x_tok": np.ascontiguousarray(x),
        "x_rows_tok": rows,
        "x_rows_fm": np.ascontiguousarray(rows.T),
    }

    ws = WIDTHS[vi]
    parts = []
    col = np.arange(512, dtype=np.int64)
    for (t, sc) in sim_slab_order(vi):
        r = lo + 128 * t + np.arange(128, dtype=np.int64)
        cols = 512 * sc + col
        mask = np.where((cols[None, :] < r[:, None]) & (r[:, None] >= 0), 0.0, NEG)
        parts.append(mask.astype(BF))
    m["sim_slab"] = np.ascontiguousarray(np.concatenate(parts, axis=1))

    def rot_slice(tab, fill):
        out = np.full((128, RT - 128), fill, dtype=np.float32)
        a0 = c0 - CH
        s0 = max(0, a0)
        out[:, s0 - a0:] = tab[:, s0:c0 + CH]
        return out

    m["rot_cq"] = rot_slice(Cq, 1.0)
    m["rot_sq"] = rot_slice(Sq, 0.0)
    m["rot_ck"] = rot_slice(Ck, 1.0)
    m["rot_sk"] = rot_slice(Sk, 0.0)

    m["shift_flag"] = np.full((128, 1), 0.0 if v == 0 else 1.0, dtype=np.float32)

    pp_ = np.arange(128)[:, None, None]
    kc_ = np.arange(8)[None, :, None]
    q_ = np.arange(512)[None, None, :]
    diff = q_ - 128 * kc_ + 512 - pp_
    m["la_mask"] = np.where((diff >= 0) & (diff <= 512), 1.0, 0.0).astype(BF)

    m["iden"] = np.eye(128).astype(BF)
    e8 = np.zeros((4, 128, 8), dtype=np.float32)
    e8t = np.zeros((4, 8, 128), dtype=np.float32)
    for c in range(4):
        for hh in range(2):
            h = 2 * c + hh
            e8[c, 64 * hh:64 * hh + 64, h] = 1.0
            e8t[c, h, 64 * hh:64 * hh + 64] = 1.0
    m["e8"] = np.ascontiguousarray(e8.transpose(1, 0, 2))
    m["e8t"] = np.ascontiguousarray(e8t.transpose(1, 0, 2))
    m["ones_col"] = np.ones((128, 1), dtype=BF)
    m["ones_row"] = np.ones((1, 128), dtype=np.float32)
    m["ones_row_bf"] = np.ones((1, 128), dtype=BF)

    for k in ("L0_attn_W_q", "L0_attn_W_k", "L0_attn_gain", "L0_attn_bias",
              "L0_ffn_wqkv", "L0_ffn_q_scale", "L0_ffn_k_scale", "L0_ffn_wo",
              "L0_ffn_ln_g", "L0_ffn_ln_b", "L0_ffn_w1", "L0_ffn_b1",
              "L0_ffn_w2", "L0_ffn_b2"):
        a = np.asarray(inputs[k], dtype=np.float32)
        m[k] = np.ascontiguousarray(a.reshape(a.shape if a.ndim > 1 else (1, -1)))
    m["log_mix"] = np.asarray(inputs["L0_attn_log_mix"], np.float32).reshape(1, 1)
    m["log_scale"] = np.asarray(inputs["L0_attn_log_scale"], np.float32).reshape(1, 1)
    m["variant"] = np.asarray([[vi]], dtype=np.uint32)
    return m


class _AutoDrainEng:
    _SKIP = {"wait_ge", "drain", "dma_start", "dma_start_transpose", "reg_load",
             "alloc_register", "nop", "engine_nop", "wait_op", "sem_inc",
             "indirect_dma_start", "If_eq", "Else", "engine", "memzero"}

    def __init__(self, eng):
        object.__setattr__(self, "_e", eng)

    def __getattr__(self, n):
        a = getattr(self._e, n)
        if (not callable(a)) or n in self._SKIP or n[0].isupper() or not AUTO_DRAIN:
            return a
        eng = self._e

        def w(*args, **kw):
            r = a(*args, **kw)
            eng.drain()
            return r
        return w


AUTO_DRAIN = True


class NCWrap:
    def __init__(self, nc):
        object.__setattr__(self, "_nc", nc)
        object.__setattr__(self, "vector", _AutoDrainEng(nc.vector))
        object.__setattr__(self, "scalar", _AutoDrainEng(nc.scalar))
        object.__setattr__(self, "gpsimd", _AutoDrainEng(nc.gpsimd))

    def __getattr__(self, n):
        return getattr(self._nc, n)


class Sem:
    def __init__(self, stack, nc, name, issuer=None):
        self.h = stack.enter_context(nc.semaphore(name))
        self.n = 0
        self.issuer = issuer
        self.ckpt = 0

    def inc(self, instr, k=1):
        instr.then_inc(self.h, k)
        self.n += k
        return instr

    def dma(self, instr):
        return self.inc(instr, 16)

    def wait(self, eng, at=None):
        v = self.n if at is None else at
        if self.issuer is not None and at is None and self.ckpt != self.n:
            self.issuer.wait_ge(self.h, self.n)
            self.ckpt = self.n
        eng.wait_ge(self.h, v)


class Ctx:
    def __init__(self, nc, stack):
        self.nc = nc
        self.stack = stack
        self.d = {}
        self.b = {}
        issuers = {"qs": nc.sync, "qg": nc.gpsimd, "qa": nc.scalar}
        for name in ("qs", "qg", "qa", "p", "v", "sc", "pl"):
            setattr(self, name, Sem(stack, nc, "g_" + name, issuer=issuers.get(name)))
        self.sems = [self.qs, self.qg, self.qa, self.p, self.v, self.sc, self.pl]

    def sb(self, name, shape, dtype, stack=None, **kw):
        t = (stack or self.stack).enter_context(
            self.nc.sbuf_tensor("sb_" + name, shape, dtype, **kw))
        self.b[name] = t
        return t

    def barrier(self, engines=None):
        nc = self.nc
        engs = engines or (nc.sync, nc.tensor, nc.vector, nc.scalar, nc.gpsimd)
        for eng in engs:
            for s in self.sems:
                s.wait(eng)


def emit_ln_fm(ctx, s, x_chunks, out_chunks, tcols, ws,
               gain_cols=None, beta_cols=None):
    nc = ctx.nc
    ones = ctx.b["ones_col_f32"]
    onesr = ctx.b["ones_row"]
    sq = ws["sq"]
    stat_ps = ws["stat"]
    rep_ps = ws["rep"]
    stat_sb = ws["stat_sb"]
    rep_sb = ws["rep_sb"]
    tc = tcols
    s["v"].wait(nc.scalar)
    s["v"].wait(nc.tensor)
    for c in range(8):
        s["sc"].wait(nc.tensor)
        nc.tensor.matmul(stat_ps[0:1, :tc], ones[:, 0:1], x_chunks[c],
                         start=(c == 0), stop=(c == 7))
        s["p"].wait(nc.scalar)
        i = nc.scalar.activation(out=sq[:, :tc], in_=x_chunks[c], func=AF.Square)
        s["sc"].inc(i)
        s["sc"].wait(nc.tensor)
        i = nc.tensor.matmul(stat_ps[32:33, :tc], ones[:, 0:1], sq[:, :tc],
                             start=(c == 0), stop=(c == 7))
        s["p"].inc(i)
    s["p"].wait(nc.vector)
    mu = ws["mu_sb"][0:1, 0:tc]
    va = stat_sb[0:1, 0:tc]
    nc.vector.tensor_scalar(out=mu, in0=stat_ps[0:1, :tc], scalar1=1.0 / D,
                            scalar2=None, op0=ALU.mult)
    nc.vector.tensor_scalar(out=va, in0=stat_ps[32:33, :tc], scalar1=1.0 / D,
                            scalar2=None, op0=ALU.mult)
    nc.vector.tensor_tensor(rep_sb[0:1, :tc], mu, mu, ALU.mult)
    nc.vector.tensor_tensor(va, va, rep_sb[0:1, :tc], ALU.subtract)
    i = nc.vector.tensor_scalar(out=va, in0=va, scalar1=1e-5, scalar2=None, op0=ALU.add)
    s["v"].inc(i)
    s["v"].wait(nc.scalar)
    nc.scalar.activation(out=va, in_=va, func=AF.Ln)
    i = nc.scalar.activation(out=va, in_=va, func=AF.Exp, scale=-0.5)
    s["sc"].inc(i)
    s["sc"].wait(nc.vector)
    i = nc.vector.tensor_tensor(mu, mu, va, ALU.mult)
    s["v"].inc(i)
    s["v"].wait(nc.tensor)
    nc.tensor.matmul(rep_ps[:, 0:tc], onesr[0:1, :], va, start=True, stop=True)
    i = nc.tensor.matmul(rep_ps[:, tc:2 * tc], onesr[0:1, :], mu, start=True, stop=True)
    s["p"].inc(i)
    s["p"].wait(nc.scalar)
    nc.scalar.copy(out=rep_sb[:, 0:tc], in_=rep_ps[:, 0:tc])
    i = nc.scalar.copy(out=rep_sb[:, tc:2 * tc], in_=rep_ps[:, tc:2 * tc])
    s["sc"].inc(i)
    s["sc"].wait(nc.vector)
    for c in range(8):
        nc.vector.scalar_tensor_tensor(out=sq[:, :tc], in0=x_chunks[c],
                                       scalar=1.0, in1=rep_sb[:, 0:tc],
                                       op0=ALU.bypass, op1=ALU.mult)
        if gain_cols is None:
            i = nc.vector.tensor_tensor(out_chunks[c], sq[:, :tc],
                                        rep_sb[:, tc:2 * tc], ALU.subtract)
        else:
            nc.vector.tensor_tensor(sq[:, :tc], sq[:, :tc],
                                    rep_sb[:, tc:2 * tc], ALU.subtract)
            i = nc.vector.tensor_scalar(out=out_chunks[c], in0=sq[:, :tc],
                                        scalar1=gain_cols[c], scalar2=beta_cols[c],
                                        op0=ALU.mult, op1=ALU.add)
    s["v"].inc(i)


def emit_ln_tok_sweep(ctx, src_ap, dst_ap, nrows):
    nc = ctx.nc
    q = ctx.qs
    with (
        nc.sbuf_tensor("lt_x" + dst_ap.tensor.name, [128, 2, 1024], F32) as xt,
        nc.sbuf_tensor("lt_n" + dst_ap.tensor.name, [128, 2, 1024], BF16) as xnt,
        nc.sbuf_tensor("lt_s" + dst_ap.tensor.name, [128, 2, 4], F32) as stb,
    ):
        for i_t in range(nrows // 128):
            bi = i_t % 2
            if i_t >= 1:
                ctx.v.wait(nc.sync)
                ctx.sc.wait(nc.sync)
            q.dma(nc.sync.dma_start(out=xt[:, bi, :],
                                    in_=src_ap[128 * i_t:128 * (i_t + 1), :]))
            q.wait(nc.vector)
            q.wait(nc.scalar)
            ctx.qg.wait(nc.scalar)
            ctx.qg.wait(nc.vector)
            nc.vector.reduce_sum(stb[:, bi, 0:1], xt[:, bi, :], axis=AX.X)
            i = nc.vector.tensor_scalar(out=stb[:, bi, 0:1], in0=stb[:, bi, 0:1],
                                        scalar1=1.0 / D, scalar2=None, op0=ALU.mult)
            ctx.v.inc(i)
            i = nc.scalar.activation(out=xnt[:, bi, :], in_=xt[:, bi, :],
                                     func=AF.Square, accum_out=stb[:, bi, 1:2])
            ctx.sc.inc(i)
            ctx.v.wait(nc.scalar)
            ctx.sc.wait(nc.vector)
            nc.vector.tensor_scalar(out=stb[:, bi, 1:2], in0=stb[:, bi, 1:2],
                                    scalar1=1.0 / D, scalar2=None, op0=ALU.mult)
            nc.vector.tensor_tensor(stb[:, bi, 2:3], stb[:, bi, 0:1],
                                    stb[:, bi, 0:1], ALU.mult)
            nc.vector.tensor_tensor(stb[:, bi, 1:2], stb[:, bi, 1:2],
                                    stb[:, bi, 2:3], ALU.subtract)
            i = nc.vector.tensor_scalar(out=stb[:, bi, 1:2], in0=stb[:, bi, 1:2],
                                        scalar1=1e-5, scalar2=None, op0=ALU.add)
            ctx.v.inc(i)
            ctx.v.wait(nc.scalar)
            nc.scalar.activation(out=stb[:, bi, 1:2], in_=stb[:, bi, 1:2], func=AF.Ln)
            i = nc.scalar.activation(out=stb[:, bi, 1:2], in_=stb[:, bi, 1:2],
                                     func=AF.Exp, scale=-0.5)
            ctx.sc.inc(i)
            ctx.sc.wait(nc.vector)
            i = nc.vector.tensor_scalar(out=xnt[:, bi, :], in0=xt[:, bi, :],
                                        scalar1=stb[:, bi, 0:1], scalar2=stb[:, bi, 1:2],
                                        op0=ALU.subtract, op1=ALU.mult)
            ctx.v.inc(i)
            ctx.v.wait(nc.gpsimd)
            ctx.qg.dma(nc.gpsimd.dma_start(out=dst_ap[128 * i_t:128 * (i_t + 1), :],
                                           in_=xnt[:, bi, :]))
        ctx.qg.wait(nc.sync)
        ctx.qg.wait(nc.scalar)
        ctx.qg.wait(nc.vector)


def emit_prepass(ctx):
    nc = ctx.nc
    d = ctx.d

    iden = ctx.sb("iden_sb", [128, 128], BF16)
    iden32 = ctx.sb("iden32_sb", [128, 128], F32)
    ones_col = ctx.sb("ones_col", [128, 1], BF16)
    ones_col_f = ctx.sb("ones_col_f32", [128, 1], F32)
    ones_row = ctx.sb("ones_row", [1, 128], F32)
    ones_row_bf = ctx.sb("ones_row_bf", [1, 128], BF16)
    la_mask = ctx.sb("la_mask", [128, 8, 512], BF16)
    e8 = ctx.sb("e8_sb", [128, 4, 8], F32)
    e8t = ctx.sb("e8t_sb", [8, 4, 128], F32)
    sflag = ctx.sb("sflag", [128, 1], F32)
    gain_rep = ctx.sb("gain_rep", [128, 1024], F32, stack=ctx.attn_es, side="right")
    bias_rep = ctx.sb("bias_rep", [128, 1024], F32, stack=ctx.attn_es, side="right")
    scal = ctx.sb("scal", [128, 4], F32)
    wqT = ctx.sb("wqT", [128, 8, 32], BF16, stack=ctx.attn_es, side="right")
    wkT = ctx.sb("wkT", [128, 8, 32], BF16, stack=ctx.attn_es, side="right")
    kT = ctx.sb("kT", [32, T], BF16, stack=ctx.attn_es, side="right")
    qT = ctx.sb("qT", [32, RT], BF16, stack=ctx.attn_es, side="right")

    q = ctx.qs
    q.dma(nc.sync.dma_start(out=iden[:], in_=d["iden"][:]))
    q.dma(nc.sync.dma_start(out=la_mask[:], in_=d["la_mask"][:]))
    q.dma(nc.sync.dma_start(out=e8[:], in_=d["e8"][:]))
    q.dma(nc.sync.dma_start(out=e8t[:], in_=d["e8t"][:]))
    q.dma(nc.sync.dma_start(out=sflag[:], in_=d["shift_flag"][:]))
    q.dma(nc.sync.dma_start(out=ones_col[:], in_=d["ones_col"][:]))
    q.dma(nc.sync.dma_start(out=ones_row[:], in_=d["ones_row"][:]))
    q.dma(nc.sync.dma_start(out=ones_row_bf[:], in_=d["ones_row_bf"][:]))

    es = ExitStack()
    wq_sb = es.enter_context(nc.sbuf_tensor("pp_w", [32, 1024], F32))
    wk_sb = es.enter_context(nc.sbuf_tensor("pp_w2", [32, 1024], F32))
    row_sb = es.enter_context(nc.sbuf_tensor("pp_row", [1, 1024], F32))
    row2_sb = es.enter_context(nc.sbuf_tensor("pp_row2", [1, 1024], F32))
    sc1 = es.enter_context(nc.sbuf_tensor("pp_sc1", [1, 4], F32))
    pps = es.enter_context(nc.psum_tensor("pp_ps", [128, 512], F32))
    pps2 = es.enter_context(nc.psum_tensor("pp_ps2", [128, 512], F32))
    ppst = es.enter_context(nc.psum_tensor("pp_pst", [128, 128], F32))
    if True:
        q.dma(nc.sync.dma_start(out=wq_sb[:], in_=d["L0_attn_W_q"][:]))
        q.dma(nc.sync.dma_start(out=wk_sb[:], in_=d["L0_attn_W_k"][:]))
        q.dma(nc.sync.dma_start(out=row_sb[:], in_=d["L0_attn_gain"][:]))
        q.dma(nc.sync.dma_start(out=row2_sb[:], in_=d["L0_attn_bias"][:]))
        q.dma(nc.sync.dma_start(out=sc1[0:1, 0:1], in_=d["log_mix"][:]))
        q.dma(nc.sync.dma_start(out=sc1[0:1, 1:2], in_=d["log_scale"][:]))
        q.wait(nc.sync)
        q.wait(nc.vector)
        ctx.v.inc(nc.vector.tensor_copy(out=iden32[:], in_=iden[:]))
        ctx.v.inc(nc.vector.tensor_copy(out=ones_col_f[:], in_=ones_col[:]))
        q.wait(nc.tensor)
        ctx.v.wait(nc.tensor)
        for c in range(8):
            i = nc.tensor.transpose(ppst[0:128, 0:32], wq_sb[:, 128 * c:128 * (c + 1)], iden32[0:32, 0:32])
            ctx.p.inc(i)
            ctx.p.wait(nc.vector)
            ctx.v.inc(nc.vector.tensor_copy(out=wqT[:, c, :], in_=ppst[0:128, 0:32]))
            ctx.v.wait(nc.tensor)
            i = nc.tensor.transpose(ppst[0:128, 0:32], wk_sb[:, 128 * c:128 * (c + 1)], iden32[0:32, 0:32])
            ctx.p.inc(i)
            ctx.p.wait(nc.vector)
            ctx.v.inc(nc.vector.tensor_copy(out=wkT[:, c, :], in_=ppst[0:128, 0:32]))
            ctx.v.wait(nc.tensor)
        nc.tensor.matmul(pps[:, :], ones_row[0:1, :], row_sb[0:1, 0:512], start=True, stop=True)
        i = nc.tensor.matmul(pps2[:, :], ones_row[0:1, :], row_sb[0:1, 512:1024], start=True, stop=True)
        ctx.p.inc(i)
        ctx.p.wait(nc.scalar)
        nc.scalar.copy(out=gain_rep[:, 0:512], in_=pps[:, :])
        ctx.sc.inc(nc.scalar.copy(out=gain_rep[:, 512:1024], in_=pps2[:, :]))
        ctx.sc.wait(nc.tensor)
        nc.tensor.matmul(pps[:, :], ones_row[0:1, :], row2_sb[0:1, 0:512], start=True, stop=True)
        i = nc.tensor.matmul(pps2[:, :], ones_row[0:1, :], row2_sb[0:1, 512:1024], start=True, stop=True)
        ctx.p.inc(i)
        ctx.p.wait(nc.scalar)
        nc.scalar.copy(out=bias_rep[:, 0:512], in_=pps[:, :])
        ctx.sc.inc(nc.scalar.copy(out=bias_rep[:, 512:1024], in_=pps2[:, :]))
        q.wait(nc.scalar)
        ctx.sc.inc(nc.scalar.activation(out=sc1[0:1, 2:3], in_=sc1[0:1, 0:1], func=AF.Sigmoid))
        nc.scalar.activation(out=sc1[0:1, 3:4], in_=sc1[0:1, 1:2], func=AF.Abs)
        nc.scalar.activation(out=sc1[0:1, 3:4], in_=sc1[0:1, 3:4], func=AF.Exp, scale=-1.0)
        ctx.sc.inc(nc.scalar.activation(out=sc1[0:1, 3:4], in_=sc1[0:1, 3:4], func=AF.Ln, bias=1.0))
        ctx.sc.wait(nc.vector)
        nc.vector.tensor_relu(row2_sb[0:1, 0:1], sc1[0:1, 1:2])
        nc.vector.tensor_tensor(row2_sb[0:1, 0:1], row2_sb[0:1, 0:1], sc1[0:1, 3:4], ALU.add)
        nc.vector.tensor_scalar(out=row2_sb[0:1, 2:3], in0=row2_sb[0:1, 0:1],
                                scalar1=0.01, scalar2=None, op0=ALU.add)
        nc.vector.tensor_copy(out=row2_sb[0:1, 0:1], in_=sc1[0:1, 2:3])
        i = nc.vector.tensor_scalar(out=row2_sb[0:1, 1:2], in0=sc1[0:1, 2:3],
                                    scalar1=-1.0, scalar2=1.0, op0=ALU.mult, op1=ALU.add)
        ctx.v.inc(i)
        ctx.v.wait(nc.tensor)
        i = nc.tensor.matmul(pps[:, 0:3], ones_row[0:1, :], row2_sb[0:1, 0:3],
                             start=True, stop=True)
        ctx.p.inc(i)
        ctx.p.wait(nc.scalar)
        ctx.sc.inc(nc.scalar.copy(out=scal[:, 0:3], in_=pps[:, 0:3]))

        emit_ln_tok_sweep(ctx, d["x_tok"], d["xn_tok"], T)
        emit_ln_tok_sweep(ctx, d["x_rows_tok"], d["xnrows_tok"], RT)

        fm_es = ExitStack()
        fm = fm_es.enter_context(nc.sbuf_tensor("pp_fm", [128, 8, T], BF16, is_transpose_dest=True))
        if True:
            for c in range(8):
                ctx.qa.dma(nc.scalar.dma_start(
                    out=fm[:, c, :], in_=d["xn_tok"][:, 128 * c:128 * (c + 1)],
                    transpose=True))
            ctx.qa.wait(nc.sync)
            ctx.qa.wait(nc.tensor)
            for c in range(8):
                q.dma(nc.sync.dma_start(out=d["xn_fm"][128 * c:128 * (c + 1), :],
                                        in_=fm[:, c, :]))
            for sc_i in range(T // 512):
                for c in range(8):
                    i = nc.tensor.matmul(pps[0:32, :], wkT[:, c, :],
                                         fm[:, c, 512 * sc_i:512 * (sc_i + 1)],
                                         start=(c == 0), stop=(c == 7))
                ctx.p.inc(i)
                ctx.p.wait(nc.vector)
                ctx.v.inc(nc.vector.tensor_copy(out=kT[:, 512 * sc_i:512 * (sc_i + 1)],
                                                in_=pps[0:32, :]))
                ctx.v.wait(nc.tensor)
            q.wait(nc.sync)
        ctx.barrier()
        fm_es.close()

        stg_es = ExitStack()
        stg = stg_es.enter_context(nc.sbuf_tensor("pp_w2stg", [128, 2, 11, 512], BF16))
        idx = 0
        for l2 in range(2):
            for g2 in range(2):
                for h2 in range(2):
                    sl2 = idx % 2
                    if idx >= 2:
                        ctx.qs.wait(nc.gpsimd)
                    ctx.qg.dma(nc.gpsimd.dma_start(
                        out=stg[:, sl2],
                        in_=d["w2h"][l2, g2, 11 * h2:11 * h2 + 11].rearrange("i p f -> p i f")))
                    ctx.qg.wait(nc.sync)
                    ctx.qs.dma(nc.sync.dma_start(
                        out=d["w2b"][l2, g2, 11 * h2:11 * h2 + 11].rearrange("i p f -> p i f"),
                        in_=stg[:, sl2]))
                    idx += 1
        ctx.qs.wait(nc.sync)
        stg_es.close()

        sg = {"q": ctx.qs, "g": ctx.qg, "p": ctx.p, "v": ctx.v, "sc": ctx.sc}
        xr = es.enter_context(nc.sbuf_tensor("pp_xr", [128, 8, 512], F32))
        xn = es.enter_context(nc.sbuf_tensor("pp_xn", [128, 8, 512], BF16))
        sq = es.enter_context(nc.sbuf_tensor("pp_sq", [128, 512], F32))
        stat_sb = es.enter_context(nc.sbuf_tensor("pp_stsb", [33, 512], F32))
        mu_sb = es.enter_context(nc.sbuf_tensor("pp_musb", [1, 512], F32))
        rep_sb = es.enter_context(nc.sbuf_tensor("pp_repsb", [128, 1024], F32))
        stat_ps = es.enter_context(nc.psum_tensor("pp_stat", [33, 512], F32))
        rep_ps = es.enter_context(nc.psum_tensor("pp_rep", [128, 1024], F32))
        if True:
            ws = dict(sq=sq, stat=stat_ps, rep=rep_ps, stat_sb=stat_sb, rep_sb=rep_sb, mu_sb=mu_sb)
            for ci in range((RT + 511) // 512):
                t0 = 512 * ci
                tc = min(512, RT - t0)
                q.dma(nc.sync.dma_start(
                    out=xr[:, :, :tc],
                    in_=d["x_rows_fm"].rearrange("(c p) t -> p c t", p=128)[:, :, t0:t0 + tc]))
                q.wait(nc.vector)
                q.wait(nc.scalar)
                q.wait(nc.tensor)
                emit_ln_fm(ctx, sg, [xr[:, c, :tc] for c in range(8)],
                           [xn[:, c, :tc] for c in range(8)], tc, ws)
                ctx.v.wait(nc.gpsimd)
                ctx.qg.dma(nc.gpsimd.dma_start(
                    out=d["xnrows"].rearrange("(c p) t -> p c t", p=128)[:, :, t0:t0 + tc],
                    in_=xn[:, :, :tc]))
                ctx.v.wait(nc.tensor)
                for c in range(8):
                    i = nc.tensor.matmul(pps2[0:32, :tc], wqT[:, c, :], xn[:, c, :tc],
                                         start=(c == 0), stop=(c == 7))
                ctx.p.inc(i)
                ctx.p.wait(nc.vector)
                ctx.v.inc(nc.vector.tensor_copy(out=qT[:, t0:t0 + tc], in_=pps2[0:32, :tc]))
                ctx.v.wait(nc.tensor)
                ctx.qg.wait(nc.sync)
    ctx.barrier()
    es.close()


def emit_attn(ctx, vi, s):
    nc = ctx.nc
    d = ctx.d
    groups = attn_groups(vi)
    order = sim_slab_order(vi)
    slab_off = {}
    off = 0
    for key in order:
        slab_off[key] = off
        off += 512
    qT = ctx.b["qT"]
    kT = ctx.b["kT"]
    iden = ctx.b["iden_sb"]
    iden32 = ctx.b["iden32_sb"]
    scal = ctx.b["scal"]
    gain_rep = ctx.b["gain_rep"]
    bias_rep = ctx.b["bias_rep"]
    ISQ32 = 1.0 / math.sqrt(32.0)

    aes = ExitStack()
    sa = aes.enter_context(nc.sbuf_tensor("at_sa%d" % vi, [128, G, 16, 512], BF16))
    msb = aes.enter_context(nc.sbuf_tensor("at_ms%d" % vi, [128, 16, 512], F32))
    xf = aes.enter_context(nc.sbuf_tensor("at_xf%d" % vi, [128, 2, 8, 512], BF16))
    xt = aes.enter_context(nc.sbuf_tensor("at_xt%d" % vi, [128, 2, 4, 1024], BF16))
    sl = aes.enter_context(nc.sbuf_tensor("at_sl%d" % vi, [128, G, 512], BF16))
    rb = aes.enter_context(nc.sbuf_tensor("at_rb%d" % vi, [128, 8, G * 128], BF16))
    scb = aes.enter_context(nc.sbuf_tensor("at_sb%d" % vi, [128, 2, 512], F32))
    m8 = aes.enter_context(nc.sbuf_tensor("at_m8%d" % vi, [128, G, 128], BF16))
    t8 = aes.enter_context(nc.sbuf_tensor("at_t8%d" % vi, [128, 8], BF16))
    nr = aes.enter_context(nc.sbuf_tensor("at_nr%d" % vi, [128, G, 4], F32))
    aT = aes.enter_context(nc.sbuf_tensor("at_aT%d" % vi, [128, 4, 128], BF16))
    ep = aes.enter_context(nc.sbuf_tensor("at_ep%d" % vi, [128, 3, 1024], F32))
    xnt_t = aes.enter_context(nc.sbuf_tensor("at_xn%d" % vi, [128, 1024], BF16))
    xr_t = aes.enter_context(nc.sbuf_tensor("at_xr%d" % vi, [128, 1024], F32))
    x1T = aes.enter_context(nc.sbuf_tensor("at_xT%d" % vi, [128, 8, 128], F32))
    if True:
        sa_f = sa.ap().rearrange("p g a b -> p (g a b)")
        ms_f = msb.ap().rearrange("p a b -> p (a b)")
        for (tiles, wmax, tws) in groups:
            g0 = tiles[0]
            nsc = wmax // 512
            s["q"].dma(nc.sync.dma_start(
                out=rb[:, :, :len(tiles) * 128],
                in_=d["xnrows"].rearrange("(c p) t -> p c t", p=128)[:, :, 128 * g0:128 * (g0 + len(tiles))]))
            with (
                nc.psum_tensor("ps_sim0_%d_%d" % (vi, g0), [128, 512], F32) as ps0,
                nc.psum_tensor("ps_sim1_%d_%d" % (vi, g0), [128, 512], F32) as ps1,
                nc.psum_tensor("ps_sim2_%d_%d" % (vi, g0), [128, 512], F32) as ps2,
            ):
                simps = [ps0, ps1, ps2]
                for sc_i in range(nsc):
                    bi = sc_i % 2
                    if sc_i >= 2:
                        s["p"].wait(nc.sync)
                    s["q"].dma(nc.sync.dma_start(
                        out=xf[:, bi, :, :],
                        in_=d["xn_fm"].rearrange("(c p) t -> p c t", p=128)[:, :, 512 * sc_i:512 * (sc_i + 1)]))
                    for j, t in enumerate(tiles):
                        if tws[j] <= sc_i * 512:
                            continue
                        if sc_i >= 1:
                            s["v"].wait(nc.sync)
                        s["q"].dma(nc.sync.dma_start(
                            out=sl[:, j, :],
                            in_=d["sim_slab"][:, slab_off[(t, sc_i)]:slab_off[(t, sc_i)] + 512]))
                        s["q"].wait(nc.tensor)
                        s["v"].wait(nc.tensor)
                        for c in range(8):
                            i = nc.tensor.matmul(simps[j][:, :], rb[:, c, 128 * j:128 * (j + 1)],
                                                 xf[:, bi, c, :], start=(c == 0), stop=(c == 7))
                        s["p"].inc(i)
                        s["p"].wait(nc.vector)
                        s["q"].wait(nc.vector)
                        i = nc.vector.scalar_tensor_tensor(
                            out=sa_f[:, 8192 * j + 512 * sc_i:8192 * j + 512 * (sc_i + 1)],
                            in0=simps[j][:, :], scalar=1.0 / 32.0,
                            in1=sl[:, j, :], op0=ALU.mult, op1=ALU.add)
                        s["v"].inc(i)
                        i = nc.vector.max(out=m8[:, j, 8 * sc_i:8 * sc_i + 8],
                                          in_=sa_f[:, 8192 * j + 512 * sc_i:8192 * j + 512 * (sc_i + 1)])
                        s["v"].inc(i)
            with nc.psum_tensor("ps_scr_%d_%d" % (vi, g0), [128, 2, 512], F32) as scps:
                for j, t in enumerate(tiles):
                    ntc = tws[j] // 512
                    s["sc"].wait(nc.vector)
                    i = nc.vector.max(out=t8[:, :], in_=m8[:, j, 0:8 * ntc])
                    s["v"].inc(i)
                    for sc_i in range(ntc):
                        pbi = sc_i % 2
                        s["sc"].wait(nc.tensor)
                        i = nc.tensor.matmul(scps[:, pbi, :], qT[:, 128 * t:128 * (t + 1)],
                                             kT[:, 512 * sc_i:512 * (sc_i + 1)],
                                             start=True, stop=True)
                        s["p"].inc(i)
                        s["p"].wait(nc.scalar)
                        s["v"].wait(nc.scalar)
                        i = nc.scalar.activation(out=scb[:, pbi, :], in_=scps[:, pbi, :],
                                                 func=AF.Copy, bias=C_OFF, scale=ISQ32)
                        s["sc"].inc(i)
                        if t in SCORE_SLAB_TILES[vi]:
                            s["v"].wait(nc.sync)
                            s["q"].dma(nc.sync.dma_start(
                                out=sl[:, 0, :],
                                in_=d["sim_slab"][:, slab_off[(t, sc_i)]:slab_off[(t, sc_i)] + 512]))
                            s["q"].wait(nc.vector)
                            s["sc"].wait(nc.vector)
                            i = nc.vector.tensor_tensor(scb[:, pbi, :], scb[:, pbi, :],
                                                        sl[:, 0, :], ALU.add)
                            s["v"].inc(i)
                        s["sc"].wait(nc.vector)
                        i = nc.vector.scalar_tensor_tensor(
                            out=ms_f[:, 512 * sc_i:512 * (sc_i + 1)],
                            in0=sa_f[:, 8192 * j + 512 * sc_i:8192 * j + 512 * (sc_i + 1)],
                            scalar=t8[:, 7:8],
                            in1=scb[:, pbi, :], op0=ALU.is_ge, op1=ALU.mult)
                        s["v"].inc(i)
                    i = nc.vector.reduce_max(nr[:, j, 0:1], ms_f[:, 0:tws[j]],
                                             axis=AX.X, negate=True)
                    s["v"].inc(i)
                    s["v"].wait(nc.scalar)
                    i = nc.scalar.activation(
                        out=sa_f[:, 8192 * j:8192 * j + tws[j]],
                        in_=ms_f[:, 0:tws[j]],
                        func=AF.Exp, bias=nr[:, j, 0:1], accum_out=nr[:, j, 1:2])
                    s["sc"].inc(i)
                    s["sc"].wait(nc.vector)
                    i = nc.vector.reciprocal(out=nr[:, j, 2:3], in_=nr[:, j, 1:2])
                    s["v"].inc(i)
            pmes = ExitStack()
            pm0 = pmes.enter_context(nc.psum_tensor("ps_m0_%d_%d" % (vi, g0), [128, 2, 512], F32))
            pm1 = pmes.enter_context(nc.psum_tensor("ps_m1_%d_%d" % (vi, g0), [128, 2, 512], F32))
            pm2 = pmes.enter_context(nc.psum_tensor("ps_m2_%d_%d" % (vi, g0), [128, 2, 512], F32))
            tpes = ExitStack()
            tpps = tpes.enter_context(nc.psum_tensor("ps_tp_%d_%d" % (vi, g0), [128, 512], BF16))
            if True:
                msgps = [pm0, pm1, pm2]
                for sc_i in range(nsc):
                    bi = sc_i % 2
                    if sc_i >= 2:
                        s["p"].wait(nc.gpsimd)
                    s["g"].dma(nc.gpsimd.dma_start(
                        out=xt[:, bi, :, :],
                        in_=d["xn_tok"].rearrange("(a p) f -> p a f", p=128)[:, 4 * sc_i:4 * (sc_i + 1), :]))
                    for j, t in enumerate(tiles):
                        if tws[j] <= sc_i * 512:
                            continue
                        s["g"].wait(nc.tensor)
                        for q4 in range(4):
                            i = nc.tensor.transpose(
                                tpps[:, 128 * q4:128 * (q4 + 1)],
                                sa_f[:, 8192 * j + 512 * sc_i + 128 * q4:8192 * j + 512 * sc_i + 128 * (q4 + 1)],
                                iden[:])
                        s["p"].inc(i)
                        s["p"].wait(nc.vector)
                        i = nc.vector.tensor_copy(
                            out=aT.ap().rearrange("p a b -> p (a b)")[:, :],
                            in_=tpps[:, :])
                        s["v"].inc(i)
                        s["v"].wait(nc.tensor)
                        first = (sc_i == 0)
                        last = (sc_i == tws[j] // 512 - 1)
                        for q4 in range(4):
                            nc.tensor.matmul(msgps[j][:, 0, :], aT[:, q4, :],
                                             xt[:, bi, q4, 0:512],
                                             start=(first and q4 == 0), stop=(last and q4 == 3))
                            i = nc.tensor.matmul(msgps[j][:, 1, :], aT[:, q4, :],
                                                 xt[:, bi, q4, 512:1024],
                                                 start=(first and q4 == 0), stop=(last and q4 == 3))
                        s["p"].inc(i)
                tpes.close()
                with nc.psum_tensor("ps_x1_%d_%d" % (vi, g0), [128, 2, 512], F32) as xtp:
                    for j, t in enumerate(tiles):
                        s["q"].wait(nc.sync, at=None)
                        s["q"].dma(nc.sync.dma_start(
                            out=xnt_t[:, :], in_=d["xnrows_tok"][128 * t:128 * (t + 1), :]))
                        s["q"].dma(nc.sync.dma_start(
                            out=xr_t[:, :], in_=d["x_rows_tok"][128 * t:128 * (t + 1), :]))
                        s["q"].wait(nc.vector)
                        s["p"].wait(nc.vector)
                        msgv = msgps[j].ap().rearrange("p a b -> p (a b)")
                        nc.vector.tensor_scalar(out=ep[:, 0, :], in0=msgv[:, :],
                                                scalar1=nr[:, j, 2:3], scalar2=scal[:, 1:2],
                                                op0=ALU.mult, op1=ALU.mult)
                        nc.vector.scalar_tensor_tensor(out=ep[:, 1, :], in0=xnt_t[:, :],
                                                       scalar=scal[:, 0:1], in1=ep[:, 0, :],
                                                       op0=ALU.mult, op1=ALU.add)
                        nc.vector.tensor_tensor(ep[:, 1, :], ep[:, 1, :], gain_rep[:, :], ALU.mult)
                        i = nc.vector.tensor_tensor(ep[:, 1, :], ep[:, 1, :], bias_rep[:, :], ALU.add)
                        s["v"].inc(i)
                        s["v"].wait(nc.scalar)
                        i = nc.scalar.activation(out=ep[:, 2, :], in_=ep[:, 1, :], func=AF.Gelu)
                        s["sc"].inc(i)
                        s["sc"].wait(nc.vector)
                        nc.vector.tensor_scalar(out=ep[:, 2, :], in0=ep[:, 2, :],
                                                scalar1=scal[:, 2:3], scalar2=None, op0=ALU.mult)
                        i = nc.vector.tensor_tensor(ep[:, 0, :], ep[:, 2, :], xr_t[:, :], ALU.add)
                        s["v"].inc(i)
                        s["v"].wait(nc.tensor)
                        s["q"].wait(nc.tensor)
                        for c in range(8):
                            i = nc.tensor.transpose(
                                xtp[:, c // 4, 128 * (c % 4):128 * (c % 4) + 128],
                                ep[:, 0, 128 * c:128 * (c + 1)], iden32[:])
                            if c in (3, 7):
                                s["p"].inc(i)
                        s["p"].wait(nc.vector)
                        x1T_f = x1T.ap().rearrange("p a b -> p (a b)")
                        nc.vector.tensor_copy(out=x1T_f[:, 0:512], in_=xtp[:, 0, :])
                        i = nc.vector.tensor_copy(out=x1T_f[:, 512:1024], in_=xtp[:, 1, :])
                        s["v"].inc(i)
                        s["v"].wait(nc.sync)
                        s["q"].dma(nc.sync.dma_start(
                            out=d["x1_fm"].rearrange("(c p) t -> p c t", p=128)[:, :, 128 * t:128 * (t + 1)],
                            in_=x1T[:, :, :]))
            pmes.close()
            for eng in (nc.sync, nc.tensor, nc.vector, nc.scalar, nc.gpsimd):
                for ss in s.values():
                    ss.wait(eng)
    aes.close()


def emit_la(ctx, layer, t0, q0, nw, in_stage, ln_first, z0_stage,
            resid_stage, out_stage):
    nc = ctx.nc
    d = ctx.d
    s = {"q": ctx.qs, "g": ctx.qg, "p": ctx.p, "v": ctx.v, "sc": ctx.sc}
    iden = ctx.b["iden_sb"]
    ones_col = ctx.b["ones_col"]
    ones_row = ctx.b["ones_row"]
    la_mask = ctx.b["la_mask"]
    e8 = ctx.b["e8_sb"]
    e8t = ctx.b["e8t_sb"]
    sflag = ctx.b["sflag"]
    nch = (RT - t0) // 512
    NK = RT - t0

    les = ExitStack()
    wqkv = les.enter_context(nc.sbuf_tensor("la_wqkv_%d" % layer, [128, 8, 1536], BF16))
    wo = les.enter_context(nc.sbuf_tensor("la_wo_%d" % layer, [128, 4, 1024], BF16))
    qs_col = les.enter_context(nc.sbuf_tensor("la_qs_%d" % layer, [128, 1], F32))
    ks_col = les.enter_context(nc.sbuf_tensor("la_ks_%d" % layer, [128, 1], F32))
    kh = les.enter_context(nc.sbuf_tensor("la_kh_%d" % layer, [128, 4, NK], BF16))
    qh = les.enter_context(nc.sbuf_tensor("la_qh_%d" % layer, [128, 4, NK - 512], BF16))
    vt = les.enter_context(nc.sbuf_tensor("la_vt_%d" % layer, [128, NK // 128, 512], BF16))
    if True:
        ctx.qg.dma(nc.gpsimd.dma_start(
            out=wqkv[:],
            in_=d["L0_ffn_wqkv"][layer].rearrange("(c p) f -> p c f", p=128)))
        ctx.qg.dma(nc.gpsimd.dma_start(
            out=wo[:],
            in_=d["L0_ffn_wo"][layer].rearrange("(c p) f -> p c f", p=128)))
        s["q"].dma(nc.sync.dma_start(out=qs_col[:], in_=d["qscale_col"][layer]))
        s["q"].dma(nc.sync.dma_start(out=ks_col[:], in_=d["kscale_col"][layer]))
        pa_es = ExitStack()
        zt = pa_es.enter_context(nc.sbuf_tensor("laA_z_%d" % layer, [128, 8, 513], F32))
        qraw = pa_es.enter_context(nc.sbuf_tensor("laA_qr_%d" % layer, [128, 4, 512], F32))
        kraw = pa_es.enter_context(nc.sbuf_tensor("laA_kr_%d" % layer, [128, 4, 512], F32))
        ub = pa_es.enter_context(nc.sbuf_tensor("laA_u_%d" % layer, [128, 512], F32))
        ab = pa_es.enter_context(nc.sbuf_tensor("laA_a_%d" % layer, [128, 512], F32))
        bb = pa_es.enter_context(nc.sbuf_tensor("laA_b_%d" % layer, [128, 512], F32))
        sq = pa_es.enter_context(nc.sbuf_tensor("laA_sq_%d" % layer, [128, 512], F32))
        stat_sb = pa_es.enter_context(nc.sbuf_tensor("laA_st_%d" % layer, [33, 512], F32))
        rep_sb = pa_es.enter_context(nc.sbuf_tensor("laA_rp_%d" % layer, [128, 1024], F32))
        r8 = pa_es.enter_context(nc.sbuf_tensor("laA_r8_%d" % layer, [8, 512], F32))
        rrep = pa_es.enter_context(nc.sbuf_tensor("laA_rr_%d" % layer, [128, 512], F32))
        rot = pa_es.enter_context(nc.sbuf_tensor("laA_rot_%d" % layer, [128, 4, 512], F32))
        ps0 = pa_es.enter_context(nc.psum_tensor("laA_ps_%d" % layer, [128, 512], F32))
        ps1 = pa_es.enter_context(nc.psum_tensor("laA_ps2_%d" % layer, [128, 512], F32))
        ps8 = pa_es.enter_context(nc.psum_tensor("laA_st8_%d" % layer, [8, 512], F32))
        stat_ps = pa_es.enter_context(nc.psum_tensor("laA_stat_%d" % layer, [33, 512], F32))
        rep_ps = pa_es.enter_context(nc.psum_tensor("laA_rep_%d" % layer, [128, 1024], F32))
        mu_sb = pa_es.enter_context(nc.sbuf_tensor("laA_mu_%d" % layer, [1, 512], F32))
        if True:
            lnws = dict(sq=sq, stat=stat_ps, rep=rep_ps, stat_sb=stat_sb, rep_sb=rep_sb, mu_sb=mu_sb)
            ctx.qg.wait(nc.tensor)
            ctx.qg.wait(nc.sync)
            s["q"].wait(nc.tensor)
            s["q"].wait(nc.vector)
            for ci in range(nch):
                start = t0 + 512 * ci
                s["v"].wait(nc.sync)
                s["p"].wait(nc.sync)
                s["sc"].wait(nc.sync)
                ctx.qg.wait(nc.sync)
                s["q"].dma(nc.sync.dma_start(
                    out=zt[:, :, :],
                    in_=in_stage.rearrange("(c p) t -> p c t", p=128)[:, :, start - 1:start + 512]))
                s["q"].wait(nc.vector)
                s["q"].wait(nc.scalar)
                s["q"].wait(nc.tensor)
                if ln_first:
                    emit_ln_fm(ctx, s, [zt[:, c, 0:1] for c in range(8)],
                               [zt[:, c, 0:1] for c in range(8)], 1, lnws)
                    emit_ln_fm(ctx, s, [zt[:, c, 1:513] for c in range(8)],
                               [zt[:, c, 1:513] for c in range(8)], 512, lnws)
                    s["v"].wait(nc.gpsimd)
                    ctx.qg.dma(nc.gpsimd.dma_start(
                        out=z0_stage.rearrange("(c p) t -> p c t", p=128)[:, :, start:start + 512],
                        in_=zt[:, :, 1:513]))
                if start - 1 <= HALO - 1 < start + 512:
                    loc = HALO - start
                    ctx.qg.wait(nc.vector)
                    i = nc.vector.tensor_scalar(out=zt[:, 4:8, loc:loc + 1],
                                                in0=zt[:, 4:8, loc:loc + 1],
                                                scalar1=sflag[:, 0:1], scalar2=None,
                                                op0=ALU.mult)
                    s["v"].inc(i)
                hviews = [zt[:, c, 1:513] for c in range(4)] + [zt[:, c, 0:512] for c in range(4, 8)]
                s["v"].wait(nc.vector)
                if ci == 0:
                    hbf_ctx = nc.sbuf_tensor("laA_hbf_%d" % layer, [128, 8, 512], BF16)
                    hbf = hbf_ctx.__enter__()
                for c in range(8):
                    i = nc.vector.tensor_copy(out=hbf[:, c, :], in_=hviews[c])
                s["v"].inc(i)
                s["v"].wait(nc.tensor)
                for fc in range(8):
                    dst = qraw if fc < 4 else kraw
                    ps = ps0 if fc % 2 == 0 else ps1
                    for c in range(8):
                        i = nc.tensor.matmul(ps[:, :], wqkv[:, c, 128 * fc:128 * (fc + 1)],
                                             hbf[:, c, :], start=(c == 0), stop=(c == 7))
                    s["p"].inc(i)
                    s["p"].wait(nc.scalar)
                    i = nc.scalar.copy(out=dst[:, fc % 4, :], in_=ps[:, :])
                    s["sc"].inc(i)
                    s["sc"].wait(nc.tensor)
                for t4 in range(4):
                    ps = ps0 if t4 % 2 == 0 else ps1
                    for c in range(8):
                        i = nc.tensor.matmul(ps[:, :], hbf[:, c, 128 * t4:128 * (t4 + 1)],
                                             wqkv[:, c, 1024:1536], start=(c == 0), stop=(c == 7))
                    s["p"].inc(i)
                    s["p"].wait(nc.vector)
                    i = nc.vector.tensor_copy(out=vt[:, 4 * ci + t4, :], in_=ps[:, :])
                    s["v"].inc(i)
                    s["v"].wait(nc.tensor)
                for (rslot, rname) in ((0, "rot_cq"), (1, "rot_sq"), (2, "rot_ck"), (3, "rot_sk")):
                    s["q"].dma(nc.sync.dma_start(
                        out=rot[:, rslot, :],
                        in_=d[rname][:, start - B1_T0:start - B1_T0 + 512]))
                if DEBUG_LA and layer == 0 and ci == 4:
                    s["sc"].wait(nc.gpsimd)
                    ctx.qg.dma(nc.gpsimd.dma_start(out=d["dbg_kraw"][:], in_=kraw[:, :, :]))
                    ctx.qg.dma(nc.gpsimd.dma_start(out=d["dbg_hbf"][:], in_=hbf[:, :, :]))
                    ctx.qg.wait(nc.vector)
                    ctx.qg.wait(nc.scalar)
                for (raw, outfm, sccol, cslot, sslot) in (
                        (qraw, qh, qs_col, 0, 1), (kraw, kh, ks_col, 2, 3)):
                    if outfm is qh and ci == 0:
                        continue
                    s["v"].wait(nc.scalar)
                    s["p"].wait(nc.scalar)
                    for c in range(4):
                        i = nc.scalar.activation(out=sq[:, :], in_=raw[:, c, :], func=AF.Square)
                        s["sc"].inc(i)
                        s["sc"].wait(nc.tensor)
                        i = nc.tensor.matmul(ps8[:, :], e8[:, c, :], sq[:, :],
                                             start=(c == 0), stop=(c == 3))
                        s["p"].inc(i)
                        s["p"].wait(nc.scalar)
                    s["v"].wait(nc.scalar)
                    i = nc.scalar.copy(out=r8[:, :], in_=ps8[:, :])
                    s["sc"].inc(i)
                    s["sc"].wait(nc.vector)
                    i = nc.vector.tensor_scalar_max(r8[:, :], r8[:, :], 1e-24)
                    s["v"].inc(i)
                    s["v"].wait(nc.scalar)
                    nc.scalar.activation(out=r8[:, :], in_=r8[:, :], func=AF.Ln)
                    i = nc.scalar.activation(out=r8[:, :], in_=r8[:, :], func=AF.Exp, scale=-0.5)
                    s["sc"].inc(i)
                    s["sc"].wait(nc.vector)
                    i = nc.vector.tensor_copy(out=r8[:, :], in_=r8[:, :])
                    s["v"].inc(i)
                    if DEBUG_LA and layer == 0 and ci == 4 and raw is kraw:
                        s["v"].wait(nc.gpsimd)
                        ctx.qg.dma(nc.gpsimd.dma_start(out=d["dbg_r8"][:], in_=r8[:, :]))
                        ctx.qg.wait(nc.tensor)
                        ctx.qg.wait(nc.vector)
                    s["v"].wait(nc.tensor)
                    for c in range(4):
                        i = nc.tensor.matmul(rep_ps[:, 0:512], e8t[:, c, :], r8[:, :],
                                             start=True, stop=True)
                        s["p"].inc(i)
                        s["p"].wait(nc.scalar)
                        i = nc.scalar.copy(out=rrep[:, :], in_=rep_ps[:, 0:512])
                        s["sc"].inc(i)
                        s["sc"].wait(nc.vector)
                        s["q"].wait(nc.vector)
                        nc.vector.scalar_tensor_tensor(out=ub[:, :], in0=raw[:, c, :],
                                                       scalar=sccol[:, 0:1], in1=rrep[:, :],
                                                       op0=ALU.mult, op1=ALU.mult)
                        nc.vector.tensor_tensor(ab[:, :], ub[:, :], rot[:, cslot, :], ALU.mult)
                        for po in (0, 32, 64, 96):
                            src = po + 32 if po % 64 == 0 else po - 32
                            nc.vector.tensor_tensor(bb[po:po + 32, :], ub[src:src + 32, :],
                                                    rot[src:src + 32, sslot, :], ALU.mult)
                        oc = 512 * (ci - 1) if outfm is qh else 512 * ci
                        i = nc.vector.tensor_tensor(outfm[:, c, oc:oc + 512],
                                                    ab[:, :], bb[:, :], ALU.add)
                        s["v"].inc(i)
                        s["v"].wait(nc.tensor)
            if nch > 0:
                hbf_ctx.__exit__(None, None, None)
            if DEBUG_LA and layer == 0:
                ctx.qg.dma(nc.gpsimd.dma_start(out=d["dbg_qh"][:], in_=qh[:, :, 0:RT - CH - 512]))
                ctx.qg.dma(nc.gpsimd.dma_start(out=d["dbg_kh"][:], in_=kh[:, :, 0:RT - CH]))
                ctx.qg.dma(nc.gpsimd.dma_start(out=d["dbg_vt"][:], in_=vt[:, 0:(RT - CH) // 128, :]))
                ctx.qg.wait(nc.sync)
        ctx.barrier()
        pa_es.close()
        pb_es = ExitStack()
        eTr = pb_es.enter_context(nc.sbuf_tensor("laB_eT_%d" % layer, [128, 2, 4, 512], BF16))
        eraw = pb_es.enter_context(nc.sbuf_tensor("laB_er_%d" % layer, [128, 2, 4, 512], BF16))
        ofm = pb_es.enter_context(nc.sbuf_tensor("laB_of_%d" % layer, [128, 4, 512], F32))
        obf = pb_es.enter_context(nc.sbuf_tensor("laB_ob_%d" % layer, [128, 4, 512], BF16))
        srep = pb_es.enter_context(nc.sbuf_tensor("laB_sr_%d" % layer, [128, 512], F32))
        rqp = pb_es.enter_context(nc.sbuf_tensor("laB_rp_%d" % layer, [128, 4, 512], BF16))
        rqn = pb_es.enter_context(nc.sbuf_tensor("laB_rn_%d" % layer, [128, 4, 512], BF16))
        kst4 = pb_es.enter_context(nc.sbuf_tensor("laB_k4_%d" % layer, [128, 4, 2], BF16))
        sums = pb_es.enter_context(nc.sbuf_tensor("laB_sm_%d" % layer, [1, 4096], F32))
        brow2 = pb_es.enter_context(nc.sbuf_tensor("laB_br_%d" % layer, [1, 512], BF16))
        resid = pb_es.enter_context(nc.sbuf_tensor("laB_zi_%d" % layer, [128, 8, 512], F32))
        scps = pb_es.enter_context(nc.psum_tensor("laB_sc_%d" % layer, [128, 4, 512], F32))
        oTps = pb_es.enter_context(nc.psum_tensor("laB_oT_%d" % layer, [64, 512], F32))
        swps = pb_es.enter_context(nc.psum_tensor("laB_sw_%d" % layer, [1, 512], F32))
        bdps = pb_es.enter_context(nc.psum_tensor("laB_bd_%d" % layer, [1, 512], F32))
        srps = pb_es.enter_context(nc.psum_tensor("laB_sp_%d" % layer, [128, 512], F32))
        spw = (HALO - q0) // 512 if (HALO - q0) % 512 == 0 else -1
        ones_row_bf = ctx.b["ones_row_bf"]
        if True:
            scps_f = scps.ap().rearrange("p a b -> p (a b)")
            eraw_f = eraw.ap().rearrange("p s a b -> p (s a b)")
            eTr_f = eTr.ap().rearrange("p s a b -> p (s a b)")
            mask_f = la_mask.ap().rearrange("p a b -> p (a b)")
            for w in range(nw):
                q0w = q0 + WIN * w
                qcol = q0w - t0
                kb = q0w - WIN
                kcol = kb - t0
                s["v"].wait(nc.sync)
                ctx.qg.wait(nc.sync)
                s["q"].dma(nc.sync.dma_start(
                    out=resid[:, :, :],
                    in_=resid_stage.rearrange("(c p) t -> p c t", p=128)[:, :, q0w:q0w + 512]))
                s["p"].wait(nc.vector)
                s["p"].wait(nc.scalar)
                for c in range(4):
                    nc.vector.reduce_max(kst4[:, c, 0:1], kh[:, c, kcol:kcol + 1024], axis=AX.X)
                    i = nc.vector.tensor_reduce(kst4[:, c, 1:2], kh[:, c, kcol:kcol + 1024],
                                                axis=AX.X, op=ALU.min, negate=True)
                s["v"].inc(i)
                nc.scalar.activation(out=rqp[:, :, :], in_=qh[:, :, qcol - 512:qcol], func=AF.Relu)
                i = nc.scalar.activation(out=rqn[:, :, :], in_=qh[:, :, qcol - 512:qcol],
                                         func=AF.Relu, scale=-1.0)
                s["sc"].inc(i)
                for h in range(HEADS):
                    c = h // 2
                    po = 64 * (h % 2)
                    s["v"].wait(nc.tensor)
                    s["sc"].wait(nc.tensor)
                    nc.tensor.matmul(bdps[0:1, :], kst4[po:po + 64, c, 0:1],
                                     rqp[po:po + 64, c, :], start=True, stop=False)
                    i = nc.tensor.matmul(bdps[0:1, :], kst4[po:po + 64, c, 1:2],
                                         rqn[po:po + 64, c, :], start=False, stop=True)
                    s["p"].inc(i)
                    s["p"].wait(nc.scalar)
                    i = nc.scalar.activation(out=brow2[0:1, :], in_=bdps[0:1, :],
                                             func=AF.Copy, scale=-1.0, bias=5.625)
                    s["sc"].inc(i)
                    for g4 in range(2):
                        s["sc"].wait(nc.tensor)
                        for j in range(4):
                            kc = 4 * g4 + j
                            nc.tensor.matmul(scps[:, j, :],
                                             kh[po:po + 64, c, kcol + 128 * kc:kcol + 128 * (kc + 1)],
                                             qh[po:po + 64, c, qcol - 512:qcol],
                                             start=True, stop=False)
                            i = nc.tensor.matmul(scps[:, j, :], ones_row_bf[0:1, :],
                                                 brow2[0:1, :], start=False, stop=True)
                        s["p"].inc(i)
                        s["p"].wait(nc.scalar)
                        s["v"].wait(nc.scalar)
                        i = nc.scalar.activation(out=eraw_f[:, 2048 * g4:2048 * (g4 + 1)],
                                                 in_=scps_f[:, :], func=AF.Exp,
                                                 scale=8.0)
                        s["sc"].inc(i)
                        s["sc"].wait(nc.vector)
                        s["p"].wait(nc.vector)
                        i = nc.vector.scalar_tensor_tensor(
                            out=eTr_f[:, 2048 * g4:2048 * (g4 + 1)],
                            in0=eraw_f[:, 2048 * g4:2048 * (g4 + 1)],
                            scalar=1e30,
                            in1=mask_f[:, 2048 * g4:2048 * (g4 + 1)],
                            op0=ALU.min, op1=ALU.mult)
                        if w == spw and g4 == 0:
                            i = nc.vector.tensor_scalar(
                                out=eTr_f[:, 2048 * g4:2048 * (g4 + 1)],
                                in0=eTr_f[:, 2048 * g4:2048 * (g4 + 1)],
                                scalar1=sflag[:, 0:1], scalar2=None, op0=ALU.mult)
                        s["v"].inc(i)
                        s["v"].wait(nc.tensor)
                        for j in range(4):
                            kc = 4 * g4 + j
                            vidx = (kb + 128 * kc - t0) // 128
                            nc.tensor.matmul(oTps[0:64, :],
                                             vt[:, vidx, 64 * h:64 * h + 64],
                                             eTr[:, g4, j, :], start=(kc == 0), stop=(kc == 7))
                        for j in range(4):
                            kc = 4 * g4 + j
                            i = nc.tensor.matmul(swps[0:1, :], ones_col[:, 0:1],
                                                 eTr[:, g4, j, :], start=(kc == 0), stop=(kc == 7))
                        s["p"].inc(i)
                    s["p"].wait(nc.scalar)
                    s["p"].wait(nc.vector)
                    i = nc.scalar.copy(out=ofm[po:po + 64, c, :], in_=oTps[0:64, :])
                    s["sc"].inc(i)
                    i = nc.vector.tensor_copy(out=sums[0:1, 512 * h:512 * (h + 1)], in_=swps[0:1, :])
                    s["v"].inc(i)
                nc.vector.tensor_scalar(out=sums[0:1, :], in0=sums[0:1, :],
                                        scalar1=1e-30, scalar2=None, op0=ALU.add)
                i = nc.vector.reciprocal(out=sums[0:1, :], in_=sums[0:1, :])
                s["v"].inc(i)
                s["v"].wait(nc.tensor)
                s["sc"].wait(nc.tensor)
                for c in range(4):
                    nc.tensor.matmul(srps[0:64, :], ones_row[0:1, 0:64],
                                     sums[0:1, 512 * 2 * c:512 * (2 * c + 1)],
                                     start=True, stop=True, skip_group_check=True)
                    i = nc.tensor.matmul(srps[64:128, :], ones_row[0:1, 0:64],
                                         sums[0:1, 512 * (2 * c + 1):512 * (2 * c + 2)],
                                         start=True, stop=True, skip_group_check=True)
                    s["p"].inc(i)
                    s["p"].wait(nc.scalar)
                    i = nc.scalar.copy(out=srep[:, :], in_=srps[:, :])
                    s["sc"].inc(i)
                    s["sc"].wait(nc.vector)
                    i = nc.vector.tensor_tensor(obf[:, c, :], ofm[:, c, :], srep[:, :], ALU.mult)
                    s["v"].inc(i)
                    s["v"].wait(nc.tensor)
                s["q"].wait(nc.vector)
                for dc in range(8):
                    ps = scps[:, dc % 4, :]
                    s["v"].wait(nc.tensor)
                    for ic in range(4):
                        i = nc.tensor.matmul(ps, wo[:, ic, 128 * dc:128 * (dc + 1)],
                                             obf[:, ic, :], start=(ic == 0), stop=(ic == 3))
                    s["p"].inc(i)
                    s["p"].wait(nc.vector)
                    i = nc.vector.scalar_tensor_tensor(out=resid[:, dc, :], in0=ps,
                                                       scalar=1.0, in1=resid[:, dc, :],
                                                       op0=ALU.bypass, op1=ALU.add)
                    s["v"].inc(i)
                s["v"].wait(nc.gpsimd)
                ctx.qg.dma(nc.gpsimd.dma_start(
                    out=out_stage.rearrange("(c p) t -> p c t", p=128)[:, :, q0w:q0w + 512],
                    in_=resid[:, :, :]))
                ctx.qg.wait(nc.vector)
        ctx.barrier()
        pb_es.close()
    les.close()


def emit_mlp(ctx, layer, t0, in_stage, out_stage, final):
    nc = ctx.nc
    d = ctx.d
    s = {"q": ctx.qs, "g": ctx.qg, "p": ctx.p, "v": ctx.v, "sc": ctx.sc}
    sflag = ctx.b["sflag"]
    nch = (RT - t0 + 511) // 512
    mes = ExitStack()
    w1 = mes.enter_context(nc.sbuf_tensor("ml_w1_%d" % layer, [128, 8, 5632], BF16))
    w2r = mes.enter_context(nc.sbuf_tensor("ml_w2r_%d" % layer, [128, 2, 11, 512], BF16))
    zin = mes.enter_context(nc.sbuf_tensor("ml_zi_%d" % layer, [128, 8, 513], F32))
    lnh = mes.enter_context(nc.sbuf_tensor("ml_ln_%d" % layer, [128, 8, 512], BF16))
    hh = mes.enter_context(nc.sbuf_tensor("ml_hh_%d" % layer, [128, 22, 512], BF16))
    gl = mes.enter_context(nc.sbuf_tensor("ml_gl_%d" % layer, [128, 2, 512], BF16))
    x1t = mes.enter_context(nc.sbuf_tensor("ml_x1_%d" % layer, [128, 8, 512], F32))
    sq = mes.enter_context(nc.sbuf_tensor("ml_sq_%d" % layer, [128, 512], F32))
    stat_sb = mes.enter_context(nc.sbuf_tensor("ml_st_%d" % layer, [33, 512], F32))
    rep_sb = mes.enter_context(nc.sbuf_tensor("ml_rp_%d" % layer, [128, 1024], F32))
    lng = mes.enter_context(nc.sbuf_tensor("ml_g_%d" % layer, [128, 8], F32))
    lnb = mes.enter_context(nc.sbuf_tensor("ml_b_%d" % layer, [128, 8], F32))
    b1c = mes.enter_context(nc.sbuf_tensor("ml_b1_%d" % layer, [128, 44], F32))
    b2c = mes.enter_context(nc.sbuf_tensor("ml_b2_%d" % layer, [128, 8], F32))
    stat_ps = mes.enter_context(nc.psum_tensor("ml_stat_%d" % layer, [33, 512], F32))
    rep_ps = mes.enter_context(nc.psum_tensor("ml_rep_%d" % layer, [128, 1024], F32))
    queues = [ctx.qs, ctx.qg]
    engs = {id(ctx.qs): nc.sync, id(ctx.qg): nc.gpsimd}
    mm_done = {}

    def issue_group(kg):
        g, half = (kg % 4) // 2, kg % 2
        q = queues[kg % 2]
        eng = engs[id(q)]
        if kg - 2 in mm_done:
            s["p"].wait(eng, at=mm_done[kg - 2])
        q.dma(eng.dma_start(
            out=w2r[:, kg % 2, :, :],
            in_=d["w2b"][layer, g, 11 * half:11 * half + 11].rearrange("i p f -> p i f")))

    if True:
        ctx.qg.dma(nc.gpsimd.dma_start(
            out=w1[:], in_=d["w1p"][layer].rearrange("(c p) f -> p c f", p=128)))
        s["q"].dma(nc.sync.dma_start(out=lng[:], in_=d["lng_col"][layer]))
        s["q"].dma(nc.sync.dma_start(out=lnb[:], in_=d["lnb_col"][layer]))
        s["q"].dma(nc.sync.dma_start(out=b1c[:], in_=d["b1p_col"][layer]))
        s["q"].dma(nc.sync.dma_start(out=b2c[:], in_=d["b2_col"][layer]))
        mu_sb = mes.enter_context(nc.sbuf_tensor("ml_mu_%d" % layer, [1, 512], F32))
        lnws = dict(sq=sq, stat=stat_ps, rep=rep_ps, stat_sb=stat_sb, rep_sb=rep_sb, mu_sb=mu_sb)
        for ci in range(nch):
            start = t0 + 512 * ci
            tc = min(512, RT - start)
            K = 4 * ci
            s["v"].wait(nc.sync)
            s["q"].dma(nc.sync.dma_start(
                out=zin[:, :, :tc + 1],
                in_=in_stage.rearrange("(c p) t -> p c t", p=128)[:, :, start - 1:start + tc]))
            if ci == 0:
                ctx.qg.wait(nc.tensor)
            issue_group(K)
            issue_group(K + 1)
            s["q"].wait(nc.vector)
            s["q"].wait(nc.scalar)
            s["q"].wait(nc.tensor)
            if start - 1 <= HALO - 1 < start + tc:
                loc = HALO - start
                i = nc.vector.tensor_scalar(out=zin[:, 4:8, loc:loc + 1],
                                            in0=zin[:, 4:8, loc:loc + 1],
                                            scalar1=sflag[:, 0:1], scalar2=None,
                                            op0=ALU.mult)
                s["v"].inc(i)
                s["v"].wait(nc.tensor)
                s["v"].wait(nc.scalar)
            hviews = [zin[:, c, 1:1 + tc] for c in range(4)] + [zin[:, c, 0:tc] for c in range(4, 8)]
            emit_ln_fm(ctx, s, hviews, [lnh[:, c, :tc] for c in range(8)], tc, lnws,
                       gain_cols=[lng[:, c:c + 1] for c in range(8)],
                       beta_cols=[lnb[:, c:c + 1] for c in range(8)])
            s["v"].wait(nc.tensor)
            vcnt = {}
            with (
                nc.psum_tensor("ml_pa_%d_%d" % (layer, ci), [128, 2, 512], F32) as psA,
                nc.psum_tensor("ml_pb_%d_%d" % (layer, ci), [128, 2, 512], F32) as psB,
            ):
                for ih in range(22):
                    sl = ih % 2
                    if ih >= 2:
                        s["v"].wait(nc.tensor, at=vcnt[ih - 2])
                    for c in range(8):
                        nc.tensor.matmul(psA[:, sl, :tc], w1[:, c, 128 * ih:128 * (ih + 1)],
                                         lnh[:, c, :tc], start=(c == 0), stop=(c == 7))
                    for c in range(8):
                        i = nc.tensor.matmul(psB[:, sl, :tc], w1[:, c, 2816 + 128 * ih:2816 + 128 * (ih + 1)],
                                             lnh[:, c, :tc], start=(c == 0), stop=(c == 7))
                    s["p"].inc(i)
                    pc = s["p"].n
                    s["p"].wait(nc.scalar, at=pc)
                    if ih >= 2:
                        s["v"].wait(nc.scalar, at=vcnt[ih - 2])
                    i = nc.scalar.activation(out=gl[:, sl, :tc], in_=psB[:, sl, :tc], func=AF.Gelu,
                                             bias=b1c[:, 22 + ih:23 + ih])
                    s["sc"].inc(i)
                    s["sc"].wait(nc.vector, at=s["sc"].n)
                    s["p"].wait(nc.vector, at=pc)
                    i = nc.vector.scalar_tensor_tensor(out=hh[:, ih, :tc], in0=psA[:, sl, :tc],
                                                       scalar=b1c[:, ih:ih + 1], in1=gl[:, sl, :tc],
                                                       op0=ALU.add, op1=ALU.mult)
                    s["v"].inc(i)
                    vcnt[ih] = s["v"].n
                s["v"].wait(nc.tensor)
            if final:
                s["q"].dma(nc.sync.dma_start(
                    out=x1t[:, :, :tc],
                    in_=d["x1_fm"].rearrange("(c p) t -> p c t", p=128)[:, :, start:start + tc]))
                s["q"].wait(nc.vector)
            with nc.psum_tensor("ml_pz_%d_%d" % (layer, ci), [128, 4, 512], F32) as psZ4:
                for k4 in range(4):
                    kg = K + k4
                    g, half = k4 // 2, k4 % 2
                    if k4 == 2:
                        s["v"].wait(nc.tensor)
                    queues[kg % 2].wait(nc.tensor)
                    for i_g in range(11):
                        ic = 11 * half + i_g
                        for dc4 in range(4):
                            i = nc.tensor.matmul(
                                psZ4[:, dc4, :tc], w2r[:, kg % 2, i_g, 128 * dc4:128 * (dc4 + 1)],
                                hh[:, ic, :tc], start=(ic == 0), stop=(ic == 21))
                    s["p"].inc(i)
                    mm_done[kg] = s["p"].n
                    if k4 < 2:
                        issue_group(kg + 2)
                    if half == 1:
                        s["p"].wait(nc.vector)
                        for dc4 in range(4):
                            dc = 4 * g + dc4
                            i = nc.vector.scalar_tensor_tensor(
                                out=zin[:, dc, 1:1 + tc], in0=psZ4[:, dc4, :tc],
                                scalar=b2c[:, dc:dc + 1], in1=zin[:, dc, 1:1 + tc],
                                op0=ALU.add, op1=ALU.add)
                            if final:
                                i = nc.vector.tensor_tensor(zin[:, dc, 1:1 + tc],
                                                            zin[:, dc, 1:1 + tc],
                                                            x1t[:, dc, :tc], ALU.add)
                        s["v"].inc(i)
            s["v"].wait(nc.gpsimd)
            if final:
                ctx.qg.dma(nc.gpsimd.dma_start(
                    out=d["out_fm"].rearrange("(c p) t -> p c t", p=128)[:, :, start - HALO:start - HALO + tc],
                    in_=zin[:, :, 1:1 + tc]))
            else:
                ctx.qg.dma(nc.gpsimd.dma_start(
                    out=out_stage.rearrange("(c p) t -> p c t", p=128)[:, :, start:start + tc],
                    in_=zin[:, :, 1:1 + tc]))
            ctx.qg.wait(nc.vector)
            ctx.qg.wait(nc.sync)
    ctx.barrier()
    mes.close()


_IN_SPECS = {
    "x_tok": ([T, D], F32), "x_rows_tok": ([RT, D], F32), "x_rows_fm": ([D, RT], F32),
    "sim_slab": ([128, None], BF16),
    "rot_cq": ([128, RT - 128], F32), "rot_sq": ([128, RT - 128], F32),
    "rot_ck": ([128, RT - 128], F32), "rot_sk": ([128, RT - 128], F32),
    "shift_flag": ([128, 1], F32),
    "la_mask": ([128, 8, 512], BF16),
    "iden": ([128, 128], BF16), "e8": ([128, 4, 8], F32), "e8t": ([8, 4, 128], F32),
    "ones_col": ([128, 1], BF16), "ones_row": ([1, 128], F32),
    "ones_row_bf": ([1, 128], BF16),
    "L0_attn_W_q": ([32, D], F32), "L0_attn_W_k": ([32, D], F32),
    "L0_attn_gain": ([1, D], F32), "L0_attn_bias": ([1, D], F32),
    "log_mix": ([1, 1], F32), "log_scale": ([1, 1], F32),
    "L0_ffn_wqkv": ([2, D, 1536], F32), "L0_ffn_wo": ([2, 512, D], F32),
    "qscale_col": ([2, 128, 1], F32), "kscale_col": ([2, 128, 1], F32),
    "lng_col": ([2, 128, 8], F32), "lnb_col": ([2, 128, 8], F32),
    "w1p": ([2, D, 5632], F32), "b1p_col": ([2, 128, 44], F32),
    "w2h": ([2, 2, 22, 128, 512], F32), "b2_col": ([2, 128, 8], F32),
    "variant": ([1, 1], U32),
}
_SLAB_W = max(len(sim_slab_order(v)) for v in range(4)) * 512


def build_program():
    nc0 = bass.Bass("TRN2", target_bir_lowering=False)
    nc = NCWrap(nc0)
    stack = ExitStack()
    ctx = Ctx(nc, stack)
    d = ctx.d
    for name, (shape, dt) in _IN_SPECS.items():
        shp = [s if s is not None else _SLAB_W for s in shape]
        d[name] = nc.dram_tensor(name, shp, dt, kind="ExternalInput").ap()
    for name, shape, dt in (
            ("xn_tok", [T, D], BF16), ("xn_fm", [D, T], BF16),
            ("xnrows", [D, RT], BF16), ("xnrows_tok", [RT, D], BF16),
            ("x1_fm", [D, RT], F32), ("z0_fm", [D, RT], F32),
            ("z1_fm", [D, RT], F32), ("z2_fm", [D, RT], F32),
            ("z3_fm", [D, RT], F32), ("w2b", [2, 2, 22, 128, 512], BF16)):
        d[name] = nc.dram_tensor(name, shape, dt, kind="Internal").ap()
    d["out_fm"] = nc.dram_tensor("out_fm", [D, CH], F32, kind="ExternalOutput").ap()
    if DEBUG_LA:
        d["dbg_kraw"] = nc.dram_tensor("dbg_kraw", [128, 4, 512], F32, kind="Internal").ap()
        d["dbg_hbf"] = nc.dram_tensor("dbg_hbf", [128, 8, 512], BF16, kind="Internal").ap()
        d["dbg_r8"] = nc.dram_tensor("dbg_r8", [8, 512], F32, kind="Internal").ap()
        d["dbg_qh"] = nc.dram_tensor("dbg_qh", [128, 4, RT - CH - 512], BF16, kind="Internal").ap()
        d["dbg_kh"] = nc.dram_tensor("dbg_kh", [128, 4, RT - CH], BF16, kind="Internal").ap()
        d["dbg_vt"] = nc.dram_tensor("dbg_vt", [128, (RT - CH) // 128, 512], BF16, kind="Internal").ap()

    with stack:
        ctx.attn_es = ExitStack()
        vf = ctx.sb("vflag_sb", [1, 1], U32)
        ctx.qs.dma(nc.sync.dma_start(out=vf[:], in_=d["variant"][:]))

        emit_prepass(ctx)
        if PHASES < 1.5:
            return nc0

        engines = [nc.sync, nc.tensor, nc.vector, nc.scalar, nc.gpsimd]
        regs = nc.alloc_registers("vreg")
        for eng in engines:
            ctx.qs.wait(eng)
            eng.reg_load(regs[eng.engine], vf[0:1, 0:1])
        iss = {"q": nc.sync, "g": nc.gpsimd}
        sV = [{k: Sem(stack, nc, "a%d_" % v + k, issuer=iss.get(k))
               for k in ("q", "g", "p", "v", "sc")} for v in range(4)]
        with nc.If_eq(regs, 0):
            emit_attn(ctx, 0, sV[0])
        with nc.Else():
            with nc.If_eq(regs, 1):
                emit_attn(ctx, 1, sV[1])
            with nc.Else():
                with nc.If_eq(regs, 2):
                    emit_attn(ctx, 2, sV[2])
                with nc.Else():
                    emit_attn(ctx, 3, sV[3])
        ctx.attn_es.close()
        if PHASES < 2.5:
            if PHASES == 2:
                i = nc.sync.dma_start(out=d["out_fm"][:, :],
                                      in_=d["x1_fm"][:, DUMP_OFF:DUMP_OFF + CH])
                ctx.qs.inc(i, 16)
                ctx.qs.wait(nc.sync)
            return nc0

        emit_la(ctx, 0, B1_T0, LA1_Q0, LA1_NW, d["x1_fm"], True,
                d["z0_fm"], d["z0_fm"], d["z1_fm"])
        if PHASES < 3.5:
            if PHASES == 3:
                i = nc.sync.dma_start(out=d["out_fm"][:, :],
                                      in_=d["z1_fm"][:, DUMP_OFF:DUMP_OFF + CH])
                ctx.qs.inc(i, 16)
                ctx.qs.wait(nc.sync)
            return nc0
        emit_mlp(ctx, 0, B2_T0, d["z1_fm"], d["z2_fm"], final=False)
        if PHASES < 4.5:
            if PHASES == 4:
                i = nc.sync.dma_start(out=d["out_fm"][:, :],
                                      in_=d["z2_fm"][:, DUMP_OFF:DUMP_OFF + CH])
                ctx.qs.inc(i, 16)
                ctx.qs.wait(nc.sync)
            return nc0
        emit_la(ctx, 1, B3_T0, LA2_Q0, LA2_NW, d["z2_fm"], False,
                None, d["z2_fm"], d["z3_fm"])
        if PHASES < 5.5:
            if PHASES == 5:
                i = nc.sync.dma_start(out=d["out_fm"][:, :],
                                      in_=d["z3_fm"][:, DUMP_OFF:DUMP_OFF + CH])
                ctx.qs.inc(i, 16)
                ctx.qs.wait(nc.sync)
            return nc0
        emit_mlp(ctx, 1, B4_T0, d["z3_fm"], None, final=True)
    return nc0


_NC_CACHE = None
DEBUG_LA = False
PHASES = 6
DUMP_OFF = HALO


def make_in_maps(inputs, cids=range(8)):
    in_maps = []
    for cid in cids:
        m = _host_core_inputs(inputs, cid)
        sl = m["sim_slab"]
        if sl.shape[1] < _SLAB_W:
            m["sim_slab"] = np.concatenate(
                [sl, np.zeros((128, _SLAB_W - sl.shape[1]), dtype=BF)], axis=1)
        w1 = m.pop("L0_ffn_w1")
        w1p = np.zeros((2, D, 5632), dtype=np.float32)
        w1p[:, :, 0:HID] = w1[:, :, 0:HID]
        w1p[:, :, 2816:2816 + HID] = w1[:, :, HID:2 * HID]
        m["w1p"] = w1p
        w2 = m.pop("L0_ffn_w2")
        w2p = np.zeros((2, 2816, D), dtype=np.float32)
        w2p[:, 0:HID, :] = w2
        m["w2h"] = np.ascontiguousarray(
            w2p.reshape(2, 22, 128, 2, 512).transpose(0, 3, 1, 2, 4))
        b1 = m.pop("L0_ffn_b1")
        b1p = np.zeros((2, 5632), dtype=np.float32)
        b1p[:, 0:HID] = b1[:, 0:HID]
        b1p[:, 2816:2816 + HID] = b1[:, HID:2 * HID]
        m["b1p_col"] = np.ascontiguousarray(
            b1p.reshape(2, 44, 128).transpose(0, 2, 1))
        m["b2_col"] = np.ascontiguousarray(
            m.pop("L0_ffn_b2").reshape(2, 8, 128).transpose(0, 2, 1))
        m["lng_col"] = np.ascontiguousarray(
            m.pop("L0_ffn_ln_g").reshape(2, 8, 128).transpose(0, 2, 1))
        m["lnb_col"] = np.ascontiguousarray(
            m.pop("L0_ffn_ln_b").reshape(2, 8, 128).transpose(0, 2, 1))
        m["qscale_col"] = np.ascontiguousarray(
            np.tile(m.pop("L0_ffn_q_scale"), (1, 2))[:, :, None])
        m["kscale_col"] = np.ascontiguousarray(
            np.tile(m.pop("L0_ffn_k_scale"), (1, 2))[:, :, None])
        in_maps.append(m)
    return in_maps


def kernel(**inputs):
    global _NC_CACHE
    in_maps = make_in_maps(inputs)
    if _NC_CACHE is None:
        _NC_CACHE = build_program()
    res = run_bass_kernel_spmd(_NC_CACHE, in_maps, core_ids=list(range(8)))
    out = np.empty((B, T, D), dtype=np.float32)
    for cid in range(8):
        b, v = cid // 4, cid % 4
        out[b, v * CH:(v + 1) * CH, :] = np.asarray(res.results[cid]["out_fm"]).T
    return out
